# version 07
# baseline (speedup 1.0000x reference)
import contextlib
import numpy as np
import ml_dtypes
import concourse.bass as bass
import concourse.mybir as mybir
from concourse.bass_utils import run_bass_kernel_spmd

F32 = mybir.dt.float32
BF16 = mybir.dt.bfloat16
FP8 = mybir.dt.float8e4
AF = mybir.ActivationFunctionType
ALU = mybir.AluOpType
AX = mybir.AxisListType.X

NT = 32
_DBG_INTERNAL = False
D = 1024
INV_D = 1.0 / D
LN_EPS = 1e-5


class Prog:
    ENGINES = ("tensor", "vector", "scalar", "gpsimd", "sync")

    def __init__(self):
        self.ops = {e: [] for e in self.ENGINES}
        self.cnt = {}

    def val(self, sem):
        return self.cnt.get(id(sem), 0)

    def emit(self, eng, build, waits=(), inc=None, inc_amt=1):
        n = None
        if inc is not None:
            k = id(inc)
            self.cnt[k] = self.cnt.get(k, 0) + inc_amt
            n = self.cnt[k]
        pend = getattr(self, "pending", None)
        extra = pend.pop(eng, []) if pend else []
        waits = tuple((s, v) for w in list(waits) + extra if w is not None
                      for (s, v) in [w] if v and v > 0)

        def f(e, build=build, waits=waits, inc=inc, amt=inc_amt):
            for s, v in waits:
                e.wait_ge(s, v)
            inst = build(e)
            if inc is not None:
                inst.then_inc(inc, amt)
        self.ops[eng].append(f)
        return n

    def set_barrier(self, bar):
        self.pending = {e: list(bar) for e in self.ENGINES}

    def run(self, block):
        for eng in self.ENGINES:
            dec = getattr(block, eng)

            def body(e, eng=eng):
                for f in self.ops[eng]:
                    f(e)
            dec(body)


class LNPhase:

    def __init__(self, stack, nc, name):
        t = lambda nm: stack.enter_context(nc.sbuf_tensor(name + nm, [128, NT], F32))
        self.sall = t("_s")
        self.qall = t("_q")
        self.mall = t("_m")
        self.rall = t("_r")
        self.ball = t("_b")
        self.scr1 = t("_t1")
        self.scr2 = t("_t2")
        self.stat_acts = []
        self.stat_dves = []
        self.ready = None
        self.q_zeroed = None

    def emit_zero(self, P, c_dve):
        self.q_zeroed = P.emit("vector", lambda e: e.memset(self.qall[:], 0.0), inc=c_dve)

    def emit_math(self, P, c_act, c_dve, eps128, invd128, neg1128, eps_wait=None):
        last_sq = self.stat_acts[-1]
        last_rd = self.stat_dves[-1]
        n_m = P.emit("vector", lambda e: e.tensor_scalar(
            self.mall[:], self.sall[:], invd128[:], None, ALU.mult),
            waits=[eps_wait, (c_dve, last_rd)], inc=c_dve)
        n_s1 = P.emit("vector", lambda e: e.tensor_tensor(
            self.scr1[:], self.mall[:], self.mall[:], ALU.mult),
            waits=[(c_dve, n_m)], inc=c_dve)
        n_s2 = P.emit("vector", lambda e: e.tensor_scalar(
            self.scr2[:], self.qall[:], invd128[:], None, ALU.mult),
            waits=[(c_act, last_sq)], inc=c_dve)
        nv = P.emit("vector", lambda e: e.tensor_tensor(
            self.scr2[:], self.scr2[:], self.scr1[:], ALU.subtract),
            waits=[(c_dve, n_s1), (c_dve, n_s2)], inc=c_dve)
        n_ln = P.emit("scalar", lambda e: e.activation(
            self.scr1[:], self.scr2[:], AF.Ln, bias=eps128[:]),
            waits=[(c_dve, nv), eps_wait], inc=c_act)
        nr = P.emit("scalar", lambda e: e.activation(
            self.rall[:], self.scr1[:], AF.Exp, scale=-0.5),
            waits=[(c_act, n_ln)], inc=c_act)
        self.ready = P.emit("vector", lambda e: e.scalar_tensor_tensor(
            self.ball[:], self.mall[:], neg1128[:], self.rall[:], ALU.mult, ALU.mult),
            waits=[(c_act, nr), (c_dve, n_m)], inc=c_dve)

    def apply(self, t):
        return self.rall[:, t:t + 1], self.ball[:, t:t + 1]


def emit_l1ffn(nc, P, sems, x_in, out, ident, outer=None):
    c_pe, c_act, c_dve, d_sy, d_gp = sems
    kern = nc.dram_tensor("k_kern", [D, D], BF16, kind="ExternalInput").ap()
    s_in = nc.dram_tensor("k_s_in", [1, 1], F32, kind="ExternalInput").ap()
    s_k = nc.dram_tensor("k_s_k", [1, 1], F32, kind="ExternalInput").ap()
    bias = nc.dram_tensor("k_bias", [1, D], BF16, kind="ExternalInput").ap()
    _k = "ExternalOutput" if _DBG_INTERNAL else "Internal"
    ybuf = nc.dram_tensor("k_ybuf", [NT * 128, D], BF16, kind=_k).ap()
    if _DBG_INTERNAL:
        kdbg = nc.dram_tensor("k_dbg", [5, 128], F32, kind="ExternalOutput").ap()

    x_t = x_in.rearrange("(t p) d -> t p d", p=128)
    y_t = ybuf.rearrange("(t p) d -> t p d", p=128)
    o_t = out.rearrange("(t p) d -> t p d", p=128)

    with (outer if outer is not None else contextlib.ExitStack()) as stack:
        sb = lambda nm, sh, dt: stack.enter_context(nc.sbuf_tensor("k" + nm, sh, dt))
        ps = lambda nm, sh, dt: stack.enter_context(nc.psum_tensor("k" + nm, sh, dt))

        XB = sb("XB", [128, 4 * D], F32)
        KB = sb("KB", [128, 8 * D], BF16)
        K8 = sb("K8", [128, 8 * D], FP8)
        kdiv = sb("kdiv", [128, 2 * D], F32)
        onesf = sb("onesf", [1, 128], F32)
        onesb = sb("onesb", [1, 128], BF16)
        sA = sb("sA", [1, 1], F32)
        sB = sb("sB", [1, 1], F32)
        btrow = sb("btrow", [1, D], BF16)
        s128 = sb("s128", [128, 1], F32)
        k128 = sb("k128", [128, 1], F32)
        ss128 = sb("ss128", [128, 1], F32)
        rs128 = sb("rs128", [128, 1], F32)
        rk128 = sb("rk128", [128, 1], F32)
        eps128 = sb("eps128", [128, 1], F32)
        invd128 = sb("invd128", [128, 1], F32)
        neg1128 = sb("neg1128", [128, 1], F32)
        biasbc = sb("biasbc", [128, D], BF16)
        idt = sb("idt", [128, 128], BF16)
        sqscr = sb("sqscr", [128, D], F32)
        XH = sb("XH", [128, 2 * D], BF16)
        XF = sb("XF", [128, 2 * D], BF16)
        xdiv = sb("xdiv", [128, 2 * D], F32)
        X8 = sb("X8", [128, 2 * D], FP8)
        YB = sb("YB", [128, 2 * D], BF16)
        YN = sb("YN", [128, 2 * D], F32)
        OT = sb("OT", [128, 2 * D], F32)
        ptr = [ps("ptr0", [128, 128], BF16), ps("ptr1", [128, 128], BF16)]
        py = [ps("py0", [128, 512], F32), ps("py1", [128, 512], F32)]

        ln1 = LNPhase(stack, nc, "kln1")
        ln2 = LNPhase(stack, nc, "kln2")

        ln1.emit_zero(P, c_dve)
        ln2.emit_zero(P, c_dve)
        for c in range(8):
            d_kb = P.emit("sync", lambda e, c=c: e.dma_start(
                out=KB[:, c * D:(c + 1) * D],
                in_=kern[c * 128:(c + 1) * 128, :]), inc=d_sy, inc_amt=16)
        d_sa = P.emit("sync", lambda e: e.dma_start(out=sA[:], in_=s_in), inc=d_sy, inc_amt=16)
        d_sb = P.emit("sync", lambda e: e.dma_start(out=sB[:], in_=s_k), inc=d_sy, inc_amt=16)
        d_bt = P.emit("sync", lambda e: e.dma_start(out=btrow[:], in_=bias), inc=d_sy, inc_amt=16)
        d_id = P.emit("sync", lambda e: e.dma_start(out=idt[:], in_=ident), inc=d_sy, inc_amt=16)
        P.emit("gpsimd", lambda e: e.memset(onesf[:], 1.0))
        P.emit("gpsimd", lambda e: e.memset(onesb[:], 1.0))
        P.emit("gpsimd", lambda e: e.memset(eps128[:], LN_EPS))
        P.emit("gpsimd", lambda e: e.memset(invd128[:], INV_D))
        n_eps = P.emit("gpsimd", lambda e: e.memset(neg1128[:], -1.0), inc=d_gp)

        n1 = P.emit("tensor", lambda e: e.matmul(py[0][:, 0:1], onesf[:], sA[:],
                                                 start=True, stop=True),
                    waits=[(d_sy, d_sa), (d_gp, n_eps)], inc=c_pe)
        ns128 = P.emit("vector", lambda e: e.tensor_copy(s128[:], py[0][:, 0:1]),
                       waits=[(c_pe, n1)], inc=c_dve)
        n2 = P.emit("tensor", lambda e: e.matmul(py[0][:, 4:5], onesf[:], sB[:],
                                                 start=True, stop=True),
                    waits=[(d_sy, d_sb), (c_dve, ns128)], inc=c_pe)
        n3 = P.emit("vector", lambda e: e.tensor_copy(k128[:], py[0][:, 4:5]),
                    waits=[(c_pe, n2)], inc=c_dve)
        n_ss = P.emit("vector", lambda e: e.tensor_tensor(ss128[:], s128[:], k128[:], ALU.mult),
                      waits=[(c_dve, n3)], inc=c_dve)
        P.emit("vector", lambda e: e.reciprocal(rs128[:], s128[:]), waits=[(c_dve, ns128)])
        n4 = P.emit("vector", lambda e: e.reciprocal(rk128[:], k128[:]),
                    waits=[(c_dve, n3)], inc=c_dve)
        nb0 = P.emit("tensor", lambda e: e.matmul(py[1][:], onesb[:], btrow[:, 0:512],
                                                  start=True, stop=True),
                     waits=[(d_sy, d_bt), (d_gp, n_eps)], inc=c_pe)
        nb0e = P.emit("scalar", lambda e: e.activation(biasbc[:, 0:512], py[1][:], AF.Identity),
                      waits=[(c_pe, nb0)], inc=c_act)
        nb1 = P.emit("tensor", lambda e: e.matmul(py[1][:], onesb[:], btrow[:, 512:1024],
                                                  start=True, stop=True),
                     waits=[(c_act, nb0e)], inc=c_pe)
        nb1e = P.emit("scalar", lambda e: e.activation(biasbc[:, 512:1024], py[1][:], AF.Identity),
                      waits=[(c_pe, nb1)], inc=c_act)

        k8_last = None
        k8_divs = {}
        for c in range(8):
            p = c % 2
            w = [(c_dve, n4), (d_sy, d_kb)]
            if c >= 2:
                w.append((c_act, k8_divs[p]))
            ndv = P.emit("vector", lambda e, c=c, p=p: e.tensor_scalar(
                kdiv[:, p * D:(p + 1) * D], KB[:, c * D:(c + 1) * D],
                rk128[:], None, ALU.mult), waits=w, inc=c_dve)
            k8_last = P.emit("scalar", lambda e, c=c, p=p: e.copy(
                K8[:, c * D:(c + 1) * D], kdiv[:, p * D:(p + 1) * D]),
                waits=[(c_dve, ndv)], inc=c_act)
            k8_divs[p] = k8_last

        if _DBG_INTERNAL:
            P.emit("gpsimd", lambda e: e.dma_start(
                out=kdbg[0:1, :].rearrange('a b -> b a'), in_=s128[:]), waits=[(c_dve, n4)], inc=d_gp, inc_amt=16)
            P.emit("gpsimd", lambda e: e.dma_start(
                out=kdbg[1:2, :].rearrange('a b -> b a'), in_=k128[:]), inc=d_gp, inc_amt=16)
            P.emit("gpsimd", lambda e: e.dma_start(
                out=kdbg[2:3, :].rearrange('a b -> b a'), in_=rs128[:]), inc=d_gp, inc_amt=16)
            P.emit("gpsimd", lambda e: e.dma_start(
                out=kdbg[3:4, :].rearrange('a b -> b a'), in_=rk128[:]), inc=d_gp, inc_amt=16)
            P.emit("gpsimd", lambda e: e.dma_start(
                out=kdbg[4:5, :].rearrange('a b -> b a'), in_=ss128[:]), waits=[(c_dve, P.val(c_dve))],
                inc=d_gp, inc_amt=16)

        _heat(P, c_pe, py[0], idt, (c_dve, n3))
        x_cons = {}
        for t in range(NT):
            slot = t % 4
            w = [ (c_dve, x_cons[slot][0]) if slot in x_cons else None,
                  (c_act, x_cons[slot][1]) if slot in x_cons else None ]
            nl = P.emit("sync", lambda e, t=t, slot=slot: e.dma_start(
                out=XB[:, slot * D:(slot + 1) * D], in_=x_t[t]), waits=w,
                inc=d_sy, inc_amt=16)
            src = XB[:, slot * D:(slot + 1) * D]
            nd = P.emit("vector", lambda e, src=src, t=t: e.reduce_sum(
                ln1.sall[:, t:t + 1], src, axis=AX), waits=[(d_sy, nl)], inc=c_dve)
            ln1.stat_dves.append(nd)
            na = P.emit("scalar", lambda e, src=src, t=t: e.activation(
                sqscr[:], src, AF.Square, accum_out=ln1.qall[:, t:t + 1]),
                waits=[(d_sy, nl), (c_dve, ln1.q_zeroed)], inc=c_act)
            ln1.stat_acts.append(na)
            x_cons[slot] = (nd, na)
            _heat(P, c_pe, py[0], idt, (d_sy, nl))

        ln1.emit_math(P, c_act, c_dve, eps128, invd128, neg1128, eps_wait=(d_gp, n_eps))

        xh_free = {}
        xdiv_free = {}
        x8_free = {}
        yb_cons = {}
        py_free = [(c_dve, n3), (c_act, nb1e)]
        tr_free = [None, None]
        for t in range(NT):
            slot = t % 4
            p = t % 2
            w = [(c_dve, x_cons[slot][0]), (c_act, x_cons[slot][1])]
            nl = P.emit("sync", lambda e, t=t, slot=slot: e.dma_start(
                out=XB[:, slot * D:(slot + 1) * D], in_=x_t[t]), waits=w,
                inc=d_sy, inc_amt=16)
            src = XB[:, slot * D:(slot + 1) * D]
            w = [(d_sy, nl), (c_dve, ln1.ready)]
            if p in xh_free:
                w.append((c_pe, xh_free[p]))
            sc, bi = ln1.apply(t)
            nap = P.emit("scalar", lambda e, src=src, p=p, sc=sc, bi=bi: e.activation(
                XH[:, p * D:(p + 1) * D], src, AF.Identity, bias=bi, scale=sc),
                waits=w, inc=c_act)
            x_cons[slot] = (x_cons[slot][0], nap)
            last_tp = None
            for c in range(8):
                tp = c % 2
                w = [(c_act, nap)] if c < 2 else []
                if tr_free[tp] is not None:
                    w.append((c_dve, tr_free[tp]))
                if t == 0 and c == 0:
                    w.append((d_sy, d_id))
                last_tp = P.emit("tensor", lambda e, p=p, c=c, tp=tp: e.transpose(
                    ptr[tp][:], XH[:, p * D + c * 128: p * D + (c + 1) * 128],
                    idt[:]), waits=w, inc=c_pe)
                tr_free[tp] = P.emit("vector", lambda e, p=p, c=c, tp=tp: e.tensor_copy(
                    XF[:, p * D + c * 128: p * D + (c + 1) * 128], ptr[tp][:]),
                    waits=[(c_pe, last_tp)], inc=c_dve)
            xh_free[p] = last_tp
            w = [(c_dve, n4), (c_dve, tr_free[1])]
            if p in xdiv_free:
                w.append((c_act, xdiv_free[p]))
            nd = P.emit("vector", lambda e, p=p: e.tensor_scalar(
                xdiv[:, p * D:(p + 1) * D], XF[:, p * D:(p + 1) * D],
                rs128[:], None, ALU.mult), waits=w, inc=c_dve)
            w = [(c_dve, nd)]
            if p in x8_free:
                w.append((c_pe, x8_free[p]))
            n8 = P.emit("scalar", lambda e, p=p: e.copy(
                X8[:, p * D:(p + 1) * D], xdiv[:, p * D:(p + 1) * D]),
                waits=w, inc=c_act)
            xdiv_free[p] = n8
            mm_groups = []
            for half in range(2):
                nmm = None
                for c in range(8):
                    w = []
                    if c == 0:
                        w = [(c_act, n8), (c_act, k8_last), py_free[half]]
                    nmm = P.emit("tensor", lambda e, p=p, half=half, c=c: e.matmul(
                        py[half][:], X8[:, p * D + c * 128: p * D + (c + 1) * 128],
                        K8[:, c * D + half * 512: c * D + half * 512 + 512],
                        start=(c == 0), stop=(c == 7)), waits=w,
                        inc=(c_pe if c == 7 else None))
                mm_groups.append(nmm)
            x8_free[p] = mm_groups[-1]
            yevs = []
            for half in range(2):
                w = [(c_pe, mm_groups[half]), (c_dve, n_ss)]
                if p in yb_cons:
                    w += [yb_cons[p][0], yb_cons[p][1], yb_cons[p][2]]
                ne = P.emit("scalar", lambda e, p=p, half=half: e.activation(
                    YB[:, p * D + half * 512: p * D + half * 512 + 512],
                    py[half][:], AF.Identity, scale=ss128[:]),
                    waits=w, inc=c_act)
                py_free[half] = (c_act, ne)
                yevs.append(ne)
            nb = P.emit("gpsimd", lambda e, p=p: e.tensor_tensor(
                YB[:, p * D:(p + 1) * D], YB[:, p * D:(p + 1) * D],
                biasbc[:], ALU.add), waits=[(c_act, yevs[-1]), (c_act, nb1e)],
                inc=d_gp, inc_amt=1)
            ns = P.emit("sync", lambda e, t=t, p=p: e.dma_start(
                out=y_t[t], in_=YB[:, p * D:(p + 1) * D]), waits=[(d_gp, nb)],
                inc=d_sy, inc_amt=16)
            nd2 = P.emit("vector", lambda e, t=t, p=p: e.reduce_sum(
                ln2.sall[:, t:t + 1], YB[:, p * D:(p + 1) * D], axis=AX),
                waits=[(d_gp, nb)], inc=c_dve)
            ln2.stat_dves.append(nd2)
            na2 = P.emit("scalar", lambda e, t=t, p=p: e.activation(
                sqscr[:], YB[:, p * D:(p + 1) * D], AF.Square,
                accum_out=ln2.qall[:, t:t + 1]),
                waits=[(d_gp, nb), (c_dve, ln2.q_zeroed)], inc=c_act)
            ln2.stat_acts.append(na2)
            yb_cons[p] = ((c_dve, nd2), (c_act, na2), (d_sy, ns))

        ln2.emit_math(P, c_act, c_dve, eps128, invd128, neg1128, eps_wait=(d_gp, n_eps))

        ot_free = {}
        yn_war = {}
        for t in range(NT):
            slot = t % 4
            p = t % 2
            w = [(c_dve, x_cons[slot][0]), (c_act, x_cons[slot][1])]
            nlx = P.emit("sync", lambda e, t=t, slot=slot: e.dma_start(
                out=XB[:, slot * D:(slot + 1) * D], in_=x_t[t]), waits=w,
                inc=d_sy, inc_amt=16)
            w = [yb_cons[p][0], yb_cons[p][1], yb_cons[p][2]]
            nly = P.emit("sync", lambda e, t=t, p=p: e.dma_start(
                out=YB[:, p * D:(p + 1) * D], in_=y_t[t]), waits=w,
                inc=d_sy, inc_amt=16)
            sc, bi = ln2.apply(t)
            w = [(d_sy, nly), (c_dve, ln2.ready)]
            if p in yn_war:
                w.append(yn_war[p])
            na = P.emit("scalar", lambda e, p=p, sc=sc, bi=bi: e.activation(
                YN[:, p * D:(p + 1) * D], YB[:, p * D:(p + 1) * D],
                AF.Identity, bias=bi, scale=sc), waits=w, inc=c_act)
            yb_cons[p] = ((c_act, na), None, None)
            w = [(d_sy, nlx), (c_act, na)]
            if p in ot_free:
                w.append(ot_free[p][1])
            no = P.emit("vector", lambda e, t=t, p=p, slot=slot: e.tensor_tensor(
                OT[:, p * D:(p + 1) * D], XB[:, slot * D:(slot + 1) * D],
                YN[:, p * D:(p + 1) * D], ALU.add), waits=w, inc=c_dve)
            yn_war[p] = (c_dve, no)
            x_cons[slot] = (no, x_cons[slot][1])
            ns = P.emit("sync", lambda e, t=t, p=p: e.dma_start(
                out=o_t[t], in_=OT[:, p * D:(p + 1) * D]), waits=[(c_dve, no)],
                inc=d_sy, inc_amt=16)
            ot_free[p] = (no, (d_sy, ns))
            _heat(P, c_pe, py[0], idt, (d_sy, nlx), (c_act, P.val(c_act)))

    return


def _l1ffn_inmaps(inputs):
    return {
        "k_kern": np.ascontiguousarray(inputs["L1_ffn_kernel"]),
        "k_s_in": inputs["L1_ffn_input_scale"].reshape(1, 1),
        "k_s_k": inputs["L1_ffn_kernel_scale"].reshape(1, 1),
        "k_bias": inputs["L1_ffn_bias"].reshape(1, D),
    }






def emit_l0ffn(nc, P, sems, x_in, out, ident, outer=None):
    c_pe, c_act, c_dve, d_sy, d_gp = sems
    H = 1408
    NHC = 11
    w1 = nc.dram_tensor("f_w1", [8, D, H], F32, kind="ExternalInput").ap()
    w3 = nc.dram_tensor("f_w3", [8, D, H], F32, kind="ExternalInput").ap()
    w2 = nc.dram_tensor("f_w2", [8, H, D], F32, kind="ExternalInput").ap()
    sw1 = nc.dram_tensor("f_sw1", [D, H], F32, kind="ExternalInput").ap()
    sw3 = nc.dram_tensor("f_sw3", [D, H], F32, kind="ExternalInput").ap()
    sw2 = nc.dram_tensor("f_sw2", [H, D], F32, kind="ExternalInput").ap()
    b1r = nc.dram_tensor("f_b1r", [128, 8 * NHC], F32, kind="ExternalInput").ap()
    b3r = nc.dram_tensor("f_b3r", [128, 8 * NHC], F32, kind="ExternalInput").ap()
    b2m = nc.dram_tensor("f_b2m", [8, D], F32, kind="ExternalInput").ap()
    sb1r = nc.dram_tensor("f_sb1r", [128, NHC], F32, kind="ExternalInput").ap()
    sb3r = nc.dram_tensor("f_sb3r", [128, NHC], F32, kind="ExternalInput").ap()
    sb2r = nc.dram_tensor("f_sb2r", [1, D], F32, kind="ExternalInput").ap()
    gwt = nc.dram_tensor("f_gwt", [D, 8], F32, kind="ExternalInput").ap()
    gbr = nc.dram_tensor("f_gbr", [1, 8], F32, kind="ExternalInput").ap()
    _k = "ExternalOutput" if _DBG_INTERNAL else "Internal"
    ybuf = nc.dram_tensor("f_ybuf", [NT * 128, D], F32, kind=_k).ap()
    xhfb = nc.dram_tensor("f_xhfb", [8, 128, NT * 128], BF16, kind=_k).ap()

    x_t = x_in.rearrange("(t p) d -> t p d", p=128)
    o_t = out.rearrange("(t p) d -> t p d", p=128)
    y_t = ybuf.rearrange("(t p) d -> t p d", p=128)
    y_r = ybuf.rearrange("(T p) d -> p T d", p=128)
    xh_w = xhfb.rearrange("c p n -> p c n")
    w1_r = w1.rearrange("e (dc p) h -> e p dc h", p=128)
    w3_r = w3.rearrange("e (dc p) h -> e p dc h", p=128)
    w2_r = w2.rearrange("e (hc p) d -> e p hc d", p=128)
    sw1_r = sw1.rearrange("(dc p) h -> p dc h", p=128)
    sw3_r = sw3.rearrange("(dc p) h -> p dc h", p=128)
    sw2_r = sw2.rearrange("(hc p) d -> p hc d", p=128)
    gwt_r = gwt.rearrange("(dc p) e -> p dc e", p=128)

    with contextlib.ExitStack() as stack:
        sb = lambda nm, sh, dt: stack.enter_context(nc.sbuf_tensor("f" + nm, sh, dt))
        ps = lambda nm, sh, dt: stack.enter_context(nc.psum_tensor("f" + nm, sh, dt))

        XB = sb("XB", [128, 2 * D], F32)
        XH = sb("XH", [128, 2 * D], BF16)
        XFT = sb("XFT", [128, 2 * D], BF16)
        XSTR = sb("XSTR", [128, 2 * 8 * 512], BF16)
        W1B = sb("W1B", [128, 8 * H], BF16)
        W3B = sb("W3B", [128, 8 * H], BF16)
        W2B = sb("W2B", [128, NHC * D], BF16)
        HT = sb("HT", [128, NHC * 512], BF16)
        GS1 = sb("GS1", [128, 512], F32)
        GS3 = sb("GS3", [128, 512], F32)
        PSC = sb("PSC", [128, 512], F32)
        OESB = sb("OESB", [128, 4 * D], F32)
        B1S = sb("B1S", [128, 8 * NHC], F32)
        B3S = sb("B3S", [128, 8 * NHC], F32)
        SB1S = sb("SB1S", [128, NHC], F32)
        SB3S = sb("SB3S", [128, NHC], F32)
        B2T = sb("B2T", [8, D], BF16)
        SB2R = sb("SB2R", [1, D], BF16)
        GWTB = sb("GWTB", [128, 8 * 8], BF16)
        GBR1 = sb("GBR1", [1, 8], F32)
        GBB = sb("GBB", [128, 8], F32)
        GT = sb("GT", [128, 2 * 8], F32)
        GE = sb("GE", [128, 8], F32)
        GP = sb("GP", [128, 8], F32)
        GBT = sb("GBT", [128, 8], F32)
        GSC = sb("GSC", [128, 8], F32)
        GM = sb("GM", [128, 1], F32)
        RW = sb("RW", [128, NT * 8], F32)
        RWB = sb("RWB", [128, 8], BF16)
        RWF = sb("RWF", [8, NT * 128], BF16)
        YBF = sb("YBF", [128, 2 * D], F32)
        TMP = sb("TMP", [128, 2 * D], F32)
        OT = sb("OT", [128, 2 * D], F32)
        onesf = sb("onesf", [1, 128], F32)
        onesb = sb("onesb", [1, 128], BF16)
        eps128 = sb("eps128", [128, 1], F32)
        invd128 = sb("invd128", [128, 1], F32)
        neg1128 = sb("neg1128", [128, 1], F32)
        negbig = sb("negbig", [128, 1], F32)
        idt = sb("idt", [128, 128], BF16)
        sqscr = sb("sqscr", [128, D], F32)
        pa1 = ps("pa1", [128, 512], F32)
        pa3 = ps("pa3", [128, 512], F32)
        po = [ps("po0", [128, 512], F32), ps("po1", [128, 512], F32)]
        ptr = [ps("ptr0", [128, 128], BF16), ps("ptr1", [128, 128], BF16)]
        pg = ps("pg", [128, 8], F32)

        ln1 = LNPhase(stack, nc, "fln1")
        ln2 = LNPhase(stack, nc, "fln2")
        ln1.emit_zero(P, c_dve)
        ln2.emit_zero(P, c_dve)

        d_small = None
        for (dst, src_ap) in [(B1S, b1r), (B3S, b3r), (SB1S, sb1r), (SB3S, sb3r),
                              (GBR1, gbr), (idt, ident)]:
            d_small = P.emit("sync", lambda e, dst=dst, src_ap=src_ap: e.dma_start(
                out=dst[:], in_=src_ap), inc=d_sy, inc_amt=16)
        d_b2 = P.emit("gpsimd", lambda e: e.dma_start(out=B2T[:], in_=b2m),
                      inc=d_gp, inc_amt=16)
        d_sb2 = P.emit("gpsimd", lambda e: e.dma_start(out=SB2R[:], in_=sb2r),
                       inc=d_gp, inc_amt=16)
        d_gw = P.emit("gpsimd", lambda e: e.dma_start(
            out=GWTB[:].rearrange("p (dc e) -> p dc e", e=8), in_=gwt_r),
            inc=d_gp, inc_amt=16)
        P.emit("gpsimd", lambda e: e.memset(onesf[:], 1.0))
        P.emit("gpsimd", lambda e: e.memset(onesb[:], 1.0))
        P.emit("gpsimd", lambda e: e.memset(eps128[:], LN_EPS))
        P.emit("gpsimd", lambda e: e.memset(invd128[:], INV_D))
        P.emit("gpsimd", lambda e: e.memset(negbig[:], -1e30))
        n_eps = P.emit("gpsimd", lambda e: e.memset(neg1128[:], -1.0), inc=d_gp)
        ngb = P.emit("tensor", lambda e: e.matmul(pg[:], onesf[:], GBR1[:],
                                                  start=True, stop=True),
                     waits=[(d_sy, d_small), (d_gp, n_eps)], inc=c_pe)
        ngbe = P.emit("vector", lambda e: e.tensor_copy(GBB[:], pg[:]),
                      waits=[(c_pe, ngb)], inc=c_dve)

        x_cons = {}
        for t in range(NT):
            slot = t % 2
            w = [(c_dve, x_cons[slot][0]) if slot in x_cons else None,
                 (c_act, x_cons[slot][1]) if slot in x_cons else None]
            nl = P.emit("sync", lambda e, t=t, slot=slot: e.dma_start(
                out=XB[:, slot * D:(slot + 1) * D], in_=x_t[t]), waits=w,
                inc=d_sy, inc_amt=16)
            src = XB[:, slot * D:(slot + 1) * D]
            nd = P.emit("vector", lambda e, src=src, t=t: e.reduce_sum(
                ln1.sall[:, t:t + 1], src, axis=AX),
                waits=[(d_sy, nl), (c_dve, ln1.q_zeroed)], inc=c_dve)
            ln1.stat_dves.append(nd)
            na = P.emit("scalar", lambda e, src=src, t=t: e.activation(
                sqscr[:], src, AF.Square, accum_out=ln1.qall[:, t:t + 1]),
                waits=[(d_sy, nl), (c_dve, ln1.q_zeroed)], inc=c_act)
            ln1.stat_acts.append(na)
            x_cons[slot] = (nd, na)
        ln1.emit_math(P, c_act, c_dve, eps128, invd128, neg1128, eps_wait=(d_gp, n_eps))

        xh_free = {}
        xft_store = {}
        tr_free = [None, None]
        xh_stores = []
        for t in range(NT):
            slot = t % 2
            p = t % 2
            w = [(c_dve, x_cons[slot][0]), (c_act, x_cons[slot][1])]
            nl = P.emit("sync", lambda e, t=t, slot=slot: e.dma_start(
                out=XB[:, slot * D:(slot + 1) * D], in_=x_t[t]), waits=w,
                inc=d_sy, inc_amt=16)
            src = XB[:, slot * D:(slot + 1) * D]
            w = [(d_sy, nl), (c_dve, ln1.ready)]
            if p in xh_free:
                w.append((c_pe, xh_free[p]))
            sc, bi = ln1.apply(t)
            nap = P.emit("scalar", lambda e, src=src, p=p, sc=sc, bi=bi: e.activation(
                XH[:, p * D:(p + 1) * D], src, AF.Identity, bias=bi, scale=sc),
                waits=w, inc=c_act)
            x_cons[slot] = (x_cons[slot][0], nap)
            last_tp = None
            last_ev = None
            for c in range(8):
                tp = c % 2
                w = [(c_act, nap)] if c < 2 else []
                if tr_free[tp] is not None:
                    w.append((c_dve, tr_free[tp]))
                if t == 0 and c == 0:
                    w.append((d_sy, d_small))
                if c < 2 and p in xft_store:
                    w.append(xft_store[p])
                last_tp = P.emit("tensor", lambda e, p=p, c=c, tp=tp: e.transpose(
                    ptr[tp][:], XH[:, p * D + c * 128: p * D + (c + 1) * 128],
                    idt[:]), waits=w, inc=c_pe)
                last_ev = P.emit("vector", lambda e, p=p, c=c, tp=tp: e.tensor_copy(
                    XFT[:, p * D + c * 128: p * D + (c + 1) * 128], ptr[tp][:]),
                    waits=[(c_pe, last_tp)], inc=c_dve)
                tr_free[tp] = last_ev
            xh_free[p] = last_tp
            nst = P.emit("sync", lambda e, t=t, p=p: e.dma_start(
                out=xh_w[:, :, t * 128:(t + 1) * 128],
                in_=XFT[:, p * D:(p + 1) * D].rearrange("p (c i) -> p c i", i=128)),
                waits=[(c_dve, last_ev)], inc=d_sy, inc_amt=16)
            xft_store[p] = (d_sy, nst)
            xh_stores.append(nst)
        xh_all = xh_stores[-1]

        gt_war = {}
        rwf_done = []
        xstr_war = {}
        xstr_ld = {}
        dve_chain = None
        rw_tp = None
        for tc in range(8):
            sp = tc % 2
            w = [(d_sy, xh_all)]
            if sp in xstr_war:
                w.append(xstr_war[sp])
            nxl = P.emit("sync", lambda e, tc=tc, sp=sp: e.dma_start(
                out=XSTR[:, sp * 4096:(sp + 1) * 4096].rearrange("p (c n) -> p c n", n=512),
                in_=xh_w[:, :, tc * 512:(tc + 1) * 512]), waits=w,
                inc=d_sy, inc_amt=16)
            xstr_ld[tc] = nxl
            for tsub in range(4):
                t = 4 * tc + tsub
                gp_ = t % 2
                w = [(d_sy, nxl), (d_gp, d_gw)]
                if gp_ in gt_war:
                    w.append(gt_war[gp_])
                nmm = None
                for dc in range(8):
                    nmm = P.emit("tensor", lambda e, sp=sp, tsub=tsub, dc=dc: e.matmul(
                        pg[:],
                        XSTR[:, sp * 4096 + dc * 512 + tsub * 128: sp * 4096 + dc * 512 + (tsub + 1) * 128],
                        GWTB[:, dc * 8:(dc + 1) * 8],
                        start=(dc == 0), stop=(dc == 7)),
                        waits=(w if dc == 0 else []), inc=(c_pe if dc == 7 else None))
                ng = P.emit("scalar", lambda e, gp_=gp_: e.activation(
                    GT[:, gp_ * 8:(gp_ + 1) * 8], pg[:], AF.Identity),
                    waits=[(c_pe, nmm)], inc=c_act)
                gt_war[gp_] = (c_act, ng)
                gtv = GT[:, gp_ * 8:(gp_ + 1) * 8]
                def dve(bld, extra=()):
                    nonlocal dve_chain
                    wts = [(c_dve, dve_chain)] if dve_chain else []
                    dve_chain = P.emit("vector", bld, waits=list(extra) + wts, inc=c_dve)
                    return dve_chain
                dve(lambda e, gtv=gtv: e.reduce_max(GM[:], gtv, axis=AX),
                    extra=[(c_act, ng)])
                dve(lambda e, gtv=gtv: e.tensor_scalar(GSC[:], gtv, GM[:], None, ALU.subtract))
                ne = P.emit("scalar", lambda e: e.activation(GE[:], GSC[:], AF.Exp),
                            waits=[(c_dve, dve_chain)], inc=c_act)
                dve(lambda e: e.reduce_sum(GM[:], GE[:], axis=AX), extra=[(c_act, ne)])
                dve(lambda e: e.reciprocal(GM[:], GM[:]))
                dve(lambda e: e.tensor_scalar(GP[:], GE[:], GM[:], None, ALU.mult))
                dve(lambda e: e.tensor_tensor(GBT[:], GP[:], GBB[:], ALU.add),
                    extra=[(c_dve, ngbe)])
                dve(lambda e: e.reduce_max(GM[:], GBT[:], axis=AX))
                dve(lambda e: e.tensor_scalar(GSC[:], GBT[:], GM[:], None, ALU.is_ge))
                dve(lambda e: e.scalar_tensor_tensor(GSC[:], GSC[:], negbig[:], GBT[:],
                                                     ALU.mult, ALU.add))
                dve(lambda e: e.reduce_max(GM[:], GSC[:], axis=AX))
                dve(lambda e: e.tensor_scalar(GSC[:], GBT[:], GM[:], None, ALU.is_ge))
                nrw = dve(lambda e, t=t: e.tensor_tensor(
                    RW[:, t * 8:(t + 1) * 8], GP[:], GSC[:], ALU.mult))
                w = [(c_dve, nrw)]
                if rw_tp is not None:
                    w.append((c_pe, rw_tp))
                nrb = P.emit("scalar", lambda e, t=t: e.copy(RWB[:], RW[:, t * 8:(t + 1) * 8]),
                             waits=w, inc=c_act)
                w = [(c_act, nrb)]
                if tr_free[0] is not None:
                    w.append((c_dve, tr_free[0]))
                rw_tp = P.emit("tensor", lambda e: e.transpose(
                    ptr[0][0:8, :], RWB[:], idt[:]), waits=w, inc=c_pe)
                nrf = P.emit("vector", lambda e, t=t: e.tensor_copy(
                    RWF[:, t * 128:(t + 1) * 128], ptr[0][0:8, :]),
                    waits=[(c_pe, rw_tp)], inc=c_dve)
                tr_free[0] = nrf
                rwf_done.append(nrf)
            xstr_war[sp] = (c_pe, P.val(c_pe))

        init_done = []
        yin_war = {}
        pai = [pa1, pa3]
        pai_free = [None, None]
        for t in range(NT):
            p = t % 2
            evs = []
            for half in range(2):
                bank = pai[half]
                w = [(c_dve, rwf_done[t]), (d_gp, d_sb2), (d_gp, d_b2)]
                if pai_free[half] is not None:
                    w.append(pai_free[half])
                P.emit("tensor", lambda e, t=t, half=half, bank=bank: e.matmul(
                    bank[:], RWF[:, t * 128:(t + 1) * 128],
                    B2T[:, half * 512:(half + 1) * 512], start=True, stop=False),
                    waits=w)
                nm2 = P.emit("tensor", lambda e, half=half, bank=bank: e.matmul(
                    bank[:], onesb[:], SB2R[:, half * 512:(half + 1) * 512],
                    start=False, stop=True), inc=c_pe)
                w = [(c_pe, nm2)]
                if p in yin_war:
                    w.append(yin_war[p])
                nev = P.emit("scalar", lambda e, p=p, half=half, bank=bank: e.activation(
                    TMP[:, p * D + half * 512: p * D + half * 512 + 512],
                    bank[:], AF.Identity), waits=w, inc=c_act)
                pai_free[half] = (c_act, nev)
                evs.append(nev)
            nst = P.emit("sync", lambda e, t=t, p=p: e.dma_start(
                out=y_t[t], in_=TMP[:, p * D:(p + 1) * D]),
                waits=[(c_act, evs[-1])], inc=d_sy, inc_amt=16)
            yin_war[p] = (d_sy, nst)
            init_done.append(nst)
        init_all = init_done[-1]

        region_last = [None] * 8
        oesb_war = None
        last_a_mm = None
        last_o_mm = None
        g1_war = pai_free[0]
        g3_war = pai_free[1]
        pp_war = None
        ht_war = [None] * 8
        po_free = [None, None]
        for e_i in range(9):
            shared = (e_i == 8)
            w = []
            if last_a_mm is not None:
                w.append((c_pe, last_a_mm))
            s1 = sw1_r if shared else w1_r[e_i]
            s3 = sw3_r if shared else w3_r[e_i]
            s2 = sw2_r if shared else w2_r[e_i]
            P.emit("gpsimd", lambda e, s1=s1: e.dma_start(
                out=W1B[:].rearrange("p (dc h) -> p dc h", h=H), in_=s1), waits=w,
                inc=d_gp, inc_amt=16)
            P.emit("gpsimd", lambda e, s3=s3: e.dma_start(
                out=W3B[:].rearrange("p (dc h) -> p dc h", h=H), in_=s3),
                inc=d_gp, inc_amt=16)
            w2w = []
            if last_o_mm is not None:
                w2w.append((c_pe, last_o_mm))
            wl = P.emit("gpsimd", lambda e, s2=s2: e.dma_start(
                out=W2B[:].rearrange("p (hc d) -> p hc d", d=D), in_=s2), waits=w2w,
                inc=d_gp, inc_amt=16)
            for tc in range(8):
                sp = tc % 2
                w = [(d_sy, xh_all)]
                if sp in xstr_war:
                    w.append(xstr_war[sp])
                nxl = P.emit("sync", lambda e, tc=tc, sp=sp: e.dma_start(
                    out=XSTR[:, sp * 4096:(sp + 1) * 4096].rearrange(
                        "p (c n) -> p c n", n=512),
                    in_=xh_w[:, :, tc * 512:(tc + 1) * 512]), waits=w,
                    inc=d_sy, inc_amt=16)
                nh = None
                nh_list = []
                for hc in range(NHC):
                    for dc in range(8):
                        w = []
                        if dc == 0:
                            w = [(d_gp, wl), (d_sy, nxl)]
                            if g1_war is not None:
                                w.append(g1_war)
                        nmm = P.emit("tensor", lambda e, sp=sp, hc=hc, dc=dc: e.matmul(
                            pa1[:],
                            W1B[:, dc * H + hc * 128: dc * H + (hc + 1) * 128],
                            XSTR[:, sp * 4096 + dc * 512: sp * 4096 + (dc + 1) * 512],
                            start=(dc == 0), stop=(dc == 7)),
                            waits=w, inc=(c_pe if dc == 7 else None))
                    a1mm = nmm
                    for dc in range(8):
                        w = []
                        if dc == 0 and g3_war is not None:
                            w = [g3_war]
                        nmm = P.emit("tensor", lambda e, sp=sp, hc=hc, dc=dc: e.matmul(
                            pa3[:],
                            W3B[:, dc * H + hc * 128: dc * H + (hc + 1) * 128],
                            XSTR[:, sp * 4096 + dc * 512: sp * 4096 + (dc + 1) * 512],
                            start=(dc == 0), stop=(dc == 7)),
                            waits=w, inc=(c_pe if dc == 7 else None))
                    a3mm = nmm
                    last_a_mm = a3mm
                    bslice = (SB1S[:, hc:hc + 1] if shared
                              else B1S[:, e_i * NHC + hc: e_i * NHC + hc + 1])
                    b3slice = (SB3S[:, hc:hc + 1] if shared
                               else B3S[:, e_i * NHC + hc: e_i * NHC + hc + 1])
                    w = [(c_pe, a1mm)]
                    if pp_war is not None:
                        w.append(pp_war)
                    ng1 = P.emit("scalar", lambda e, bslice=bslice: e.activation(
                        GS1[:], pa1[:], AF.Identity, bias=bslice), waits=w, inc=c_act)
                    g1_war = (c_act, ng1)
                    w = [(c_pe, a3mm)]
                    if pp_war is not None:
                        w.append(pp_war)
                    ng3 = P.emit("vector", lambda e, b3slice=b3slice: e.tensor_scalar(
                        GS3[:], pa3[:], b3slice, None, ALU.add), waits=w, inc=c_dve)
                    g3_war = (c_dve, ng3)
                    npp = P.emit("gpsimd", lambda e: e.tensor_tensor(
                        PSC[:], GS1[:], GS3[:], ALU.mult),
                        waits=[(c_act, ng1), (c_dve, ng3)], inc=d_gp, inc_amt=1)
                    pp_war = (d_gp, npp)
                    w = [(d_gp, npp)]
                    if ht_war[tc] is not None:
                        w.append(ht_war[tc])
                    nh = P.emit("scalar", lambda e, hc=hc: e.activation(
                        HT[:, hc * 512:(hc + 1) * 512], PSC[:], AF.Silu),
                        waits=w, inc=c_act)
                    nh_list.append(nh)
                h_done = nh
                last_ev = None
                for tsub in range(4):
                    for dh in range(2):
                        pb = (tsub * 2 + dh) % 2
                        for hc in range(NHC):
                            w = [(c_act, nh_list[hc])]
                            if hc == 0 and po_free[pb] is not None:
                                w.append(po_free[pb])
                            nmm = P.emit("tensor", lambda e, tsub=tsub, dh=dh, hc=hc, pb=pb: e.matmul(
                                po[pb][:],
                                HT[:, hc * 512 + tsub * 128: hc * 512 + (tsub + 1) * 128],
                                W2B[:, hc * D + dh * 512: hc * D + dh * 512 + 512],
                                start=(hc == 0), stop=(hc == NHC - 1)),
                                waits=w, inc=(c_pe if hc == NHC - 1 else None))
                        last_o_mm = nmm
                        ht_war[tc] = (c_pe, nmm)
                        dst = OESB[:, tsub * D + dh * 512: tsub * D + dh * 512 + 512]
                        w = [(c_pe, nmm)]
                        if oesb_war is not None:
                            w.append(oesb_war)
                        if last_ev is not None:
                            w.append((c_dve, last_ev))
                        if shared:
                            last_ev = P.emit("vector", lambda e, dst=dst, pb=pb: e.tensor_copy(
                                dst, po[pb][:]), waits=w, inc=c_dve)
                        else:
                            rwsl = RW[:, (4 * tc + tsub) * 8 + e_i:
                                      (4 * tc + tsub) * 8 + e_i + 1]
                            last_ev = P.emit("vector",
                                lambda e, dst=dst, pb=pb, rwsl=rwsl: e.tensor_scalar(
                                    dst, po[pb][:], rwsl, None, ALU.mult),
                                waits=w, inc=c_dve)
                        po_free[pb] = (c_dve, last_ev)
                w = [(c_dve, last_ev), (d_sy, init_all)]
                if region_last[tc] is not None:
                    w.append((d_gp, region_last[tc]))
                nadd = P.emit("gpsimd", lambda e, tc=tc: e.dma_start(
                    out=y_r[:, 4 * tc:4 * tc + 4, :],
                    in_=OESB[:].rearrange("p (T d) -> p T d", d=D),
                    accum_op=ALU.add), waits=w, inc=d_gp, inc_amt=16)
                region_last[tc] = nadd
                oesb_war = (d_gp, nadd)
                xstr_war[sp] = (c_pe, last_a_mm)

        yb_cons = {}
        for t in range(NT):
            slot = t % 2
            w = [(d_gp, region_last[t // 4])]
            if slot in yb_cons:
                w += [(c_dve, yb_cons[slot][0]), (c_act, yb_cons[slot][1])]
            nl = P.emit("sync", lambda e, t=t, slot=slot: e.dma_start(
                out=YBF[:, slot * D:(slot + 1) * D], in_=y_t[t]), waits=w,
                inc=d_sy, inc_amt=16)
            src = YBF[:, slot * D:(slot + 1) * D]
            nd = P.emit("vector", lambda e, src=src, t=t: e.reduce_sum(
                ln2.sall[:, t:t + 1], src, axis=AX),
                waits=[(d_sy, nl), (c_dve, ln2.q_zeroed)], inc=c_dve)
            ln2.stat_dves.append(nd)
            na = P.emit("scalar", lambda e, src=src, t=t: e.activation(
                sqscr[:], src, AF.Square, accum_out=ln2.qall[:, t:t + 1]),
                waits=[(d_sy, nl), (c_dve, ln2.q_zeroed)], inc=c_act)
            ln2.stat_acts.append(na)
            yb_cons[slot] = (nd, na)
        ln2.emit_math(P, c_act, c_dve, eps128, invd128, neg1128, eps_wait=(d_gp, n_eps))

        yn_war = {}
        ot_free = {}
        for t in range(NT):
            slot = t % 2
            p = t % 2
            w = [(c_dve, x_cons[slot][0]), (c_act, x_cons[slot][1])]
            nlx = P.emit("sync", lambda e, t=t, slot=slot: e.dma_start(
                out=XB[:, slot * D:(slot + 1) * D], in_=x_t[t]), waits=w,
                inc=d_sy, inc_amt=16)
            w = [(c_dve, yb_cons[slot][0]), (c_act, yb_cons[slot][1])]
            nly = P.emit("sync", lambda e, t=t, slot=slot: e.dma_start(
                out=YBF[:, slot * D:(slot + 1) * D], in_=y_t[t]), waits=w,
                inc=d_sy, inc_amt=16)
            sc, bi = ln2.apply(t)
            w = [(d_sy, nly), (c_dve, ln2.ready)]
            if p in yn_war:
                w.append(yn_war[p])
            na = P.emit("scalar", lambda e, p=p, sc=sc, bi=bi, slot=slot: e.activation(
                TMP[:, p * D:(p + 1) * D], YBF[:, slot * D:(slot + 1) * D],
                AF.Identity, bias=bi, scale=sc), waits=w, inc=c_act)
            yb_cons[slot] = (yb_cons[slot][0], na)
            w = [(d_sy, nlx), (c_act, na)]
            if p in ot_free:
                w.append(ot_free[p])
            no = P.emit("vector", lambda e, t=t, p=p, slot=slot: e.tensor_tensor(
                OT[:, p * D:(p + 1) * D], XB[:, slot * D:(slot + 1) * D],
                TMP[:, p * D:(p + 1) * D], ALU.add), waits=w, inc=c_dve)
            yn_war[p] = (c_dve, no)
            x_cons[slot] = (no, x_cons[slot][1])
            ns = P.emit("sync", lambda e, t=t, p=p: e.dma_start(
                out=o_t[t], in_=OT[:, p * D:(p + 1) * D]), waits=[(c_dve, no)],
                inc=d_sy, inc_amt=16)
            ot_free[p] = (d_sy, ns)
            _heat(P, c_pe, pa1, idt, (d_sy, nlx), (c_act, P.val(c_act)))

    return


def _l0ffn_inmaps(inputs):
    b1 = inputs["L0_ffn_b1"]
    b3 = inputs["L0_ffn_b3"]
    return {
        "f_w1": inputs["L0_ffn_w1"],
        "f_w3": inputs["L0_ffn_w3"],
        "f_w2": inputs["L0_ffn_w2"],
        "f_sw1": inputs["L0_ffn_sw1"],
        "f_sw3": inputs["L0_ffn_sw3"],
        "f_sw2": inputs["L0_ffn_sw2"],
        "f_b1r": np.ascontiguousarray(b1.reshape(8, 11, 128).transpose(2, 0, 1).reshape(128, 88)),
        "f_b3r": np.ascontiguousarray(b3.reshape(8, 11, 128).transpose(2, 0, 1).reshape(128, 88)),
        "f_b2m": inputs["L0_ffn_b2"],
        "f_sb1r": np.ascontiguousarray(inputs["L0_ffn_sb1"].reshape(11, 128).T),
        "f_sb3r": np.ascontiguousarray(inputs["L0_ffn_sb3"].reshape(11, 128).T),
        "f_sb2r": inputs["L0_ffn_sb2"].reshape(1, D),
        "f_gwt": np.ascontiguousarray(inputs["L0_ffn_gate_w"].T),
        "f_gbr": inputs["L0_ffn_gate_b"].reshape(1, 8),
    }






def emit_l0attn(nc, P, sems, x_in, out, ident, outer=None):
    c_pe, c_act, c_dve, d_sy, d_gp = sems
    wqt = nc.dram_tensor("a_wqt", [D, D], F32, kind="ExternalInput").ap()
    wvt = nc.dram_tensor("a_wvt", [D, D], F32, kind="ExternalInput").ap()
    wat = nc.dram_tensor("a_wat", [64, 64], F32, kind="ExternalInput").ap()
    wot = nc.dram_tensor("a_wot", [D, D], F32, kind="ExternalInput").ap()
    borow = nc.dram_tensor("a_borow", [1, D], F32, kind="ExternalInput").ap()
    abuf = nc.dram_tensor("a_abuf", [NT * 128, D], BF16, kind="Internal").ap()
    atb = nc.dram_tensor("a_atb", [D, NT * 128], BF16, kind="Internal").ap()

    x_t = x_in.rearrange("(t p) d -> t p d", p=128)
    o_t = out.rearrange("(t p) d -> t p d", p=128)
    a_t = abuf.rearrange("(t p) d -> t p d", p=128)
    at_w = atb.rearrange("(c r) t -> r c t", r=128)
    at_r = atb.rearrange("(h a) t -> a h t", a=64)
    wqt_r = wqt.rearrange("(dc p) q -> p dc q", p=128)
    wvt_r = wvt.rearrange("(dc p) q -> p dc q", p=128)
    wot_r = wot.rearrange("(ic p) q -> p ic q", p=128)

    with (outer if outer is not None else contextlib.ExitStack()) as stack:
        sb = lambda nm, sh, dt: stack.enter_context(nc.sbuf_tensor("a" + nm, sh, dt))
        ps = lambda nm, sh, dt: stack.enter_context(nc.psum_tensor("a" + nm, sh, dt))

        XB = sb("XB", [128, 2 * D], F32)
        XH = sb("XH", [128, 2 * D], BF16)
        XFT = sb("XFT", [128, 2 * D], BF16)
        WQT = sb("WQT", [128, 8 * D], BF16)
        WVT = sb("WVT", [128, 8 * D], BF16)
        WOT = sb("WOT", [128, 8 * D], BF16)
        WAT = sb("WAT", [128, 64], BF16)
        BOR = sb("BOR", [1, D], BF16)
        QKF = sb("QKF", [64, 2 * 2048], BF16)
        VEX = sb("VEX", [128, 2 * D], BF16)
        SCT = sb("SCT", [128, D], F32)
        ETL = sb("ETL", [128, D], F32)
        ATN = sb("ATN", [128, 2 * D], BF16)
        ATT = sb("ATT", [128, 2 * D], BF16)
        LRC = sb("LRC", [128, 16], F32)
        LRC2 = sb("LRC2", [128, 16], F32)
        RS = sb("RS", [64, 16], F32)
        OU2S = sb("OU2S", [64, 16 * 64], BF16)
        ATL = sb("ATL", [64, 2 * 2 * D], BF16)
        FOF = sb("FOF", [128, 2 * D], BF16)
        OB = sb("OB", [128, 2 * D], F32)
        ABL = sb("ABL", [128, 2 * D], BF16)
        YN = sb("YN", [128, 2 * D], F32)
        OT = sb("OT", [128, 2 * D], F32)
        onesf = sb("onesf", [1, 128], F32)
        onesb = sb("onesb", [1, 128], BF16)
        onec = sb("onec", [128, 1], BF16)
        eps128 = sb("eps128", [128, 1], F32)
        invd128 = sb("invd128", [128, 1], F32)
        neg1128 = sb("neg1128", [128, 1], F32)
        idt = sb("idt", [128, 128], BF16)
        sqscr = sb("sqscr", [128, D], F32)
        ptr = [ps("ptr0", [128, 128], BF16)]
        pA = ps("pA", [128, 512], F32)
        pB = ps("pB", [128, 512], F32)
        psc = [ps("psc0", [128, 512], F32), ps("psc1", [128, 512], F32)]
        psums = ps("psums", [64, 16], F32)
        OUACC = sb("OUACC", [64, 16 * 64], F32)
        SACC = sb("SACC", [64, 16], F32)

        ln1 = LNPhase(stack, nc, "aln1")
        ln2 = LNPhase(stack, nc, "aln2")

        ln1.emit_zero(P, c_dve)
        ln2.emit_zero(P, c_dve)
        P.emit("vector", lambda e: e.memset(OUACC[:], 0.0))
        acc_zero = P.emit("vector", lambda e: e.memset(SACC[:], 0.0), inc=c_dve)

        d_id = P.emit("sync", lambda e: e.dma_start(out=idt[:], in_=ident),
                      inc=d_sy, inc_amt=16)
        d_wq = P.emit("gpsimd", lambda e: e.dma_start(
            out=WQT[:].rearrange("p (dc q) -> p dc q", q=D), in_=wqt_r),
            inc=d_gp, inc_amt=16)
        d_wv = P.emit("gpsimd", lambda e: e.dma_start(
            out=WVT[:].rearrange("p (dc q) -> p dc q", q=D), in_=wvt_r),
            inc=d_gp, inc_amt=16)
        d_wo = P.emit("gpsimd", lambda e: e.dma_start(
            out=WOT[:].rearrange("p (ic q) -> p ic q", q=D), in_=wot_r),
            inc=d_gp, inc_amt=16)
        P.emit("gpsimd", lambda e: e.dma_start(out=WAT[0:64, :], in_=wat),
               inc=d_gp, inc_amt=16)
        d_wa = P.emit("gpsimd", lambda e: e.dma_start(out=WAT[64:128, :], in_=wat),
                      inc=d_gp, inc_amt=16)
        d_bo = P.emit("gpsimd", lambda e: e.dma_start(out=BOR[:], in_=borow),
                      inc=d_gp, inc_amt=16)
        P.emit("gpsimd", lambda e: e.memset(onesf[:], 1.0))
        P.emit("gpsimd", lambda e: e.memset(onesb[:], 1.0))
        P.emit("gpsimd", lambda e: e.memset(onec[:], 1.0))
        P.emit("gpsimd", lambda e: e.memset(eps128[:], LN_EPS))
        P.emit("gpsimd", lambda e: e.memset(invd128[:], INV_D))
        n_eps = P.emit("gpsimd", lambda e: e.memset(neg1128[:], -1.0), inc=d_gp)

        x_cons = {}
        for t in range(NT):
            slot = t % 2
            w = [(c_dve, x_cons[slot][0]) if slot in x_cons else None,
                 (c_act, x_cons[slot][1]) if slot in x_cons else None]
            nl = P.emit("sync", lambda e, t=t, slot=slot: e.dma_start(
                out=XB[:, slot * D:(slot + 1) * D], in_=x_t[t]), waits=w,
                inc=d_sy, inc_amt=16)
            src = XB[:, slot * D:(slot + 1) * D]
            nd = P.emit("vector", lambda e, src=src, t=t: e.reduce_sum(
                ln1.sall[:, t:t + 1], src, axis=AX),
                waits=[(d_sy, nl), (c_dve, ln1.q_zeroed)], inc=c_dve)
            ln1.stat_dves.append(nd)
            na = P.emit("scalar", lambda e, src=src, t=t: e.activation(
                sqscr[:], src, AF.Square, accum_out=ln1.qall[:, t:t + 1]),
                waits=[(d_sy, nl), (c_dve, ln1.q_zeroed)], inc=c_act)
            ln1.stat_acts.append(na)
            x_cons[slot] = (nd, na)
            _heat(P, c_pe, pA, idt, (d_sy, nl))
        ln1.emit_math(P, c_act, c_dve, eps128, invd128, neg1128, eps_wait=(d_gp, n_eps))

        xh_free = {}
        tr_free = None
        xft_cons = {}
        qkf_cons = {}
        vex_cons = {}
        atn_cons = {}
        att_store = {}
        sct_war = None
        etl_war = None
        lrc_war = None
        pa_free = None
        pb_free = None
        psc_free = [None, None]
        at_stores = []
        ou_last = [None, None]
        ps_last = None
        for t in range(NT):
            slot = t % 2
            p = t % 2
            w = [(c_dve, x_cons[slot][0]), (c_act, x_cons[slot][1])]
            nl = P.emit("sync", lambda e, t=t, slot=slot: e.dma_start(
                out=XB[:, slot * D:(slot + 1) * D], in_=x_t[t]), waits=w,
                inc=d_sy, inc_amt=16)
            src = XB[:, slot * D:(slot + 1) * D]
            w = [(d_sy, nl), (c_dve, ln1.ready)]
            if p in xh_free:
                w.append((c_pe, xh_free[p]))
            sc, bi = ln1.apply(t)
            nap = P.emit("scalar", lambda e, src=src, p=p, sc=sc, bi=bi: e.activation(
                XH[:, p * D:(p + 1) * D], src, AF.Identity, bias=bi, scale=sc),
                waits=w, inc=c_act)
            x_cons[slot] = (x_cons[slot][0], nap)
            last_tp = None
            last_ev = None
            for c in range(8):
                w = [(c_act, nap)] if c == 0 else []
                if tr_free is not None:
                    w.append((c_dve, tr_free))
                if t == 0 and c == 0:
                    w.append((d_sy, d_id))
                if c == 0 and p in xft_cons:
                    w.append(xft_cons[p])
                last_tp = P.emit("tensor", lambda e, p=p, c=c: e.transpose(
                    ptr[0][:], XH[:, p * D + c * 128: p * D + (c + 1) * 128],
                    idt[:]), waits=w, inc=c_pe)
                last_ev = P.emit("vector", lambda e, p=p, c=c: e.tensor_copy(
                    XFT[:, p * D + c * 128: p * D + (c + 1) * 128], ptr[0][:]),
                    waits=[(c_pe, last_tp)], inc=c_dve)
                tr_free = last_ev
            xh_free[p] = last_tp
            xft_ready = last_ev
            qk_evh = {}
            qk_evs = None
            for h in range(16):
                bank = pA if h % 2 == 0 else pB
                col = ((h // 2) % 2) * 128
                for dc in range(8):
                    w = []
                    if dc == 0:
                        w = [(c_dve, xft_ready), (d_gp, d_wq)]
                        if h < 2:
                            if h % 2 == 0 and pa_free is not None:
                                w.append(pa_free)
                            if h % 2 == 1 and pb_free is not None:
                                w.append(pb_free)
                        if h >= 2:
                            w.append((c_dve, qk_evh[h - 2]))
                        if p in qkf_cons and h < 2:
                            w.append(qkf_cons[p])
                    nmm = P.emit("tensor", lambda e, p=p, h=h, dc=dc, bank=bank, col=col: e.matmul(
                        bank[0:64, col:col + 128],
                        WQT[:, dc * D + h * 64: dc * D + h * 64 + 64],
                        XFT[:, p * D + dc * 128: p * D + (dc + 1) * 128],
                        start=(dc == 0), stop=(dc == 7)),
                        waits=w, inc=(c_pe if dc == 7 else None))
                qk_evs = P.emit("vector", lambda e, p=p, h=h, bank=bank, col=col: e.tensor_copy(
                    QKF[:, p * 2048 + h * 128: p * 2048 + (h + 1) * 128],
                    bank[0:64, col:col + 128]),
                    waits=[(c_pe, nmm)], inc=c_dve)
                qk_evh[h] = qk_evs
            pa_free = (c_dve, qk_evh[14])
            pb_free0 = (c_dve, qk_evh[15])
            v_evs = None
            for half in range(2):
                for dc in range(8):
                    w = []
                    if dc == 0:
                        w = [(c_dve, xft_ready), (d_gp, d_wv), pb_free0]
                        if pb_free is not None:
                            w.append(pb_free)
                        if half == 1:
                            w.append((c_dve, v_evs))
                    nmm = P.emit("tensor", lambda e, p=p, half=half, dc=dc: e.matmul(
                        pB[:],
                        XFT[:, p * D + dc * 128: p * D + (dc + 1) * 128],
                        WVT[:, dc * D + half * 512: dc * D + half * 512 + 512],
                        start=(dc == 0), stop=(dc == 7)),
                        waits=w, inc=(c_pe if dc == 7 else None))
                w = [(c_pe, nmm)]
                if p in vex_cons:
                    w.append(vex_cons[p])
                v_evs = P.emit("vector", lambda e, p=p, half=half: e.tensor_copy(
                    VEX[:, p * D + half * 512: p * D + half * 512 + 512], pB[:]),
                    waits=w, inc=c_dve)
            pb_free = (c_dve, v_evs)
            vex_ready = v_evs
            exp_acts = []
            for bank in range(2):
                nmm = None
                for hh in range(8):
                    h = bank * 8 + hh
                    w = [(c_dve, qk_evs), (d_gp, d_wa)]
                    if hh == 0 and psc_free[bank] is not None:
                        w.append(psc_free[bank])
                    nmm = P.emit("tensor", lambda e, p=p, h=h, hh=hh, bank=bank: e.matmul(
                        psc[bank][:, hh * 64:(hh + 1) * 64],
                        QKF[0:64, p * 2048 + h * 128: p * 2048 + (h + 1) * 128],
                        WAT[0:64, :], start=True, stop=True),
                        waits=(w if hh == 0 else []), inc=(c_pe if hh == 7 else None))
                w = [(c_pe, nmm)]
                if sct_war is not None:
                    w.append(sct_war)
                nex = P.emit("scalar", lambda e, bank=bank: e.activation(
                    ETL[:, bank * 512:(bank + 1) * 512], psc[bank][:], AF.Exp,
                    scale=0.125), waits=w, inc=c_act)
                exp_acts.append(nex)
                psc_free[bank] = (c_act, nex)
            sct_war = (c_act, exp_acts[-1])
            w = [(c_act, exp_acts[-1])]
            if lrc_war is not None:
                w.append(lrc_war)
            for hq in range(16):
                nls = P.emit("vector", lambda e, hq=hq: e.reduce_sum(
                    LRC[:, hq:hq + 1], ETL[:, hq * 64:(hq + 1) * 64], axis=AX),
                    waits=(w if hq == 0 else []), inc=(c_dve if hq == 15 else None))
            nlr = P.emit("vector", lambda e: e.reciprocal(LRC2[:], LRC[:]),
                         waits=[(c_dve, nls)], inc=c_dve)
            at_ops_d = None
            at_ops_a = None
            for h in range(16):
                lsl = LRC2[:, h:h + 1]
                esl = ETL[:, h * 64:(h + 1) * 64]
                dsl = ATN[:, p * D + h * 64: p * D + (h + 1) * 64]
                w = [(c_dve, nlr)]
                if h < 2 and p in atn_cons:
                    w.append(atn_cons[p])
                if h % 2 == 0:
                    at_ops_d = P.emit("vector", lambda e, dsl=dsl, esl=esl, lsl=lsl: e.tensor_scalar(
                        dsl, esl, lsl, None, ALU.mult), waits=w, inc=c_dve)
                else:
                    at_ops_a = P.emit("scalar", lambda e, dsl=dsl, esl=esl, lsl=lsl: e.activation(
                        dsl, esl, AF.Identity, scale=lsl), waits=w, inc=c_act)
            lrc_war = (c_dve, at_ops_d)
            atn_ready = [(c_dve, at_ops_d), (c_act, at_ops_a)]
            add_waits = [(c_dve, acc_zero)]
            for bank in range(2):
                nmm = None
                for hh in range(8):
                    h = bank * 8 + hh
                    w = list(atn_ready) + [(c_dve, vex_ready)]
                    if t == 0 and h == 0:
                        w.append((d_gp, n_eps))
                    if hh == 0:
                        w.append(psc_free[bank])
                    nmm = P.emit("tensor", lambda e, p=p, h=h, hh=hh, bank=bank: e.matmul(
                        psc[bank][0:64, hh * 64:(hh + 1) * 64],
                        ATN[:, p * D + h * 64: p * D + (h + 1) * 64],
                        VEX[:, p * D + h * 64: p * D + (h + 1) * 64],
                        start=True, stop=True),
                        waits=(w if hh == 0 else list(atn_ready)[:0]),
                        inc=(c_pe if hh == 7 else None))
                nadd = P.emit("vector", lambda e, bank=bank: e.tensor_tensor(
                    OUACC[:, bank * 512:(bank + 1) * 512],
                    OUACC[:, bank * 512:(bank + 1) * 512],
                    psc[bank][0:64, :], ALU.add),
                    waits=[(c_pe, nmm)] + add_waits, inc=c_dve)
                psc_free[bank] = (c_dve, nadd)
            nms = None
            for h in range(16):
                w = []
                if h == 0:
                    w = list(atn_ready)
                    if ps_last is not None:
                        w.append(ps_last)
                nms = P.emit("tensor", lambda e, p=p, h=h: e.matmul(
                    psums[:, h:h + 1],
                    ATN[:, p * D + h * 64: p * D + (h + 1) * 64],
                    onec[:], start=True, stop=True),
                    waits=w, inc=(c_pe if h == 15 else None))
            nsadd = P.emit("vector", lambda e: e.tensor_tensor(
                SACC[:], SACC[:], psums[0:64, :], ALU.add),
                waits=[(c_pe, nms)] + add_waits, inc=c_dve)
            ps_last = (c_dve, nsadd)
            last_ev2 = None
            for c in range(8):
                w = list(atn_ready) if c == 0 else []
                if tr_free is not None:
                    w.append((c_dve, tr_free))
                if c == 0 and p in att_store:
                    w.append(att_store[p])
                ntp = P.emit("tensor", lambda e, p=p, c=c: e.transpose(
                    ptr[0][:], ATN[:, p * D + c * 128: p * D + (c + 1) * 128],
                    idt[:]), waits=w, inc=c_pe)
                last_ev2 = P.emit("vector", lambda e, p=p, c=c: e.tensor_copy(
                    ATT[:, p * D + c * 128: p * D + (c + 1) * 128], ptr[0][:]),
                    waits=[(c_pe, ntp)], inc=c_dve)
                tr_free = last_ev2
            nst = P.emit("sync", lambda e, t=t, p=p: e.dma_start(
                out=at_w[:, :, t * 128:(t + 1) * 128],
                in_=ATT[:, p * D:(p + 1) * D].rearrange("p (c i) -> p c i", i=128)),
                waits=[(c_dve, last_ev2)], inc=d_sy, inc_amt=16)
            att_store[p] = (d_sy, nst)
            at_stores.append(nst)
            atn_cons[p] = (c_pe, P.val(c_pe))
            qkf_cons[p] = (c_pe, P.val(c_pe))
            vex_cons[p] = (c_pe, P.val(c_pe))
            xft_cons[p] = (c_pe, P.val(c_pe))
        nrc = P.emit("vector", lambda e: e.reciprocal(RS[:], SACC[:]),
                     waits=[ps_last, psc_free[0], psc_free[1]], inc=c_dve)
        prev = nrc
        for h in range(16):
            prev = P.emit("vector", lambda e, h=h: e.tensor_scalar(
                OU2S[:, h * 64:(h + 1) * 64], OUACC[:, h * 64:(h + 1) * 64],
                RS[:, h:h + 1], None, ALU.mult), waits=[(c_dve, prev)], inc=c_dve)
        ou2_done = (c_dve, prev)

        atl_cons = {}
        fof_cons = {}
        ob_war = {}
        pfo_free = [None, None]
        pab_free = [None, None]
        ab_stores = []
        for t in range(NT):
            p = t % 2
            w = [(d_sy, at_stores[t])]
            if p in atl_cons:
                w.append(atl_cons[p])
            nla = P.emit("sync", lambda e, t=t, p=p: e.dma_start(
                out=ATL[0:64, p * 2 * D:(p + 1) * 2 * D].rearrange(
                    "a (h i) -> a h i", i=128),
                in_=at_r[:, :, t * 128:(t + 1) * 128]), waits=w,
                inc=d_sy, inc_amt=16)
            evs = None
            ev_hist = {}
            for h in range(16):
                bank = h % 2
                col = ((h // 2) % 4) * 128
                w = [(d_sy, nla)]
                if h == 0:
                    w.append(ou2_done)
                if h < 2 and psc_free[bank] is not None:
                    w.append(psc_free[bank])
                if h >= 2:
                    w.append((c_dve, ev_hist[h - 2]))
                nmm = P.emit("tensor", lambda e, h=h, bank=bank, col=col, p=p: e.matmul(
                    psc[bank][0:64, col:col + 128],
                    OU2S[:, h * 64:(h + 1) * 64],
                    ATL[0:64, p * 2 * D + h * 128: p * 2 * D + (h + 1) * 128],
                    start=True, stop=True), waits=w, inc=c_pe)
                drow = (h % 2) * 64
                dcol = (h // 2) * 128
                w = [(c_pe, nmm)]
                if h < 2 and p in fof_cons:
                    w.append(fof_cons[p])
                evs = P.emit("vector", lambda e, h=h, bank=bank, col=col, drow=drow, dcol=dcol, p=p: e.tensor_copy(
                    FOF[drow:drow + 64, p * D + dcol: p * D + dcol + 128],
                    psc[bank][0:64, col:col + 128]),
                    waits=w, inc=c_dve)
                ev_hist[h] = evs
                psc_free[bank] = (c_dve, evs)
            fof_ready = evs
            ob_evs = []
            for half in range(2):
                bank = pA if half == 0 else pB
                for ic in range(8):
                    w = []
                    if ic == 0:
                        w = [(c_dve, fof_ready), (d_gp, d_wo)]
                        w.append(pa_free if half == 0 else pb_free)
                    P.emit("tensor", lambda e, p=p, half=half, ic=ic, bank=bank: e.matmul(
                        bank[:],
                        FOF[:, p * D + ic * 128: p * D + (ic + 1) * 128],
                        WOT[:, ic * D + half * 512: ic * D + half * 512 + 512],
                        start=(ic == 0), stop=False), waits=w)
                nmm = P.emit("tensor", lambda e, half=half, bank=bank: e.matmul(
                    bank[:], onesb[:], BOR[:, half * 512: half * 512 + 512],
                    start=False, stop=True), waits=[(d_gp, d_bo)], inc=c_pe)
                w = [(c_pe, nmm)]
                if p in ob_war:
                    w += list(ob_war[p])
                nev = P.emit("scalar", lambda e, p=p, half=half, bank=bank: e.activation(
                    OB[:, p * D + half * 512: p * D + half * 512 + 512], bank[:],
                    AF.Identity), waits=w, inc=c_act)
                if half == 0:
                    pa_free = (c_act, nev)
                else:
                    pb_free = (c_act, nev)
                ob_evs.append(nev)
            nd2 = P.emit("vector", lambda e, t=t, p=p: e.reduce_sum(
                ln2.sall[:, t:t + 1], OB[:, p * D:(p + 1) * D], axis=AX),
                waits=[(c_act, ob_evs[-1]), (c_dve, ln2.q_zeroed)], inc=c_dve)
            ln2.stat_dves.append(nd2)
            na2 = P.emit("scalar", lambda e, t=t, p=p: e.activation(
                sqscr[:], OB[:, p * D:(p + 1) * D], AF.Square,
                accum_out=ln2.qall[:, t:t + 1]),
                waits=[(c_act, ob_evs[-1]), (c_dve, ln2.q_zeroed)], inc=c_act)
            ln2.stat_acts.append(na2)
            nab = P.emit("scalar", lambda e, t=t, p=p: e.copy(
                ABL[:, p * D:(p + 1) * D], OB[:, p * D:(p + 1) * D]),
                waits=([ob_war[p][0]] if p in ob_war else []), inc=c_act)
            nst = P.emit("sync", lambda e, t=t, p=p: e.dma_start(
                out=a_t[t], in_=ABL[:, p * D:(p + 1) * D]),
                waits=[(c_act, nab)], inc=d_sy, inc_amt=16)
            ab_stores.append(nst)
            ob_war[p] = ((d_sy, nst), (c_dve, nd2))
            atl_cons[p] = (c_pe, P.val(c_pe))
            fof_cons[p] = (c_pe, P.val(c_pe))
        ln2.emit_math(P, c_act, c_dve, eps128, invd128, neg1128, eps_wait=(d_gp, n_eps))

        yn_war = {}
        ot_free = {}
        abl_cons = {}
        for t in range(NT):
            slot = t % 2
            p = t % 2
            w = [(c_dve, x_cons[slot][0]), (c_act, x_cons[slot][1])]
            nlx = P.emit("sync", lambda e, t=t, slot=slot: e.dma_start(
                out=XB[:, slot * D:(slot + 1) * D], in_=x_t[t]), waits=w,
                inc=d_sy, inc_amt=16)
            w = [(d_sy, ab_stores[t])]
            if p in abl_cons:
                w.append(abl_cons[p])
            else:
                w.append((c_act, P.val(c_act)))
            nly = P.emit("sync", lambda e, t=t, p=p: e.dma_start(
                out=ABL[:, p * D:(p + 1) * D], in_=a_t[t]), waits=w,
                inc=d_sy, inc_amt=16)
            sc, bi = ln2.apply(t)
            w = [(d_sy, nly), (c_dve, ln2.ready)]
            if p in yn_war:
                w.append(yn_war[p])
            na = P.emit("scalar", lambda e, p=p, sc=sc, bi=bi: e.activation(
                YN[:, p * D:(p + 1) * D], ABL[:, p * D:(p + 1) * D],
                AF.Identity, bias=bi, scale=sc), waits=w, inc=c_act)
            abl_cons[p] = (c_act, na)
            w = [(d_sy, nlx), (c_act, na)]
            if p in ot_free:
                w.append(ot_free[p])
            no = P.emit("vector", lambda e, t=t, p=p, slot=slot: e.tensor_tensor(
                OT[:, p * D:(p + 1) * D], XB[:, slot * D:(slot + 1) * D],
                YN[:, p * D:(p + 1) * D], ALU.add), waits=w, inc=c_dve)
            yn_war[p] = (c_dve, no)
            x_cons[slot] = (no, x_cons[slot][1])
            ns = P.emit("sync", lambda e, t=t, p=p: e.dma_start(
                out=o_t[t], in_=OT[:, p * D:(p + 1) * D]), waits=[(c_dve, no)],
                inc=d_sy, inc_amt=16)
            ot_free[p] = (d_sy, ns)
            _heat(P, c_pe, pA, idt, (d_sy, nlx), (c_act, P.val(c_act)))
    return


def _l0attn_inmaps(inputs):
    wemb = inputs["L0_attn_W_emb"]
    return {
        "a_wqt": np.ascontiguousarray(wemb[:1024, :].T),
        "a_wvt": np.ascontiguousarray(wemb[1024:, :].T),
        "a_wat": np.ascontiguousarray(inputs["L0_attn_W_anchor"].T),
        "a_wot": np.ascontiguousarray(inputs["L0_attn_W_out"].T),
        "a_borow": inputs["L0_attn_b_out"].reshape(1, D),
    }






def emit_l1attn(nc, P, sems, x_in, out, ident, outer=None):
    c_pe, c_act, c_dve, d_sy, d_gp = sems
    wu = nc.dram_tensor("h_wu", [D, D], F32, kind="ExternalInput").ap()
    wv = nc.dram_tensor("h_wv", [D, D], F32, kind="ExternalInput").ap()
    wo = nc.dram_tensor("h_wo", [D, D], F32, kind="ExternalInput").ap()
    bur = nc.dram_tensor("h_bur", [1, D], F32, kind="ExternalInput").ap()
    bvr = nc.dram_tensor("h_bvr", [1, D], F32, kind="ExternalInput").ap()
    bor = nc.dram_tensor("h_bor", [1, D], F32, kind="ExternalInput").ap()
    w1f = nc.dram_tensor("h_w1f", [64, 64], F32, kind="ExternalInput").ap()
    w2f = nc.dram_tensor("h_w2f", [64, 64], F32, kind="ExternalInput").ap()
    w3f = nc.dram_tensor("h_w3f", [64, D], F32, kind="ExternalInput").ap()
    b1c = nc.dram_tensor("h_b1c", [64, 1], F32, kind="ExternalInput").ap()
    b2c = nc.dram_tensor("h_b2c", [64, 1], F32, kind="ExternalInput").ap()
    b3r = nc.dram_tensor("h_b3r", [1, D], F32, kind="ExternalInput").ap()
    pet = nc.dram_tensor("h_pet", [64, NT * 128], BF16, kind="ExternalInput").ap()
    dect = nc.dram_tensor("h_dect", [128, NT], F32, kind="ExternalInput").ap()
    f1c = nc.dram_tensor("h_f1c", [32, 97], BF16, kind="ExternalInput").ap()
    cosf = nc.dram_tensor("h_cosf", [128, 33 * 128], BF16, kind="ExternalInput").ap()
    sinf = nc.dram_tensor("h_sinf", [128, 33 * 128], BF16, kind="ExternalInput").ap()
    nsinf = nc.dram_tensor("h_nsinf", [128, 33 * 128], BF16, kind="ExternalInput").ap()
    cosi = nc.dram_tensor("h_cosi", [128, 33 * 128], BF16, kind="ExternalInput").ap()
    sini = nc.dram_tensor("h_sini", [128, 33 * 128], BF16, kind="ExternalInput").ap()
    nsini = nc.dram_tensor("h_nsini", [128, 33 * 128], BF16, kind="ExternalInput").ap()
    abc = nc.dram_tensor("h_abc", [66, 32], BF16, kind="ExternalInput").ap()
    hbuf = nc.dram_tensor("h_hbuf", [NT * 128, D], BF16, kind="Internal").ap()
    ubuf = nc.dram_tensor("h_ubuf", [NT * 128, D], BF16, kind="Internal").ap()
    vbuf = nc.dram_tensor("h_vbuf", [NT * 128, D], BF16, kind="Internal").ap()
    dbuf = nc.dram_tensor("h_dbuf", [66, 128, D], BF16, kind="Internal").ap()
    ycbuf = nc.dram_tensor("h_ycbuf", [NT * 128, D], BF16, kind="Internal").ap()
    abuf = nc.dram_tensor("h_abuf", [NT * 128, D], BF16, kind="Internal").ap()

    x_t = x_in.rearrange("(t p) d -> t p d", p=128)
    o_t = out.rearrange("(t p) d -> t p d", p=128)
    h_t = hbuf.rearrange("(t p) d -> t p d", p=128)
    u_t = ubuf.rearrange("(t p) d -> t p d", p=128)
    v_t = vbuf.rearrange("(t p) d -> t p d", p=128)
    yc_t = ycbuf.rearrange("(t p) d -> t p d", p=128)
    a_t = abuf.rearrange("(t p) d -> t p d", p=128)
    h_f = hbuf.rearrange("(t1 t2) d -> t1 t2 d", t2=128)
    v_f = vbuf.rearrange("(t1 t2) d -> t1 t2 d", t2=128)
    yc_f = ycbuf.rearrange("(t1 t2) d -> t1 t2 d", t2=128)
    db_w = dbuf.rearrange("K t d -> t K d")
    wu_r = wu.rearrange("(dc p) j -> p dc j", p=128)
    wv_r = wv.rearrange("(dc p) j -> p dc j", p=128)
    wo_r = wo.rearrange("(ic p) j -> p ic j", p=128)

    with (outer if outer is not None else contextlib.ExitStack()) as stack:
        sb = lambda nm, sh, dt: stack.enter_context(nc.sbuf_tensor("h" + nm, sh, dt))
        ps = lambda nm, sh, dt: stack.enter_context(nc.psum_tensor("h" + nm, sh, dt))

        POOL = sb("POOL", [128, 24576], BF16)
        POOL2 = sb("POOL2", [128, 8448], F32)
        P2B = POOL2[:].bitcast(BF16)
        XB = sb("XB", [128, 2 * D], F32)
        XH = sb("XH", [128, 2 * D], BF16)
        XFT = sb("XFT", [128, 2 * D], BF16)
        UT = sb("UT", [128, 2 * D], BF16)
        VT = sb("VT", [128, 2 * D], BF16)
        AT1 = sb("AT1", [128, 64 * 97], BF16)
        HS = sb("HS", [128, 4224], BF16)
        DSB = sb("DSB", [128, 4224], BF16)
        ABL = sb("ABL", [128, 2 * D], BF16)
        F1CB = sb("F1CB", [32, 97], BF16)
        COSF = sb("COSF", [128, 4224], BF16)
        SINF = sb("SINF", [128, 4224], BF16)
        NSINF = sb("NSINF", [128, 4224], BF16)
        COSI = sb("COSI", [128, 4224], BF16)
        SINI = sb("SINI", [128, 4224], BF16)
        NSINI = sb("NSINI", [128, 4224], BF16)
        ABC = sb("ABC", [66, 32], BF16)
        DECT = sb("DECT", [128, NT], F32)
        W1C = sb("W1C", [64, 64], BF16)
        W2C = sb("W2C", [64, 64], BF16)
        B1C = sb("B1C", [64, 1], F32)
        B2C = sb("B2C", [64, 1], F32)
        B3R = sb("B3R", [1, D], BF16)
        BUR = sb("BUR", [1, D], BF16)
        BVR = sb("BVR", [1, D], BF16)
        BOR = sb("BOR", [1, D], BF16)
        onesb = sb("onesb", [1, 128], BF16)
        eps128 = sb("eps128", [128, 1], F32)
        invd128 = sb("invd128", [128, 1], F32)
        neg1128 = sb("neg1128", [128, 1], F32)
        idt = sb("idt", [128, 128], BF16)
        sqscr = sb("sqscr", [128, D], F32)
        ptr = [ps("ptr0", [128, 128], BF16)]
        pA = ps("pA", [128, 512], F32)
        pB = ps("pB", [128, 512], F32)
        pf = [ps("pf0", [128, 388], F32), ps("pf1", [128, 388], F32)]
        ps2 = [ps("ps2a", [128, 64], F32), ps("ps2b", [128, 64], F32)]
        pC = ps("pC", [32, 512], F32)

        ln1 = LNPhase(stack, nc, "hln1")
        ln2 = LNPhase(stack, nc, "hln2")

        ln1.emit_zero(P, c_dve)
        ln2.emit_zero(P, c_dve)

        ENG_SEMS = {"tensor": (c_pe, 1), "vector": (c_dve, 1), "scalar": (c_act, 1),
                    "sync": (d_sy, 16), "gpsimd": (d_gp, 16)}

        def barrier():
            bar = [(c_pe, P.val(c_pe)), (c_act, P.val(c_act)), (c_dve, P.val(c_dve)),
                   (d_sy, P.val(d_sy)), (d_gp, P.val(d_gp))]
            P.set_barrier(bar)

        pe1 = POOL[0:64, 0:4096]
        h1v = POOL[0:64, 4096:8192]
        h2v = POOL[0:64, 8192:12288]
        w3c = POOL[0:64, 12288:13312]
        hbt = POOL[:, 13312:15360]
        wub = POOL[:, 0:8192]
        wvb = POOL[:, 8192:16384]
        btv = POOL[0:32, 0:8192]
        dsw = POOL[0:66, 8192:16384]
        ycs = POOL[0:32, 16384:24576]
        wob = POOL[:, 0:8192]
        ulv = POOL[:, 8192:10240]
        yclv = POOL[:, 10240:12288]
        y2t = POOL[:, 12288:14336]
        y2f = POOL[:, 14336:16384]
        csv = P2B[:, 0:4224]
        ysv = P2B[:, 4224:8448]
        obv = POOL2[:, 4224:6272]
        ynv = POOL2[:, 6272:8320]
        otv = POOL2[:, 0:2048]

        small_loads = [(F1CB, f1c), (ABC, abc), (DECT, dect), (idt, ident),
                       (B1C, b1c), (B2C, b2c)]
        d_small = None
        for dst, s_ap in small_loads:
            d_small = P.emit("sync", lambda e, dst=dst, s_ap=s_ap: e.dma_start(
                out=dst[:], in_=s_ap), inc=d_sy, inc_amt=16)
        for dst, s_ap in [(COSF, cosf), (SINF, sinf), (NSINF, nsinf),
                          (COSI, cosi), (SINI, sini), (NSINI, nsini)]:
            d_small = P.emit("sync", lambda e, dst=dst, s_ap=s_ap: e.dma_start(
                out=dst[:], in_=s_ap), inc=d_sy, inc_amt=16)
        d_rows = None
        for dst, s_ap in [(B3R, b3r), (BUR, bur), (BVR, bvr), (BOR, bor),
                          (W1C, w1f), (W2C, w2f)]:
            d_rows = P.emit("gpsimd", lambda e, dst=dst, s_ap=s_ap: e.dma_start(
                out=dst[:], in_=s_ap), inc=d_gp, inc_amt=16)
        P.emit("gpsimd", lambda e: e.memset(onesb[:], 1.0))
        P.emit("gpsimd", lambda e: e.memset(eps128[:], LN_EPS))
        P.emit("gpsimd", lambda e: e.memset(invd128[:], INV_D))
        n_eps = P.emit("gpsimd", lambda e: e.memset(neg1128[:], -1.0), inc=d_gp)
        d_pe1 = P.emit("gpsimd", lambda e: e.dma_start(out=pe1, in_=pet),
                       inc=d_gp, inc_amt=16)
        d_w3 = P.emit("gpsimd", lambda e: e.dma_start(out=w3c, in_=w3f),
                      inc=d_gp, inc_amt=16)

        ev_prev = None
        for src_v, dst_v, wgt, bias in ((pe1, h1v, W1C, B1C), (h1v, h2v, W2C, B2C)):
            for cc in range(8):
                bank = pA if cc % 2 == 0 else pB
                w = [(d_sy, d_small), (d_gp, d_pe1), (d_gp, d_rows)]
                if cc < 2:
                    if ev_prev is not None:
                        w.append(ev_prev)
                else:
                    w.append(ev_hist2[cc - 2])
                nmm = P.emit("tensor", lambda e, src_v=src_v, cc=cc, wgt=wgt, bank=bank: e.matmul(
                    bank[0:64, :], wgt[:], src_v[:, cc * 512:(cc + 1) * 512],
                    start=True, stop=True), waits=w, inc=c_pe)
                nev = P.emit("scalar", lambda e, dst_v=dst_v, cc=cc, bias=bias, bank=bank: e.activation(
                    dst_v[:, cc * 512:(cc + 1) * 512], bank[0:64, :],
                    AF.Gelu_apprx_tanh, bias=bias[:]), waits=[(c_pe, nmm)], inc=c_act)
                if cc == 0:
                    ev_hist2 = {}
                ev_hist2[cc] = (c_act, nev)
            ev_prev = (c_act, nev)
        h2_done = (c_act, nev)
        hb_war = {}
        for t in range(NT):
            p = t % 2
            evs = []
            for half in range(2):
                bank = pA if half == 0 else pB
                w = [h2_done, (d_gp, d_w3)]
                if (t, half) != (0, 0) and (t > 1 or half > 0):
                    pass
                if t >= 1:
                    w.append(pbank_free[half])
                P.emit("tensor", lambda e, t=t, half=half, bank=bank: e.matmul(
                    bank[:], h2v[:, t * 128:(t + 1) * 128],
                    w3c[:, half * 512:(half + 1) * 512], start=True, stop=False),
                    waits=w)
                nmm = P.emit("tensor", lambda e, half=half, bank=bank: e.matmul(
                    bank[:], onesb[:], B3R[:, half * 512:(half + 1) * 512],
                    start=False, stop=True), waits=[(d_gp, d_rows)], inc=c_pe)
                w = [(c_pe, nmm)]
                if p in hb_war:
                    w.append(hb_war[p])
                nev = P.emit("vector", lambda e, t=t, p=p, half=half, bank=bank: e.tensor_scalar(
                    hbt[:, p * D + half * 512: p * D + half * 512 + 512], bank[:],
                    DECT[:, t:t + 1], None, ALU.mult), waits=w, inc=c_dve)
                if t == 0:
                    pbank_free = [None, None]
                pbank_free[half] = (c_dve, nev)
                evs.append(nev)
            nst = P.emit("sync", lambda e, t=t, p=p: e.dma_start(
                out=h_t[t], in_=hbt[:, p * D:(p + 1) * D]),
                waits=[(c_dve, evs[-1])], inc=d_sy, inc_amt=16)
            hb_war[p] = (d_sy, nst)
        barrier()

        x_cons = {}
        for t in range(NT):
            slot = t % 2
            w = [(c_dve, x_cons[slot][0]) if slot in x_cons else None,
                 (c_act, x_cons[slot][1]) if slot in x_cons else None]
            nl = P.emit("sync", lambda e, t=t, slot=slot: e.dma_start(
                out=XB[:, slot * D:(slot + 1) * D], in_=x_t[t]), waits=w,
                inc=d_sy, inc_amt=16)
            src = XB[:, slot * D:(slot + 1) * D]
            nd = P.emit("vector", lambda e, src=src, t=t: e.reduce_sum(
                ln1.sall[:, t:t + 1], src, axis=AX),
                waits=[(d_sy, nl), (c_dve, ln1.q_zeroed)], inc=c_dve)
            ln1.stat_dves.append(nd)
            na = P.emit("scalar", lambda e, src=src, t=t: e.activation(
                sqscr[:], src, AF.Square, accum_out=ln1.qall[:, t:t + 1]),
                waits=[(d_sy, nl), (c_dve, ln1.q_zeroed)], inc=c_act)
            ln1.stat_acts.append(na)
            x_cons[slot] = (nd, na)
            _heat(P, c_pe, pA, idt, (d_sy, nl))
        ln1.emit_math(P, c_act, c_dve, eps128, invd128, neg1128, eps_wait=(d_gp, n_eps))

        d_wu = P.emit("gpsimd", lambda e: e.dma_start(
            out=wub.rearrange("p (dc j) -> p dc j", j=D), in_=wu_r), inc=d_gp, inc_amt=16)
        d_wv = P.emit("gpsimd", lambda e: e.dma_start(
            out=wvb.rearrange("p (dc j) -> p dc j", j=D), in_=wv_r), inc=d_gp, inc_amt=16)

        xh_free = {}
        tr_free = [None, None]
        uv_war = {}
        pbank_free = [None, None]
        for t in range(NT):
            slot = t % 2
            p = t % 2
            w = [(c_dve, x_cons[slot][0]), (c_act, x_cons[slot][1])]
            nl = P.emit("sync", lambda e, t=t, slot=slot: e.dma_start(
                out=XB[:, slot * D:(slot + 1) * D], in_=x_t[t]), waits=w,
                inc=d_sy, inc_amt=16)
            src = XB[:, slot * D:(slot + 1) * D]
            w = [(d_sy, nl), (c_dve, ln1.ready)]
            if p in xh_free:
                w.append((c_pe, xh_free[p]))
            sc, bi = ln1.apply(t)
            nap = P.emit("scalar", lambda e, src=src, p=p, sc=sc, bi=bi: e.activation(
                XH[:, p * D:(p + 1) * D], src, AF.Identity, bias=bi, scale=sc),
                waits=w, inc=c_act)
            x_cons[slot] = (x_cons[slot][0], nap)
            last_tp = None
            last_ev = None
            for c in range(8):
                w = [(c_act, nap)] if c == 0 else []
                if tr_free[0] is not None:
                    w.append((c_dve, tr_free[0]))
                last_tp = P.emit("tensor", lambda e, p=p, c=c: e.transpose(
                    ptr[0][:], XH[:, p * D + c * 128: p * D + (c + 1) * 128],
                    idt[:]), waits=w, inc=c_pe)
                last_ev = P.emit("vector", lambda e, p=p, c=c: e.tensor_copy(
                    XFT[:, p * D + c * 128: p * D + (c + 1) * 128], ptr[0][:]),
                    waits=[(c_pe, last_tp)], inc=c_dve)
                tr_free[0] = last_ev
            xh_free[p] = last_tp
            xft_ready = last_ev
            for which, wgt, brow, dst, dbf, dwait in (
                    (0, wub, BUR, UT, u_t, d_wu), (1, wvb, BVR, VT, v_t, d_wv)):
                evs = []
                for half in range(2):
                    bank = pA if half == 0 else pB
                    for dc in range(8):
                        w = []
                        if dc == 0:
                            w = [(c_dve, xft_ready), (d_gp, dwait)]
                            if pbank_free[half] is not None:
                                w.append(pbank_free[half])
                        P.emit("tensor", lambda e, p=p, wgt=wgt, half=half, dc=dc, bank=bank: e.matmul(
                            bank[:], XFT[:, p * D + dc * 128: p * D + (dc + 1) * 128],
                            wgt[:, dc * D + half * 512: dc * D + half * 512 + 512],
                            start=(dc == 0), stop=False), waits=w)
                    nmm = P.emit("tensor", lambda e, brow=brow, half=half, bank=bank: e.matmul(
                        bank[:], onesb[:], brow[:, half * 512: half * 512 + 512],
                        start=False, stop=True), waits=[(d_gp, d_rows)], inc=c_pe)
                    w = [(c_pe, nmm)]
                    if (p, which) in uv_war:
                        w.append(uv_war[(p, which)])
                    nev = P.emit("scalar", lambda e, p=p, dst=dst, half=half, bank=bank: e.activation(
                        dst[:, p * D + half * 512: p * D + half * 512 + 512], bank[:],
                        AF.Identity), waits=w, inc=c_act)
                    pbank_free[half] = (c_act, nev)
                    evs.append(nev)
                nst = P.emit("sync", lambda e, t=t, p=p, dst=dst, dbf=dbf: e.dma_start(
                    out=dbf[t], in_=dst[:, p * D:(p + 1) * D]),
                    waits=[(c_act, evs[-1])], inc=d_sy, inc_amt=16)
                uv_war[(p, which)] = (d_sy, nst)
        barrier()

        def emit_fft_fwd(P, srcf, hc, dst_tile, bt_war, at_war, dst_war, ps2_war):
            d0 = hc * 64
            nbl = P.emit("sync", lambda e: e.dma_start(
                out=btv.rearrange("q (a b) -> q a b", b=64),
                in_=srcf[:, :, d0:d0 + 64]), waits=[bt_war], inc=d_sy, inc_amt=16)
            ev1 = {}
            nev = None
            s1_last = None
            for d in range(64):
                bank = (d // 4) % 2
                w = []
                if d % 4 == 0:
                    w = [(d_sy, nbl), (d_sy, d_small)]
                    if ev1.get(bank) is not None:
                        w.append((c_dve, ev1[bank]))
                    elif at_war is not None and d < 8:
                        w.append(at_war)
                nmm = P.emit("tensor", lambda e, d=d, bank=bank: e.matmul(
                    pf[bank][:, (d % 4) * 97:(d % 4) * 97 + 97],
                    btv[:, d::64][0:32, :],
                    F1CB[:], start=True, stop=True),
                    waits=w, inc=(c_pe if d % 4 == 3 else None))
                if d % 4 == 3:
                    s1_last = nmm
                    nev = P.emit("vector", lambda e, d=d, bank=bank: e.tensor_copy(
                        AT1[:, (d - 3) * 97:(d + 1) * 97], pf[bank][:]),
                        waits=[(c_pe, nmm)], inc=c_dve)
                    ev1[bank] = nev
            at_done = nev
            ev2 = {}
            nevc = None
            for k1 in range(33):
                for ri in range(2):
                    bank = (k1 * 2 + ri) % 2
                    if ri == 0:
                        l1, l2 = COSF, SINF
                        r1o, r2o = 0, 64
                    else:
                        l1, l2 = COSF, NSINF
                        r1o, r2o = 64, 0
                    w = [(c_dve, at_done)]
                    if ev2.get(bank) is not None:
                        w.append((c_act, ev2[bank]))
                    elif ps2_war is not None:
                        w.append(ps2_war)
                    P.emit("tensor", lambda e, k1=k1, l1=l1, r1o=r1o, bank=bank: e.matmul(
                        ps2[bank][:, 0:64], l1[:, k1 * 128:(k1 + 1) * 128],
                        AT1[:, r1o + k1::97][:, 0:64], start=True, stop=False),
                        waits=w)
                    nmm = P.emit("tensor", lambda e, k1=k1, l2=l2, r2o=r2o, bank=bank: e.matmul(
                        ps2[bank][:, 0:64], l2[:, k1 * 128:(k1 + 1) * 128],
                        AT1[:, r2o + k1::97][:, 0:64], start=False, stop=True),
                        inc=c_pe)
                    w = [(c_pe, nmm), dst_war]
                    nevc = P.emit("scalar", lambda e, k1=k1, ri=ri, bank=bank, dst_tile=dst_tile: e.copy(
                        dst_tile[:, k1 * 128 + ri * 64: k1 * 128 + ri * 64 + 64],
                        ps2[bank][:, 0:64]), waits=w, inc=c_act)
                    ev2[bank] = nevc
            return (c_pe, s1_last), (c_dve, at_done), (c_act, nevc)

        bt_war = None
        hs_done = None
        dsw_war = None
        ycs_war = None
        yc_stores = []
        at_war = None
        cs_war = None
        ys_war = None
        ds_war = None
        for hc in range(16):
            d0 = hc * 64
            bt_war, at_war, hs_done = emit_fft_fwd(P, h_f, hc, HS, bt_war, at_war,
                                                   cs_war, ds_war)
            bt_war, at_war, cs_done = emit_fft_fwd(P, v_f, hc, csv, bt_war, at_war,
                                                   cs_war, hs_done)
            def plane(tile, ri):
                return tile[:].rearrange("p (k r d) -> p k r d", r=2, d=64)[:, :, ri, :]
            w0 = [cs_done, hs_done]
            if ys_war is not None:
                w0.append(ys_war)
            if ds_war is not None:
                w0.append(ds_war)
            tmp1 = DSB[:, 0:2112]
            tmp2 = DSB[:, 2112:4224]
            n_ = P.emit("gpsimd", lambda e, tmp1=tmp1: e.tensor_tensor(
                tmp1, plane(csv, 0), plane(HS, 0), ALU.mult), waits=w0,
                inc=d_gp, inc_amt=1)
            n_ = P.emit("gpsimd", lambda e, tmp2=tmp2: e.tensor_tensor(
                tmp2, plane(csv, 1), plane(HS, 1), ALU.mult),
                waits=[(d_gp, n_)], inc=d_gp, inc_amt=1)
            n_ = P.emit("gpsimd", lambda e, tmp1=tmp1, tmp2=tmp2: e.tensor_tensor(
                plane(ysv, 0), tmp1, tmp2, ALU.subtract), waits=[(d_gp, n_)],
                inc=d_gp, inc_amt=1)
            n_ = P.emit("gpsimd", lambda e, tmp1=tmp1: e.tensor_tensor(
                tmp1, plane(csv, 0), plane(HS, 1), ALU.mult), waits=[(d_gp, n_)],
                inc=d_gp, inc_amt=1)
            n_ = P.emit("gpsimd", lambda e, tmp2=tmp2: e.tensor_tensor(
                tmp2, plane(csv, 1), plane(HS, 0), ALU.mult), waits=[(d_gp, n_)],
                inc=d_gp, inc_amt=1)
            n6_ = P.emit("gpsimd", lambda e, tmp1=tmp1, tmp2=tmp2: e.tensor_tensor(
                plane(ysv, 1), tmp1, tmp2, ALU.add), waits=[(d_gp, n_)],
                inc=d_gp, inc_amt=1)
            ys_done = (d_gp, n6_)
            ys_war = ys_done
            cs_war = ys_done
            ev2 = {}
            nevd = None
            for k1 in range(33):
                for ri in range(2):
                    bank = (k1 * 2 + ri) % 2
                    if ri == 0:
                        l1, l2 = COSI, NSINI
                        s1, s2 = 0, 1
                    else:
                        l1, l2 = COSI, SINI
                        s1, s2 = 1, 0
                    w = [ys_done]
                    if ev2.get(bank) is not None:
                        w.append((c_dve, ev2[bank]))
                    P.emit("tensor", lambda e, k1=k1, l1=l1, s1=s1, bank=bank: e.matmul(
                        ps2[bank][:, 0:64], l1[:, k1 * 128:(k1 + 1) * 128],
                        ysv[:, k1 * 128 + s1 * 64: k1 * 128 + s1 * 64 + 64],
                        start=True, stop=False), waits=w)
                    nmm = P.emit("tensor", lambda e, k1=k1, l2=l2, s2=s2, bank=bank: e.matmul(
                        ps2[bank][:, 0:64], l2[:, k1 * 128:(k1 + 1) * 128],
                        ysv[:, k1 * 128 + s2 * 64: k1 * 128 + s2 * 64 + 64],
                        start=False, stop=True), inc=c_pe)
                    w = [(c_pe, nmm)]
                    if ds_war is not None and k1 == 0 and ri == 0:
                        w.append(ds_war)
                    nevd = P.emit("vector", lambda e, k1=k1, ri=ri, bank=bank: e.tensor_copy(
                        DSB[:, (ri * 33 + k1) * 64:(ri * 33 + k1) * 64 + 64],
                        ps2[bank][:, 0:64]), waits=w, inc=c_dve)
                    ev2[bank] = nevd
            ds_done = (c_dve, nevd)
            nds = P.emit("sync", lambda e, hc=hc: e.dma_start(
                out=db_w[:, :, hc * 64:(hc + 1) * 64],
                in_=DSB[:].rearrange("p (K d) -> p K d", d=64)),
                waits=[ds_done], inc=d_sy, inc_amt=16)
            ds_war = (d_sy, nds)
            w = [(d_sy, nds)]
            if dsw_war is not None:
                w.append(dsw_war)
            ndsw = P.emit("sync", lambda e, hc=hc: e.dma_start(
                out=dsw.rearrange("q (a b) -> q a b", b=64),
                in_=dbuf[:, :, hc * 64:(hc + 1) * 64]), waits=w, inc=d_sy, inc_amt=16)
            nyev = None
            evc = {}
            for ch in range(16):
                w = [(d_sy, ndsw), (d_sy, d_small)]
                if evc.get(0) is not None:
                    w.append((c_dve, evc[0]))
                if ycs_war is not None and ch == 0:
                    w.append(ycs_war)
                nmm = P.emit("tensor", lambda e, ch=ch: e.matmul(
                    pC[:], ABC[:], dsw[:, ch * 512:(ch + 1) * 512],
                    start=True, stop=True), waits=w, inc=c_pe)
                nyev = P.emit("vector", lambda e, ch=ch: e.tensor_copy(
                    ycs[:, ch * 512:(ch + 1) * 512], pC[:]),
                    waits=[(c_pe, nmm)], inc=c_dve)
                evc[0] = nyev
            nyst = P.emit("sync", lambda e, hc=hc: e.dma_start(
                out=yc_f[:, :, hc * 64:(hc + 1) * 64],
                in_=ycs.rearrange("q (a b) -> q a b", b=64)),
                waits=[(c_dve, nyev)], inc=d_sy, inc_amt=16)
            yc_stores.append(nyst)
            ycs_war = (d_sy, nyst)
            dsw_war = (c_pe, P.val(c_pe))
        barrier()

        d_wo = P.emit("gpsimd", lambda e: e.dma_start(
            out=wob.rearrange("p (ic j) -> p ic j", j=D), in_=wo_r), inc=d_gp, inc_amt=16)
        ul_war = {}
        y2_war = {}
        ob_war = {}
        ab_stores = []
        pbank_free = [None, None]
        tr_free = [None, None]
        for t in range(NT):
            p = t % 2
            w = [(d_sy, P.val(d_sy))] if t == 0 else []
            if p in ul_war:
                w.append(ul_war[p])
            nlu = P.emit("sync", lambda e, t=t, p=p: e.dma_start(
                out=ulv[:, p * D:(p + 1) * D], in_=u_t[t]), waits=w, inc=d_sy, inc_amt=16)
            nly = P.emit("sync", lambda e, t=t, p=p: e.dma_start(
                out=yclv[:, p * D:(p + 1) * D], in_=yc_t[t]), inc=d_sy, inc_amt=16)
            w = [(d_sy, nly)]
            if p in y2_war:
                w.append(y2_war[p])
            ny2 = P.emit("gpsimd", lambda e, p=p: e.tensor_tensor(
                y2t[:, p * D:(p + 1) * D], ulv[:, p * D:(p + 1) * D],
                yclv[:, p * D:(p + 1) * D], ALU.mult), waits=w,
                inc=d_gp, inc_amt=1)
            ul_war[p] = (d_gp, ny2)
            last_tp = None
            last_ev = None
            for c in range(8):
                w = [(d_gp, ny2)] if c == 0 else []
                if tr_free[0] is not None:
                    w.append((c_dve, tr_free[0]))
                if c == 0 and p in y2_war:
                    w.append(y2_war[p])
                last_tp = P.emit("tensor", lambda e, p=p, c=c: e.transpose(
                    ptr[0][:], y2t[:, p * D + c * 128: p * D + (c + 1) * 128],
                    idt[:]), waits=w, inc=c_pe)
                last_ev = P.emit("vector", lambda e, p=p, c=c: e.tensor_copy(
                    y2f[:, p * D + c * 128: p * D + (c + 1) * 128], ptr[0][:]),
                    waits=[(c_pe, last_tp)], inc=c_dve)
                tr_free[0] = last_ev
            y2_war[p] = (c_pe, last_tp)
            y2f_ready = last_ev
            ob_evs = []
            for half in range(2):
                bank = pA if half == 0 else pB
                for ic in range(8):
                    w = []
                    if ic == 0:
                        w = [(c_dve, y2f_ready), (d_gp, d_wo)]
                        if pbank_free[half] is not None:
                            w.append(pbank_free[half])
                    P.emit("tensor", lambda e, p=p, half=half, ic=ic, bank=bank: e.matmul(
                        bank[:], y2f[:, p * D + ic * 128: p * D + (ic + 1) * 128],
                        wob[:, ic * D + half * 512: ic * D + half * 512 + 512],
                        start=(ic == 0), stop=False), waits=w)
                nmm = P.emit("tensor", lambda e, half=half, bank=bank: e.matmul(
                    bank[:], onesb[:], BOR[:, half * 512: half * 512 + 512],
                    start=False, stop=True), waits=[(d_gp, d_rows)], inc=c_pe)
                w = [(c_pe, nmm)]
                if p in ob_war:
                    w += list(ob_war[p])
                nev = P.emit("scalar", lambda e, p=p, half=half, bank=bank: e.activation(
                    obv[:, p * D + half * 512: p * D + half * 512 + 512], bank[:],
                    AF.Identity), waits=w, inc=c_act)
                pbank_free[half] = (c_act, nev)
                ob_evs.append(nev)
            nd2 = P.emit("vector", lambda e, t=t, p=p: e.reduce_sum(
                ln2.sall[:, t:t + 1], obv[:, p * D:(p + 1) * D], axis=AX),
                waits=[(c_act, ob_evs[-1]), (c_dve, ln2.q_zeroed)], inc=c_dve)
            ln2.stat_dves.append(nd2)
            na2 = P.emit("scalar", lambda e, t=t, p=p: e.activation(
                sqscr[:], obv[:, p * D:(p + 1) * D], AF.Square,
                accum_out=ln2.qall[:, t:t + 1]),
                waits=[(c_act, ob_evs[-1]), (c_dve, ln2.q_zeroed)], inc=c_act)
            ln2.stat_acts.append(na2)
            nab = P.emit("scalar", lambda e, t=t, p=p: e.copy(
                ABL[:, p * D:(p + 1) * D], obv[:, p * D:(p + 1) * D]),
                waits=([ob_war[p][0]] if p in ob_war else []), inc=c_act)
            nst = P.emit("sync", lambda e, t=t, p=p: e.dma_start(
                out=a_t[t], in_=ABL[:, p * D:(p + 1) * D]),
                waits=[(c_act, nab)], inc=d_sy, inc_amt=16)
            ab_stores.append(nst)
            ob_war[p] = ((d_sy, nst), (c_dve, nd2))
        ln2.emit_math(P, c_act, c_dve, eps128, invd128, neg1128, eps_wait=(d_gp, n_eps))

        yn_war = {}
        ot_free = {}
        abl_cons = {}
        for t in range(NT):
            slot = t % 2
            p = t % 2
            w = [(c_dve, x_cons[slot][0]), (c_act, x_cons[slot][1])]
            if t < 2:
                w.append((c_dve, P.val(c_dve)))
            nlx = P.emit("sync", lambda e, t=t, slot=slot: e.dma_start(
                out=XB[:, slot * D:(slot + 1) * D], in_=x_t[t]), waits=w,
                inc=d_sy, inc_amt=16)
            w = [(d_sy, ab_stores[t])]
            if p in abl_cons:
                w.append(abl_cons[p])
            else:
                w.append((c_act, P.val(c_act)))
            nly = P.emit("sync", lambda e, t=t, p=p: e.dma_start(
                out=ABL[:, p * D:(p + 1) * D], in_=a_t[t]), waits=w,
                inc=d_sy, inc_amt=16)
            sc, bi = ln2.apply(t)
            w = [(d_sy, nly), (c_dve, ln2.ready)]
            if p in yn_war:
                w.append(yn_war[p])
            na = P.emit("scalar", lambda e, p=p, sc=sc, bi=bi: e.activation(
                ynv[:, p * D:(p + 1) * D], ABL[:, p * D:(p + 1) * D],
                AF.Identity, bias=bi, scale=sc), waits=w, inc=c_act)
            abl_cons[p] = (c_act, na)
            w = [(d_sy, nlx), (c_act, na)]
            if p in ot_free:
                w.append(ot_free[p])
            no = P.emit("vector", lambda e, t=t, p=p, slot=slot: e.tensor_tensor(
                otv[:, p * D:(p + 1) * D], XB[:, slot * D:(slot + 1) * D],
                ynv[:, p * D:(p + 1) * D], ALU.add), waits=w, inc=c_dve)
            yn_war[p] = (c_dve, no)
            x_cons[slot] = (no, x_cons[slot][1])
            ns = P.emit("sync", lambda e, t=t, p=p: e.dma_start(
                out=o_t[t], in_=otv[:, p * D:(p + 1) * D]), waits=[(c_dve, no)],
                inc=d_sy, inc_amt=16)
            ot_free[p] = (d_sy, ns)
            _heat(P, c_pe, pA, idt, (d_sy, nlx), (c_act, P.val(c_act)))

    return


def _l1attn_consts():
    N = 8192
    t1 = np.arange(32)
    k1r = np.arange(33)
    F1C = np.zeros((32, 97), dtype=np.float32)
    F1C[:, 0:33] = np.cos(2 * np.pi * np.outer(t1, k1r) / 64)
    F1C[:, 64:97] = -np.sin(2 * np.pi * np.outer(t1, k1r) / 64)
    t2 = np.arange(128)
    k2 = np.arange(128)
    cosf = np.zeros((128, 33 * 128), dtype=np.float32)
    sinf = np.zeros((128, 33 * 128), dtype=np.float32)
    cosi = np.zeros((128, 33 * 128), dtype=np.float32)
    sini = np.zeros((128, 33 * 128), dtype=np.float32)
    for k1 in range(33):
        th = 2 * np.pi * np.outer(t2, (k1 + 64 * k2)) / N
        cosf[:, k1 * 128:(k1 + 1) * 128] = np.cos(th)
        sinf[:, k1 * 128:(k1 + 1) * 128] = np.sin(th)
        thi = th.T
        cosi[:, k1 * 128:(k1 + 1) * 128] = np.cos(thi)
        sini[:, k1 * 128:(k1 + 1) * 128] = np.sin(thi)
    ABc = np.zeros((66, 32), dtype=np.float32)
    for k1 in range(33):
        c = 1.0 if k1 in (0, 32) else 2.0
        ABc[k1, :] = c * np.cos(2 * np.pi * t1 * k1 / 64) / N
        if k1 not in (0, 32):
            ABc[33 + k1, :] = -2.0 * np.sin(2 * np.pi * t1 * k1 / 64) / N
    L, FO = 4096, 64
    positions = np.arange(L, dtype=np.float32)[:, None]
    div_term = np.exp(np.arange(0, FO, 2, dtype=np.float32) * -(np.log(10000.0) / FO))
    pe = np.zeros((L, FO), np.float32)
    pe[:, 0::2] = np.sin(positions * div_term)
    pe[:, 1::2] = np.cos(positions * div_term)
    dec = np.exp(-0.01 * np.arange(L, dtype=np.float32)).reshape(32, 128).T
    bf = ml_dtypes.bfloat16
    return {
        "f1c": F1C.astype(bf), "cosf": cosf.astype(bf), "sinf": sinf.astype(bf),
        "nsinf": (-sinf).astype(bf), "cosi": cosi.astype(bf), "sini": sini.astype(bf),
        "nsini": (-sini).astype(bf), "abc": ABc.astype(bf),
        "pet": np.ascontiguousarray(pe.T).astype(bf),
        "dect": np.ascontiguousarray(dec),
    }


_L1A_CONSTS = None


def _l1attn_inmaps(inputs):
    global _L1A_CONSTS
    if _L1A_CONSTS is None:
        _L1A_CONSTS = _l1attn_consts()
    m = {
        "h_wu": inputs["L1_attn_Wu"],
        "h_wv": inputs["L1_attn_Wv"],
        "h_wo": inputs["L1_attn_Wo"],
        "h_bur": inputs["L1_attn_bu"].reshape(1, D),
        "h_bvr": inputs["L1_attn_bv"].reshape(1, D),
        "h_bor": inputs["L1_attn_bo"].reshape(1, D),
        "h_w1f": inputs["L1_attn_W1"],
        "h_w2f": inputs["L1_attn_W2"],
        "h_w3f": inputs["L1_attn_W3"],
        "h_b1c": inputs["L1_attn_b1"].reshape(64, 1),
        "h_b2c": inputs["L1_attn_b2"].reshape(64, 1),
        "h_b3r": inputs["L1_attn_b3"].reshape(1, D),
    }
    m.update({"h_" + k: v for k, v in _L1A_CONSTS.items()})
    return m

def _heat(P, c_pe, bank, idt, wait, wait2=None):
    P.emit("tensor", lambda e: e.matmul(bank[0:1, 0:1], idt[0:1, 0:1],
                                        idt[0:1, 0:1], start=True, stop=True),
           waits=[wait, wait2], inc=c_pe)


def _bar(P, sems):
    c_pe, c_act, c_dve, d_sy, d_gp = sems
    P.set_barrier([(c_pe, P.val(c_pe)), (c_act, P.val(c_act)), (c_dve, P.val(c_dve)),
                   (d_sy, P.val(d_sy)), (d_gp, P.val(d_gp))])


def _finalize(nc, P, sems):
    c_pe, c_act, c_dve, d_sy, d_gp = sems
    P.emit("sync", lambda e: e.nop(), waits=[(d_sy, P.val(d_sy))])
    P.emit("gpsimd", lambda e: e.nop(), waits=[(d_gp, P.val(d_gp))])
    with nc.Block() as block:
        P.run(block)


def build_fused(dbg=False):
    nc = bass.Bass("TRN2", target_bir_lowering=False, debug=False)
    kind = "ExternalOutput" if dbg else "Internal"
    x0 = nc.dram_tensor("x", [NT * 128, D], F32, kind="ExternalInput").ap()
    out = nc.dram_tensor("out", [NT * 128, D], F32, kind="ExternalOutput").ap()
    ident = nc.dram_tensor("ident", [128, 128], BF16, kind="ExternalInput").ap()
    x1b = nc.dram_tensor("x1b", [NT * 128, D], F32, kind=kind).ap()
    x2b = nc.dram_tensor("x2b", [NT * 128, D], F32, kind=kind).ap()
    x3b = nc.dram_tensor("x3b", [NT * 128, D], F32, kind=kind).ap()
    P = Prog()
    with contextlib.ExitStack() as stack:
        sems = tuple(stack.enter_context(nc.semaphore(nm))
                     for nm in ("c_pe", "c_act", "c_dve", "d_sy", "d_gp"))
        emit_l0attn(nc, P, sems, x0, x1b, ident)
        _bar(P, sems)
        emit_l0ffn(nc, P, sems, x1b, x2b, ident)
        _bar(P, sems)
        emit_l1attn(nc, P, sems, x2b, x3b, ident)
        _bar(P, sems)
        emit_l1ffn(nc, P, sems, x3b, out, ident)
        _finalize(nc, P, sems)
    return nc


def _fused_inmaps(x_core, inputs):
    m = {"x": np.ascontiguousarray(x_core),
         "ident": np.eye(128, dtype=ml_dtypes.bfloat16)}
    m.update(_l0attn_inmaps(inputs))
    m.update(_l0ffn_inmaps(inputs))
    m.update(_l1attn_inmaps(inputs))
    m.update(_l1ffn_inmaps(inputs))
    return m


def _run_block(emit_fn, inmaps_fn, x_full, inputs, n_cores=8):
    nc = bass.Bass("TRN2", target_bir_lowering=False, debug=False)
    x0 = nc.dram_tensor("x", [NT * 128, D], F32, kind="ExternalInput").ap()
    out = nc.dram_tensor("out", [NT * 128, D], F32, kind="ExternalOutput").ap()
    ident = nc.dram_tensor("ident", [128, 128], BF16, kind="ExternalInput").ap()
    P = Prog()
    with contextlib.ExitStack() as stack:
        sems = tuple(stack.enter_context(nc.semaphore(nm))
                     for nm in ("c_pe", "c_act", "c_dve", "d_sy", "d_gp"))
        inner = contextlib.ExitStack()
        stack.enter_context(inner)
        emit_fn(nc, P, sems, x0, out, ident, outer=inner)
        _finalize(nc, P, sems)
    in_maps = []
    for c in range(n_cores):
        m = {"x": np.ascontiguousarray(x_full[c]),
             "ident": np.eye(128, dtype=ml_dtypes.bfloat16)}
        m.update(inmaps_fn(inputs))
        in_maps.append(m)
    res = run_bass_kernel_spmd(nc, in_maps, core_ids=list(range(n_cores)))
    return np.stack([r["out"] for r in res.results], axis=0)


def run_l0attn(x_full, inputs, n_cores=8):
    return _run_block(emit_l0attn, _l0attn_inmaps, x_full, inputs, n_cores)


def run_l0ffn(x_full, inputs, n_cores=8):
    return _run_block(emit_l0ffn, _l0ffn_inmaps, x_full, inputs, n_cores)


def run_l1attn(x_full, inputs, n_cores=8):
    return _run_block(emit_l1attn, _l1attn_inmaps, x_full, inputs, n_cores)


def run_l1ffn(x_full, inputs, n_cores=8):
    return _run_block(emit_l1ffn, _l1ffn_inmaps, x_full, inputs, n_cores)


def kernel(**inputs):
    x = inputs["L0_attn_x"]
    nc = build_fused()
    in_maps = [_fused_inmaps(x[c], inputs) for c in range(8)]
    res = run_bass_kernel_spmd(nc, in_maps, core_ids=list(range(8)))
    return np.stack([r["out"] for r in res.results], axis=0)


# version 08
# speedup vs baseline: 1.0764x; 1.0764x over previous
import contextlib
import numpy as np
import ml_dtypes
import concourse.bass as bass
import concourse.mybir as mybir
from concourse.bass_utils import run_bass_kernel_spmd

F32 = mybir.dt.float32
BF16 = mybir.dt.bfloat16
FP8 = mybir.dt.float8e4
AF = mybir.ActivationFunctionType
ALU = mybir.AluOpType
AX = mybir.AxisListType.X

NT = 32
_DBG_INTERNAL = False
D = 1024
INV_D = 1.0 / D
LN_EPS = 1e-5


class Prog:
    ENGINES = ("tensor", "vector", "scalar", "gpsimd", "sync")

    def __init__(self):
        self.ops = {e: [] for e in self.ENGINES}
        self.cnt = {}

    def val(self, sem):
        return self.cnt.get(id(sem), 0)

    def emit(self, eng, build, waits=(), inc=None, inc_amt=1):
        n = None
        if inc is not None:
            k = id(inc)
            self.cnt[k] = self.cnt.get(k, 0) + inc_amt
            n = self.cnt[k]
        pend = getattr(self, "pending", None)
        extra = pend.pop(eng, []) if pend else []
        waits = tuple((s, v) for w in list(waits) + extra if w is not None
                      for (s, v) in [w] if v and v > 0)

        def f(e, build=build, waits=waits, inc=inc, amt=inc_amt):
            for s, v in waits:
                e.wait_ge(s, v)
            inst = build(e)
            if inc is not None:
                inst.then_inc(inc, amt)
        self.ops[eng].append(f)
        return n

    def set_barrier(self, bar):
        self.pending = {e: list(bar) for e in self.ENGINES}

    def run(self, block):
        for eng in self.ENGINES:
            dec = getattr(block, eng)

            def body(e, eng=eng):
                for f in self.ops[eng]:
                    f(e)
            dec(body)


class LNPhase:

    def __init__(self, stack, nc, name):
        t = lambda nm: stack.enter_context(nc.sbuf_tensor(name + nm, [128, NT], F32))
        self.sall = t("_s")
        self.qall = t("_q")
        self.mall = t("_m")
        self.rall = t("_r")
        self.ball = t("_b")
        self.scr1 = t("_t1")
        self.scr2 = t("_t2")
        self.stat_acts = []
        self.stat_dves = []
        self.ready = None
        self.q_zeroed = None

    def emit_zero(self, P, c_dve):
        self.q_zeroed = P.emit("vector", lambda e: e.memset(self.qall[:], 0.0), inc=c_dve)

    def emit_math(self, P, c_act, c_dve, eps128, invd128, neg1128, eps_wait=None):
        last_sq = self.stat_acts[-1]
        last_rd = self.stat_dves[-1]
        n_m = P.emit("vector", lambda e: e.tensor_scalar(
            self.mall[:], self.sall[:], invd128[:], None, ALU.mult),
            waits=[eps_wait, (c_dve, last_rd)], inc=c_dve)
        n_s1 = P.emit("vector", lambda e: e.tensor_tensor(
            self.scr1[:], self.mall[:], self.mall[:], ALU.mult),
            waits=[(c_dve, n_m)], inc=c_dve)
        n_s2 = P.emit("vector", lambda e: e.tensor_scalar(
            self.scr2[:], self.qall[:], invd128[:], None, ALU.mult),
            waits=[(c_act, last_sq)], inc=c_dve)
        nv = P.emit("vector", lambda e: e.tensor_tensor(
            self.scr2[:], self.scr2[:], self.scr1[:], ALU.subtract),
            waits=[(c_dve, n_s1), (c_dve, n_s2)], inc=c_dve)
        n_ln = P.emit("scalar", lambda e: e.activation(
            self.scr1[:], self.scr2[:], AF.Ln, bias=eps128[:]),
            waits=[(c_dve, nv), eps_wait], inc=c_act)
        nr = P.emit("scalar", lambda e: e.activation(
            self.rall[:], self.scr1[:], AF.Exp, scale=-0.5),
            waits=[(c_act, n_ln)], inc=c_act)
        self.ready = P.emit("vector", lambda e: e.scalar_tensor_tensor(
            self.ball[:], self.mall[:], neg1128[:], self.rall[:], ALU.mult, ALU.mult),
            waits=[(c_act, nr), (c_dve, n_m)], inc=c_dve)

    def apply(self, t):
        return self.rall[:, t:t + 1], self.ball[:, t:t + 1]


def emit_l1ffn(nc, P, sems, x_in, out, ident, outer=None):
    c_pe, c_act, c_dve, d_sy, d_gp = sems
    kern = nc.dram_tensor("k_kern", [D, D], BF16, kind="ExternalInput").ap()
    s_in = nc.dram_tensor("k_s_in", [1, 1], F32, kind="ExternalInput").ap()
    s_k = nc.dram_tensor("k_s_k", [1, 1], F32, kind="ExternalInput").ap()
    bias = nc.dram_tensor("k_bias", [1, D], BF16, kind="ExternalInput").ap()
    _k = "ExternalOutput" if _DBG_INTERNAL else "Internal"
    ybuf = nc.dram_tensor("k_ybuf", [NT * 128, D], BF16, kind=_k).ap()
    if _DBG_INTERNAL:
        kdbg = nc.dram_tensor("k_dbg", [5, 128], F32, kind="ExternalOutput").ap()

    x_t = x_in.rearrange("(t p) d -> t p d", p=128)
    y_t = ybuf.rearrange("(t p) d -> t p d", p=128)
    o_t = out.rearrange("(t p) d -> t p d", p=128)

    with (outer if outer is not None else contextlib.ExitStack()) as stack:
        sb = lambda nm, sh, dt: stack.enter_context(nc.sbuf_tensor("k" + nm, sh, dt))
        ps = lambda nm, sh, dt: stack.enter_context(nc.psum_tensor("k" + nm, sh, dt))

        XB = sb("XB", [128, 4 * D], F32)
        KB = sb("KB", [128, 8 * D], BF16)
        K8 = sb("K8", [128, 8 * D], FP8)
        kdiv = sb("kdiv", [128, 2 * D], F32)
        onesf = sb("onesf", [1, 128], F32)
        onesb = sb("onesb", [1, 128], BF16)
        sA = sb("sA", [1, 1], F32)
        sB = sb("sB", [1, 1], F32)
        btrow = sb("btrow", [1, D], BF16)
        s128 = sb("s128", [128, 1], F32)
        k128 = sb("k128", [128, 1], F32)
        ss128 = sb("ss128", [128, 1], F32)
        rs128 = sb("rs128", [128, 1], F32)
        rk128 = sb("rk128", [128, 1], F32)
        eps128 = sb("eps128", [128, 1], F32)
        invd128 = sb("invd128", [128, 1], F32)
        neg1128 = sb("neg1128", [128, 1], F32)
        biasbc = sb("biasbc", [128, D], BF16)
        idt = sb("idt", [128, 128], BF16)
        sqscr = sb("sqscr", [128, D], F32)
        XH = sb("XH", [128, 2 * D], BF16)
        XF = sb("XF", [128, 2 * D], BF16)
        xdiv = sb("xdiv", [128, 2 * D], F32)
        X8 = sb("X8", [128, 2 * D], FP8)
        YB = sb("YB", [128, 2 * D], BF16)
        YN = sb("YN", [128, 2 * D], F32)
        OT = sb("OT", [128, 2 * D], F32)
        ptr = [ps("ptr0", [128, 128], BF16), ps("ptr1", [128, 128], BF16)]
        py = [ps("py0", [128, 512], F32), ps("py1", [128, 512], F32)]

        ln1 = LNPhase(stack, nc, "kln1")
        ln2 = LNPhase(stack, nc, "kln2")

        ln1.emit_zero(P, c_dve)
        ln2.emit_zero(P, c_dve)
        for c in range(8):
            d_kb = P.emit("sync", lambda e, c=c: e.dma_start(
                out=KB[:, c * D:(c + 1) * D],
                in_=kern[c * 128:(c + 1) * 128, :]), inc=d_sy, inc_amt=16)
        d_sa = P.emit("sync", lambda e: e.dma_start(out=sA[:], in_=s_in), inc=d_sy, inc_amt=16)
        d_sb = P.emit("sync", lambda e: e.dma_start(out=sB[:], in_=s_k), inc=d_sy, inc_amt=16)
        d_bt = P.emit("sync", lambda e: e.dma_start(out=btrow[:], in_=bias), inc=d_sy, inc_amt=16)
        d_id = P.emit("sync", lambda e: e.dma_start(out=idt[:], in_=ident), inc=d_sy, inc_amt=16)
        P.emit("gpsimd", lambda e: e.memset(onesf[:], 1.0))
        P.emit("gpsimd", lambda e: e.memset(onesb[:], 1.0))
        P.emit("gpsimd", lambda e: e.memset(eps128[:], LN_EPS))
        P.emit("gpsimd", lambda e: e.memset(invd128[:], INV_D))
        n_eps = P.emit("gpsimd", lambda e: e.memset(neg1128[:], -1.0), inc=d_gp)

        n1 = P.emit("tensor", lambda e: e.matmul(py[0][:, 0:1], onesf[:], sA[:],
                                                 start=True, stop=True),
                    waits=[(d_sy, d_sa), (d_gp, n_eps)], inc=c_pe)
        ns128 = P.emit("vector", lambda e: e.tensor_copy(s128[:], py[0][:, 0:1]),
                       waits=[(c_pe, n1)], inc=c_dve)
        n2 = P.emit("tensor", lambda e: e.matmul(py[0][:, 4:5], onesf[:], sB[:],
                                                 start=True, stop=True),
                    waits=[(d_sy, d_sb), (c_dve, ns128)], inc=c_pe)
        n3 = P.emit("vector", lambda e: e.tensor_copy(k128[:], py[0][:, 4:5]),
                    waits=[(c_pe, n2)], inc=c_dve)
        n_ss = P.emit("vector", lambda e: e.tensor_tensor(ss128[:], s128[:], k128[:], ALU.mult),
                      waits=[(c_dve, n3)], inc=c_dve)
        P.emit("vector", lambda e: e.reciprocal(rs128[:], s128[:]), waits=[(c_dve, ns128)])
        n4 = P.emit("vector", lambda e: e.reciprocal(rk128[:], k128[:]),
                    waits=[(c_dve, n3)], inc=c_dve)
        nb0 = P.emit("tensor", lambda e: e.matmul(py[1][:], onesb[:], btrow[:, 0:512],
                                                  start=True, stop=True),
                     waits=[(d_sy, d_bt), (d_gp, n_eps)], inc=c_pe)
        nb0e = P.emit("scalar", lambda e: e.activation(biasbc[:, 0:512], py[1][:], AF.Identity),
                      waits=[(c_pe, nb0)], inc=c_act)
        nb1 = P.emit("tensor", lambda e: e.matmul(py[1][:], onesb[:], btrow[:, 512:1024],
                                                  start=True, stop=True),
                     waits=[(c_act, nb0e)], inc=c_pe)
        nb1e = P.emit("scalar", lambda e: e.activation(biasbc[:, 512:1024], py[1][:], AF.Identity),
                      waits=[(c_pe, nb1)], inc=c_act)

        k8_last = None
        k8_divs = {}
        for c in range(8):
            p = c % 2
            w = [(c_dve, n4), (d_sy, d_kb)]
            if c >= 2:
                w.append((c_act, k8_divs[p]))
            ndv = P.emit("vector", lambda e, c=c, p=p: e.tensor_scalar(
                kdiv[:, p * D:(p + 1) * D], KB[:, c * D:(c + 1) * D],
                rk128[:], None, ALU.mult), waits=w, inc=c_dve)
            k8_last = P.emit("scalar", lambda e, c=c, p=p: e.copy(
                K8[:, c * D:(c + 1) * D], kdiv[:, p * D:(p + 1) * D]),
                waits=[(c_dve, ndv)], inc=c_act)
            k8_divs[p] = k8_last

        if _DBG_INTERNAL:
            P.emit("gpsimd", lambda e: e.dma_start(
                out=kdbg[0:1, :].rearrange('a b -> b a'), in_=s128[:]), waits=[(c_dve, n4)], inc=d_gp, inc_amt=16)
            P.emit("gpsimd", lambda e: e.dma_start(
                out=kdbg[1:2, :].rearrange('a b -> b a'), in_=k128[:]), inc=d_gp, inc_amt=16)
            P.emit("gpsimd", lambda e: e.dma_start(
                out=kdbg[2:3, :].rearrange('a b -> b a'), in_=rs128[:]), inc=d_gp, inc_amt=16)
            P.emit("gpsimd", lambda e: e.dma_start(
                out=kdbg[3:4, :].rearrange('a b -> b a'), in_=rk128[:]), inc=d_gp, inc_amt=16)
            P.emit("gpsimd", lambda e: e.dma_start(
                out=kdbg[4:5, :].rearrange('a b -> b a'), in_=ss128[:]), waits=[(c_dve, P.val(c_dve))],
                inc=d_gp, inc_amt=16)

        _heat(P, c_pe, py[0], idt, (c_dve, n3))
        x_cons = {}
        for t in range(NT):
            slot = t % 4
            w = [ (c_dve, x_cons[slot][0]) if slot in x_cons else None,
                  (c_act, x_cons[slot][1]) if slot in x_cons else None ]
            nl = P.emit("sync", lambda e, t=t, slot=slot: e.dma_start(
                out=XB[:, slot * D:(slot + 1) * D], in_=x_t[t]), waits=w,
                inc=d_sy, inc_amt=16)
            src = XB[:, slot * D:(slot + 1) * D]
            nd = P.emit("vector", lambda e, src=src, t=t: e.reduce_sum(
                ln1.sall[:, t:t + 1], src, axis=AX), waits=[(d_sy, nl)], inc=c_dve)
            ln1.stat_dves.append(nd)
            na = P.emit("scalar", lambda e, src=src, t=t: e.activation(
                sqscr[:], src, AF.Square, accum_out=ln1.qall[:, t:t + 1]),
                waits=[(d_sy, nl), (c_dve, ln1.q_zeroed)], inc=c_act)
            ln1.stat_acts.append(na)
            x_cons[slot] = (nd, na)
            _heat(P, c_pe, py[0], idt, (d_sy, nl))

        ln1.emit_math(P, c_act, c_dve, eps128, invd128, neg1128, eps_wait=(d_gp, n_eps))

        xh_free = {}
        xdiv_free = {}
        x8_free = {}
        yb_cons = {}
        py_free = [(c_dve, n3), (c_act, nb1e)]
        tr_free = [None, None]
        for t in range(NT):
            slot = t % 4
            p = t % 2
            w = [(c_dve, x_cons[slot][0]), (c_act, x_cons[slot][1])]
            nl = P.emit("sync", lambda e, t=t, slot=slot: e.dma_start(
                out=XB[:, slot * D:(slot + 1) * D], in_=x_t[t]), waits=w,
                inc=d_sy, inc_amt=16)
            src = XB[:, slot * D:(slot + 1) * D]
            w = [(d_sy, nl), (c_dve, ln1.ready)]
            if p in xh_free:
                w.append((c_pe, xh_free[p]))
            sc, bi = ln1.apply(t)
            nap = P.emit("scalar", lambda e, src=src, p=p, sc=sc, bi=bi: e.activation(
                XH[:, p * D:(p + 1) * D], src, AF.Identity, bias=bi, scale=sc),
                waits=w, inc=c_act)
            x_cons[slot] = (x_cons[slot][0], nap)
            last_tp = None
            for c in range(8):
                tp = c % 2
                w = [(c_act, nap)] if c < 2 else []
                if tr_free[tp] is not None:
                    w.append((c_dve, tr_free[tp]))
                if t == 0 and c == 0:
                    w.append((d_sy, d_id))
                last_tp = P.emit("tensor", lambda e, p=p, c=c, tp=tp: e.transpose(
                    ptr[tp][:], XH[:, p * D + c * 128: p * D + (c + 1) * 128],
                    idt[:]), waits=w, inc=c_pe)
                tr_free[tp] = P.emit("vector", lambda e, p=p, c=c, tp=tp: e.tensor_copy(
                    XF[:, p * D + c * 128: p * D + (c + 1) * 128], ptr[tp][:]),
                    waits=[(c_pe, last_tp)], inc=c_dve)
            xh_free[p] = last_tp
            w = [(c_dve, n4), (c_dve, tr_free[1])]
            if p in xdiv_free:
                w.append((c_act, xdiv_free[p]))
            nd = P.emit("vector", lambda e, p=p: e.tensor_scalar(
                xdiv[:, p * D:(p + 1) * D], XF[:, p * D:(p + 1) * D],
                rs128[:], None, ALU.mult), waits=w, inc=c_dve)
            w = [(c_dve, nd)]
            if p in x8_free:
                w.append((c_pe, x8_free[p]))
            n8 = P.emit("scalar", lambda e, p=p: e.copy(
                X8[:, p * D:(p + 1) * D], xdiv[:, p * D:(p + 1) * D]),
                waits=w, inc=c_act)
            xdiv_free[p] = n8
            mm_groups = []
            for half in range(2):
                nmm = None
                for c in range(8):
                    w = []
                    if c == 0:
                        w = [(c_act, n8), (c_act, k8_last), py_free[half]]
                    nmm = P.emit("tensor", lambda e, p=p, half=half, c=c: e.matmul(
                        py[half][:], X8[:, p * D + c * 128: p * D + (c + 1) * 128],
                        K8[:, c * D + half * 512: c * D + half * 512 + 512],
                        start=(c == 0), stop=(c == 7)), waits=w,
                        inc=(c_pe if c == 7 else None))
                mm_groups.append(nmm)
            x8_free[p] = mm_groups[-1]
            yevs = []
            for half in range(2):
                w = [(c_pe, mm_groups[half]), (c_dve, n_ss)]
                if p in yb_cons:
                    w += [yb_cons[p][0], yb_cons[p][1], yb_cons[p][2]]
                ne = P.emit("scalar", lambda e, p=p, half=half: e.activation(
                    YB[:, p * D + half * 512: p * D + half * 512 + 512],
                    py[half][:], AF.Identity, scale=ss128[:]),
                    waits=w, inc=c_act)
                py_free[half] = (c_act, ne)
                yevs.append(ne)
            nb = P.emit("vector", lambda e, p=p: e.tensor_tensor(
                YB[:, p * D:(p + 1) * D], YB[:, p * D:(p + 1) * D],
                biasbc[:], ALU.add), waits=[(c_act, yevs[-1]), (c_act, nb1e)], inc=c_dve)
            ns = P.emit("sync", lambda e, t=t, p=p: e.dma_start(
                out=y_t[t], in_=YB[:, p * D:(p + 1) * D]), waits=[(c_dve, nb)],
                inc=d_sy, inc_amt=16)
            nd2 = P.emit("vector", lambda e, t=t, p=p: e.reduce_sum(
                ln2.sall[:, t:t + 1], YB[:, p * D:(p + 1) * D], axis=AX),
                waits=[(c_dve, nb)], inc=c_dve)
            ln2.stat_dves.append(nd2)
            na2 = P.emit("scalar", lambda e, t=t, p=p: e.activation(
                sqscr[:], YB[:, p * D:(p + 1) * D], AF.Square,
                accum_out=ln2.qall[:, t:t + 1]),
                waits=[(c_dve, nb), (c_dve, ln2.q_zeroed)], inc=c_act)
            ln2.stat_acts.append(na2)
            yb_cons[p] = ((c_dve, nd2), (c_act, na2), (d_sy, ns))

        ln2.emit_math(P, c_act, c_dve, eps128, invd128, neg1128, eps_wait=(d_gp, n_eps))

        ot_free = {}
        yn_war = {}
        for t in range(NT):
            slot = t % 4
            p = t % 2
            w = [(c_dve, x_cons[slot][0]), (c_act, x_cons[slot][1])]
            nlx = P.emit("sync", lambda e, t=t, slot=slot: e.dma_start(
                out=XB[:, slot * D:(slot + 1) * D], in_=x_t[t]), waits=w,
                inc=d_sy, inc_amt=16)
            w = [yb_cons[p][0], yb_cons[p][1], yb_cons[p][2]]
            nly = P.emit("sync", lambda e, t=t, p=p: e.dma_start(
                out=YB[:, p * D:(p + 1) * D], in_=y_t[t]), waits=w,
                inc=d_sy, inc_amt=16)
            sc, bi = ln2.apply(t)
            w = [(d_sy, nly), (c_dve, ln2.ready)]
            if p in yn_war:
                w.append(yn_war[p])
            na = P.emit("scalar", lambda e, p=p, sc=sc, bi=bi: e.activation(
                YN[:, p * D:(p + 1) * D], YB[:, p * D:(p + 1) * D],
                AF.Identity, bias=bi, scale=sc), waits=w, inc=c_act)
            yb_cons[p] = ((c_act, na), None, None)
            w = [(d_sy, nlx), (c_act, na)]
            if p in ot_free:
                w.append(ot_free[p][1])
            no = P.emit("vector", lambda e, t=t, p=p, slot=slot: e.tensor_tensor(
                OT[:, p * D:(p + 1) * D], XB[:, slot * D:(slot + 1) * D],
                YN[:, p * D:(p + 1) * D], ALU.add), waits=w, inc=c_dve)
            yn_war[p] = (c_dve, no)
            x_cons[slot] = (no, x_cons[slot][1])
            ns = P.emit("sync", lambda e, t=t, p=p: e.dma_start(
                out=o_t[t], in_=OT[:, p * D:(p + 1) * D]), waits=[(c_dve, no)],
                inc=d_sy, inc_amt=16)
            ot_free[p] = (no, (d_sy, ns))
            _heat(P, c_pe, py[0], idt, (d_sy, nlx), (c_act, P.val(c_act)))

    return


def _l1ffn_inmaps(inputs):
    return {
        "k_kern": np.ascontiguousarray(inputs["L1_ffn_kernel"]),
        "k_s_in": inputs["L1_ffn_input_scale"].reshape(1, 1),
        "k_s_k": inputs["L1_ffn_kernel_scale"].reshape(1, 1),
        "k_bias": inputs["L1_ffn_bias"].reshape(1, D),
    }






def emit_l0ffn(nc, P, sems, x_in, out, ident, outer=None):
    c_pe, c_act, c_dve, d_sy, d_gp = sems
    H = 1408
    NHC = 11
    w1 = nc.dram_tensor("f_w1", [8, D, H], F32, kind="ExternalInput").ap()
    w3 = nc.dram_tensor("f_w3", [8, D, H], F32, kind="ExternalInput").ap()
    w2 = nc.dram_tensor("f_w2", [8, H, D], F32, kind="ExternalInput").ap()
    sw1 = nc.dram_tensor("f_sw1", [D, H], F32, kind="ExternalInput").ap()
    sw3 = nc.dram_tensor("f_sw3", [D, H], F32, kind="ExternalInput").ap()
    sw2 = nc.dram_tensor("f_sw2", [H, D], F32, kind="ExternalInput").ap()
    b1r = nc.dram_tensor("f_b1r", [128, 8 * NHC], F32, kind="ExternalInput").ap()
    b3r = nc.dram_tensor("f_b3r", [128, 8 * NHC], F32, kind="ExternalInput").ap()
    b2m = nc.dram_tensor("f_b2m", [8, D], F32, kind="ExternalInput").ap()
    sb1r = nc.dram_tensor("f_sb1r", [128, NHC], F32, kind="ExternalInput").ap()
    sb3r = nc.dram_tensor("f_sb3r", [128, NHC], F32, kind="ExternalInput").ap()
    sb2r = nc.dram_tensor("f_sb2r", [1, D], F32, kind="ExternalInput").ap()
    gwt = nc.dram_tensor("f_gwt", [D, 8], F32, kind="ExternalInput").ap()
    gbr = nc.dram_tensor("f_gbr", [1, 8], F32, kind="ExternalInput").ap()
    _k = "ExternalOutput" if _DBG_INTERNAL else "Internal"
    ybuf = nc.dram_tensor("f_ybuf", [NT * 128, D], F32, kind=_k).ap()
    xhfb = nc.dram_tensor("f_xhfb", [8, 128, NT * 128], BF16, kind=_k).ap()

    x_t = x_in.rearrange("(t p) d -> t p d", p=128)
    o_t = out.rearrange("(t p) d -> t p d", p=128)
    y_t = ybuf.rearrange("(t p) d -> t p d", p=128)
    y_r = ybuf.rearrange("(T p) d -> p T d", p=128)
    xh_w = xhfb.rearrange("c p n -> p c n")
    w1_r = w1.rearrange("e (dc p) h -> e p dc h", p=128)
    w3_r = w3.rearrange("e (dc p) h -> e p dc h", p=128)
    w2_r = w2.rearrange("e (hc p) d -> e p hc d", p=128)
    sw1_r = sw1.rearrange("(dc p) h -> p dc h", p=128)
    sw3_r = sw3.rearrange("(dc p) h -> p dc h", p=128)
    sw2_r = sw2.rearrange("(hc p) d -> p hc d", p=128)
    gwt_r = gwt.rearrange("(dc p) e -> p dc e", p=128)

    with contextlib.ExitStack() as stack:
        sb = lambda nm, sh, dt: stack.enter_context(nc.sbuf_tensor("f" + nm, sh, dt))
        ps = lambda nm, sh, dt: stack.enter_context(nc.psum_tensor("f" + nm, sh, dt))

        XB = sb("XB", [128, 2 * D], F32)
        XH = sb("XH", [128, 2 * D], BF16)
        XFT = sb("XFT", [128, 2 * D], BF16)
        XSTR = sb("XSTR", [128, 2 * 8 * 512], BF16)
        W1B = sb("W1B", [128, 8 * H], BF16)
        W3B = sb("W3B", [128, 8 * H], BF16)
        W2B = sb("W2B", [128, NHC * D], BF16)
        HT = sb("HT", [128, NHC * 512], BF16)
        GS1 = sb("GS1", [128, 512], F32)
        GS3 = sb("GS3", [128, 512], F32)
        PSC = sb("PSC", [128, 512], F32)
        OESB = sb("OESB", [128, 4 * D], F32)
        B1S = sb("B1S", [128, 8 * NHC], F32)
        B3S = sb("B3S", [128, 8 * NHC], F32)
        SB1S = sb("SB1S", [128, NHC], F32)
        SB3S = sb("SB3S", [128, NHC], F32)
        B2T = sb("B2T", [8, D], BF16)
        SB2R = sb("SB2R", [1, D], BF16)
        GWTB = sb("GWTB", [128, 8 * 8], BF16)
        GBR1 = sb("GBR1", [1, 8], F32)
        GBB = sb("GBB", [128, 8], F32)
        GT = sb("GT", [128, 2 * 8], F32)
        GE = sb("GE", [128, 8], F32)
        GP = sb("GP", [128, 8], F32)
        GBT = sb("GBT", [128, 8], F32)
        GSC = sb("GSC", [128, 8], F32)
        GM = sb("GM", [128, 1], F32)
        RW = sb("RW", [128, NT * 8], F32)
        RWB = sb("RWB", [128, 8], BF16)
        RWF = sb("RWF", [8, NT * 128], BF16)
        YBF = sb("YBF", [128, 2 * D], F32)
        TMP = sb("TMP", [128, 2 * D], F32)
        OT = sb("OT", [128, 2 * D], F32)
        onesf = sb("onesf", [1, 128], F32)
        onesb = sb("onesb", [1, 128], BF16)
        eps128 = sb("eps128", [128, 1], F32)
        invd128 = sb("invd128", [128, 1], F32)
        neg1128 = sb("neg1128", [128, 1], F32)
        negbig = sb("negbig", [128, 1], F32)
        idt = sb("idt", [128, 128], BF16)
        sqscr = sb("sqscr", [128, D], F32)
        pa1 = ps("pa1", [128, 512], F32)
        pa3 = ps("pa3", [128, 512], F32)
        po = [ps("po0", [128, 512], F32), ps("po1", [128, 512], F32)]
        ptr = [ps("ptr0", [128, 128], BF16), ps("ptr1", [128, 128], BF16)]
        pg = ps("pg", [128, 8], F32)

        ln1 = LNPhase(stack, nc, "fln1")
        ln2 = LNPhase(stack, nc, "fln2")
        ln1.emit_zero(P, c_dve)
        ln2.emit_zero(P, c_dve)

        d_small = None
        for (dst, src_ap) in [(B1S, b1r), (B3S, b3r), (SB1S, sb1r), (SB3S, sb3r),
                              (GBR1, gbr), (idt, ident)]:
            d_small = P.emit("sync", lambda e, dst=dst, src_ap=src_ap: e.dma_start(
                out=dst[:], in_=src_ap), inc=d_sy, inc_amt=16)
        d_b2 = P.emit("gpsimd", lambda e: e.dma_start(out=B2T[:], in_=b2m),
                      inc=d_gp, inc_amt=16)
        d_sb2 = P.emit("gpsimd", lambda e: e.dma_start(out=SB2R[:], in_=sb2r),
                       inc=d_gp, inc_amt=16)
        d_gw = P.emit("gpsimd", lambda e: e.dma_start(
            out=GWTB[:].rearrange("p (dc e) -> p dc e", e=8), in_=gwt_r),
            inc=d_gp, inc_amt=16)
        P.emit("gpsimd", lambda e: e.memset(onesf[:], 1.0))
        P.emit("gpsimd", lambda e: e.memset(onesb[:], 1.0))
        P.emit("gpsimd", lambda e: e.memset(eps128[:], LN_EPS))
        P.emit("gpsimd", lambda e: e.memset(invd128[:], INV_D))
        P.emit("gpsimd", lambda e: e.memset(negbig[:], -1e30))
        n_eps = P.emit("gpsimd", lambda e: e.memset(neg1128[:], -1.0), inc=d_gp)
        ngb = P.emit("tensor", lambda e: e.matmul(pg[:], onesf[:], GBR1[:],
                                                  start=True, stop=True),
                     waits=[(d_sy, d_small), (d_gp, n_eps)], inc=c_pe)
        ngbe = P.emit("vector", lambda e: e.tensor_copy(GBB[:], pg[:]),
                      waits=[(c_pe, ngb)], inc=c_dve)

        x_cons = {}
        for t in range(NT):
            slot = t % 2
            w = [(c_dve, x_cons[slot][0]) if slot in x_cons else None,
                 (c_act, x_cons[slot][1]) if slot in x_cons else None]
            nl = P.emit("sync", lambda e, t=t, slot=slot: e.dma_start(
                out=XB[:, slot * D:(slot + 1) * D], in_=x_t[t]), waits=w,
                inc=d_sy, inc_amt=16)
            src = XB[:, slot * D:(slot + 1) * D]
            nd = P.emit("vector", lambda e, src=src, t=t: e.reduce_sum(
                ln1.sall[:, t:t + 1], src, axis=AX),
                waits=[(d_sy, nl), (c_dve, ln1.q_zeroed)], inc=c_dve)
            ln1.stat_dves.append(nd)
            na = P.emit("scalar", lambda e, src=src, t=t: e.activation(
                sqscr[:], src, AF.Square, accum_out=ln1.qall[:, t:t + 1]),
                waits=[(d_sy, nl), (c_dve, ln1.q_zeroed)], inc=c_act)
            ln1.stat_acts.append(na)
            x_cons[slot] = (nd, na)
        ln1.emit_math(P, c_act, c_dve, eps128, invd128, neg1128, eps_wait=(d_gp, n_eps))

        xh_free = {}
        xft_store = {}
        tr_free = [None, None]
        xh_stores = []
        for t in range(NT):
            slot = t % 2
            p = t % 2
            w = [(c_dve, x_cons[slot][0]), (c_act, x_cons[slot][1])]
            nl = P.emit("sync", lambda e, t=t, slot=slot: e.dma_start(
                out=XB[:, slot * D:(slot + 1) * D], in_=x_t[t]), waits=w,
                inc=d_sy, inc_amt=16)
            src = XB[:, slot * D:(slot + 1) * D]
            w = [(d_sy, nl), (c_dve, ln1.ready)]
            if p in xh_free:
                w.append((c_pe, xh_free[p]))
            sc, bi = ln1.apply(t)
            nap = P.emit("scalar", lambda e, src=src, p=p, sc=sc, bi=bi: e.activation(
                XH[:, p * D:(p + 1) * D], src, AF.Identity, bias=bi, scale=sc),
                waits=w, inc=c_act)
            x_cons[slot] = (x_cons[slot][0], nap)
            last_tp = None
            last_ev = None
            for c in range(8):
                tp = c % 2
                w = [(c_act, nap)] if c < 2 else []
                if tr_free[tp] is not None:
                    w.append((c_dve, tr_free[tp]))
                if t == 0 and c == 0:
                    w.append((d_sy, d_small))
                if c < 2 and p in xft_store:
                    w.append(xft_store[p])
                last_tp = P.emit("tensor", lambda e, p=p, c=c, tp=tp: e.transpose(
                    ptr[tp][:], XH[:, p * D + c * 128: p * D + (c + 1) * 128],
                    idt[:]), waits=w, inc=c_pe)
                last_ev = P.emit("vector", lambda e, p=p, c=c, tp=tp: e.tensor_copy(
                    XFT[:, p * D + c * 128: p * D + (c + 1) * 128], ptr[tp][:]),
                    waits=[(c_pe, last_tp)], inc=c_dve)
                tr_free[tp] = last_ev
            xh_free[p] = last_tp
            nst = P.emit("sync", lambda e, t=t, p=p: e.dma_start(
                out=xh_w[:, :, t * 128:(t + 1) * 128],
                in_=XFT[:, p * D:(p + 1) * D].rearrange("p (c i) -> p c i", i=128)),
                waits=[(c_dve, last_ev)], inc=d_sy, inc_amt=16)
            xft_store[p] = (d_sy, nst)
            xh_stores.append(nst)
        xh_all = xh_stores[-1]

        gt_war = {}
        rwf_done = []
        xstr_war = {}
        xstr_ld = {}
        dve_chain = None
        rw_tp = None
        for tc in range(8):
            sp = tc % 2
            w = [(d_sy, xh_all)]
            if sp in xstr_war:
                w.append(xstr_war[sp])
            nxl = P.emit("sync", lambda e, tc=tc, sp=sp: e.dma_start(
                out=XSTR[:, sp * 4096:(sp + 1) * 4096].rearrange("p (c n) -> p c n", n=512),
                in_=xh_w[:, :, tc * 512:(tc + 1) * 512]), waits=w,
                inc=d_sy, inc_amt=16)
            xstr_ld[tc] = nxl
            for tsub in range(4):
                t = 4 * tc + tsub
                gp_ = t % 2
                w = [(d_sy, nxl), (d_gp, d_gw)]
                if gp_ in gt_war:
                    w.append(gt_war[gp_])
                nmm = None
                for dc in range(8):
                    nmm = P.emit("tensor", lambda e, sp=sp, tsub=tsub, dc=dc: e.matmul(
                        pg[:],
                        XSTR[:, sp * 4096 + dc * 512 + tsub * 128: sp * 4096 + dc * 512 + (tsub + 1) * 128],
                        GWTB[:, dc * 8:(dc + 1) * 8],
                        start=(dc == 0), stop=(dc == 7)),
                        waits=(w if dc == 0 else []), inc=(c_pe if dc == 7 else None))
                ng = P.emit("scalar", lambda e, gp_=gp_: e.activation(
                    GT[:, gp_ * 8:(gp_ + 1) * 8], pg[:], AF.Identity),
                    waits=[(c_pe, nmm)], inc=c_act)
                gt_war[gp_] = (c_act, ng)
                gtv = GT[:, gp_ * 8:(gp_ + 1) * 8]
                def dve(bld, extra=()):
                    nonlocal dve_chain
                    wts = [(c_dve, dve_chain)] if dve_chain else []
                    dve_chain = P.emit("vector", bld, waits=list(extra) + wts, inc=c_dve)
                    return dve_chain
                dve(lambda e, gtv=gtv: e.reduce_max(GM[:], gtv, axis=AX),
                    extra=[(c_act, ng)])
                dve(lambda e, gtv=gtv: e.tensor_scalar(GSC[:], gtv, GM[:], None, ALU.subtract))
                ne = P.emit("scalar", lambda e: e.activation(GE[:], GSC[:], AF.Exp),
                            waits=[(c_dve, dve_chain)], inc=c_act)
                dve(lambda e: e.reduce_sum(GM[:], GE[:], axis=AX), extra=[(c_act, ne)])
                dve(lambda e: e.reciprocal(GM[:], GM[:]))
                dve(lambda e: e.tensor_scalar(GP[:], GE[:], GM[:], None, ALU.mult))
                dve(lambda e: e.tensor_tensor(GBT[:], GP[:], GBB[:], ALU.add),
                    extra=[(c_dve, ngbe)])
                dve(lambda e: e.reduce_max(GM[:], GBT[:], axis=AX))
                dve(lambda e: e.tensor_scalar(GSC[:], GBT[:], GM[:], None, ALU.is_ge))
                dve(lambda e: e.scalar_tensor_tensor(GSC[:], GSC[:], negbig[:], GBT[:],
                                                     ALU.mult, ALU.add))
                dve(lambda e: e.reduce_max(GM[:], GSC[:], axis=AX))
                dve(lambda e: e.tensor_scalar(GSC[:], GBT[:], GM[:], None, ALU.is_ge))
                nrw = dve(lambda e, t=t: e.tensor_tensor(
                    RW[:, t * 8:(t + 1) * 8], GP[:], GSC[:], ALU.mult))
                w = [(c_dve, nrw)]
                if rw_tp is not None:
                    w.append((c_pe, rw_tp))
                nrb = P.emit("scalar", lambda e, t=t: e.copy(RWB[:], RW[:, t * 8:(t + 1) * 8]),
                             waits=w, inc=c_act)
                w = [(c_act, nrb)]
                if tr_free[0] is not None:
                    w.append((c_dve, tr_free[0]))
                rw_tp = P.emit("tensor", lambda e: e.transpose(
                    ptr[0][0:8, :], RWB[:], idt[:]), waits=w, inc=c_pe)
                nrf = P.emit("vector", lambda e, t=t: e.tensor_copy(
                    RWF[:, t * 128:(t + 1) * 128], ptr[0][0:8, :]),
                    waits=[(c_pe, rw_tp)], inc=c_dve)
                tr_free[0] = nrf
                rwf_done.append(nrf)
            xstr_war[sp] = (c_pe, P.val(c_pe))

        init_done = []
        yin_war = {}
        pai = [pa1, pa3]
        pai_free = [None, None]
        for t in range(NT):
            p = t % 2
            evs = []
            for half in range(2):
                bank = pai[half]
                w = [(c_dve, rwf_done[t]), (d_gp, d_sb2), (d_gp, d_b2)]
                if pai_free[half] is not None:
                    w.append(pai_free[half])
                P.emit("tensor", lambda e, t=t, half=half, bank=bank: e.matmul(
                    bank[:], RWF[:, t * 128:(t + 1) * 128],
                    B2T[:, half * 512:(half + 1) * 512], start=True, stop=False),
                    waits=w)
                nm2 = P.emit("tensor", lambda e, half=half, bank=bank: e.matmul(
                    bank[:], onesb[:], SB2R[:, half * 512:(half + 1) * 512],
                    start=False, stop=True), inc=c_pe)
                w = [(c_pe, nm2)]
                if p in yin_war:
                    w.append(yin_war[p])
                nev = P.emit("scalar", lambda e, p=p, half=half, bank=bank: e.activation(
                    TMP[:, p * D + half * 512: p * D + half * 512 + 512],
                    bank[:], AF.Identity), waits=w, inc=c_act)
                pai_free[half] = (c_act, nev)
                evs.append(nev)
            nst = P.emit("sync", lambda e, t=t, p=p: e.dma_start(
                out=y_t[t], in_=TMP[:, p * D:(p + 1) * D]),
                waits=[(c_act, evs[-1])], inc=d_sy, inc_amt=16)
            yin_war[p] = (d_sy, nst)
            init_done.append(nst)
        init_all = init_done[-1]

        region_last = [None] * 8
        oesb_war = None
        last_a_mm = None
        last_o_mm = None
        g1_war = pai_free[0]
        g3_war = pai_free[1]
        pp_war = None
        ht_war = [None] * 8
        po_free = [None, None]
        for e_i in range(9):
            shared = (e_i == 8)
            w = []
            if last_a_mm is not None:
                w.append((c_pe, last_a_mm))
            s1 = sw1_r if shared else w1_r[e_i]
            s3 = sw3_r if shared else w3_r[e_i]
            s2 = sw2_r if shared else w2_r[e_i]
            P.emit("gpsimd", lambda e, s1=s1: e.dma_start(
                out=W1B[:].rearrange("p (dc h) -> p dc h", h=H), in_=s1), waits=w,
                inc=d_gp, inc_amt=16)
            P.emit("gpsimd", lambda e, s3=s3: e.dma_start(
                out=W3B[:].rearrange("p (dc h) -> p dc h", h=H), in_=s3),
                inc=d_gp, inc_amt=16)
            w2w = []
            if last_o_mm is not None:
                w2w.append((c_pe, last_o_mm))
            wl = P.emit("gpsimd", lambda e, s2=s2: e.dma_start(
                out=W2B[:].rearrange("p (hc d) -> p hc d", d=D), in_=s2), waits=w2w,
                inc=d_gp, inc_amt=16)
            for tc in range(8):
                sp = tc % 2
                w = [(d_sy, xh_all)]
                if sp in xstr_war:
                    w.append(xstr_war[sp])
                nxl = P.emit("sync", lambda e, tc=tc, sp=sp: e.dma_start(
                    out=XSTR[:, sp * 4096:(sp + 1) * 4096].rearrange(
                        "p (c n) -> p c n", n=512),
                    in_=xh_w[:, :, tc * 512:(tc + 1) * 512]), waits=w,
                    inc=d_sy, inc_amt=16)
                nh = None
                nh_list = []
                for hc in range(NHC):
                    for dc in range(8):
                        w = []
                        if dc == 0:
                            w = [(d_gp, wl), (d_sy, nxl)]
                            if g1_war is not None:
                                w.append(g1_war)
                        nmm = P.emit("tensor", lambda e, sp=sp, hc=hc, dc=dc: e.matmul(
                            pa1[:],
                            W1B[:, dc * H + hc * 128: dc * H + (hc + 1) * 128],
                            XSTR[:, sp * 4096 + dc * 512: sp * 4096 + (dc + 1) * 512],
                            start=(dc == 0), stop=(dc == 7)),
                            waits=w, inc=(c_pe if dc == 7 else None))
                    a1mm = nmm
                    for dc in range(8):
                        w = []
                        if dc == 0 and g3_war is not None:
                            w = [g3_war]
                        nmm = P.emit("tensor", lambda e, sp=sp, hc=hc, dc=dc: e.matmul(
                            pa3[:],
                            W3B[:, dc * H + hc * 128: dc * H + (hc + 1) * 128],
                            XSTR[:, sp * 4096 + dc * 512: sp * 4096 + (dc + 1) * 512],
                            start=(dc == 0), stop=(dc == 7)),
                            waits=w, inc=(c_pe if dc == 7 else None))
                    a3mm = nmm
                    last_a_mm = a3mm
                    bslice = (SB1S[:, hc:hc + 1] if shared
                              else B1S[:, e_i * NHC + hc: e_i * NHC + hc + 1])
                    b3slice = (SB3S[:, hc:hc + 1] if shared
                               else B3S[:, e_i * NHC + hc: e_i * NHC + hc + 1])
                    w = [(c_pe, a1mm)]
                    if pp_war is not None:
                        w.append(pp_war)
                    ng1 = P.emit("scalar", lambda e, bslice=bslice: e.activation(
                        GS1[:], pa1[:], AF.Identity, bias=bslice), waits=w, inc=c_act)
                    g1_war = (c_act, ng1)
                    w = [(c_pe, a3mm)]
                    if pp_war is not None:
                        w.append(pp_war)
                    ng3 = P.emit("vector", lambda e, b3slice=b3slice: e.tensor_scalar(
                        GS3[:], pa3[:], b3slice, None, ALU.add), waits=w, inc=c_dve)
                    g3_war = (c_dve, ng3)
                    npp = P.emit("vector", lambda e: e.tensor_tensor(
                        PSC[:], GS1[:], GS3[:], ALU.mult),
                        waits=[(c_act, ng1), (c_dve, ng3)], inc=c_dve)
                    pp_war = (c_dve, npp)
                    w = [(c_dve, npp)]
                    if ht_war[tc] is not None:
                        w.append(ht_war[tc])
                    nh = P.emit("scalar", lambda e, hc=hc: e.activation(
                        HT[:, hc * 512:(hc + 1) * 512], PSC[:], AF.Silu),
                        waits=w, inc=c_act)
                    nh_list.append(nh)
                h_done = nh
                last_ev = None
                for tsub in range(4):
                    for dh in range(2):
                        pb = (tsub * 2 + dh) % 2
                        for hc in range(NHC):
                            w = [(c_act, nh_list[hc])]
                            if hc == 0 and po_free[pb] is not None:
                                w.append(po_free[pb])
                            nmm = P.emit("tensor", lambda e, tsub=tsub, dh=dh, hc=hc, pb=pb: e.matmul(
                                po[pb][:],
                                HT[:, hc * 512 + tsub * 128: hc * 512 + (tsub + 1) * 128],
                                W2B[:, hc * D + dh * 512: hc * D + dh * 512 + 512],
                                start=(hc == 0), stop=(hc == NHC - 1)),
                                waits=w, inc=(c_pe if hc == NHC - 1 else None))
                        last_o_mm = nmm
                        ht_war[tc] = (c_pe, nmm)
                        dst = OESB[:, tsub * D + dh * 512: tsub * D + dh * 512 + 512]
                        w = [(c_pe, nmm)]
                        if oesb_war is not None:
                            w.append(oesb_war)
                        if last_ev is not None:
                            w.append((c_dve, last_ev))
                        if shared:
                            last_ev = P.emit("vector", lambda e, dst=dst, pb=pb: e.tensor_copy(
                                dst, po[pb][:]), waits=w, inc=c_dve)
                        else:
                            rwsl = RW[:, (4 * tc + tsub) * 8 + e_i:
                                      (4 * tc + tsub) * 8 + e_i + 1]
                            last_ev = P.emit("vector",
                                lambda e, dst=dst, pb=pb, rwsl=rwsl: e.tensor_scalar(
                                    dst, po[pb][:], rwsl, None, ALU.mult),
                                waits=w, inc=c_dve)
                        po_free[pb] = (c_dve, last_ev)
                w = [(c_dve, last_ev), (d_sy, init_all)]
                if region_last[tc] is not None:
                    w.append((d_gp, region_last[tc]))
                nadd = P.emit("gpsimd", lambda e, tc=tc: e.dma_start(
                    out=y_r[:, 4 * tc:4 * tc + 4, :],
                    in_=OESB[:].rearrange("p (T d) -> p T d", d=D),
                    accum_op=ALU.add), waits=w, inc=d_gp, inc_amt=16)
                region_last[tc] = nadd
                oesb_war = (d_gp, nadd)
                xstr_war[sp] = (c_pe, last_a_mm)

        yb_cons = {}
        for t in range(NT):
            slot = t % 2
            w = [(d_gp, region_last[t // 4])]
            if slot in yb_cons:
                w += [(c_dve, yb_cons[slot][0]), (c_act, yb_cons[slot][1])]
            nl = P.emit("sync", lambda e, t=t, slot=slot: e.dma_start(
                out=YBF[:, slot * D:(slot + 1) * D], in_=y_t[t]), waits=w,
                inc=d_sy, inc_amt=16)
            src = YBF[:, slot * D:(slot + 1) * D]
            nd = P.emit("vector", lambda e, src=src, t=t: e.reduce_sum(
                ln2.sall[:, t:t + 1], src, axis=AX),
                waits=[(d_sy, nl), (c_dve, ln2.q_zeroed)], inc=c_dve)
            ln2.stat_dves.append(nd)
            na = P.emit("scalar", lambda e, src=src, t=t: e.activation(
                sqscr[:], src, AF.Square, accum_out=ln2.qall[:, t:t + 1]),
                waits=[(d_sy, nl), (c_dve, ln2.q_zeroed)], inc=c_act)
            ln2.stat_acts.append(na)
            yb_cons[slot] = (nd, na)
        ln2.emit_math(P, c_act, c_dve, eps128, invd128, neg1128, eps_wait=(d_gp, n_eps))

        yn_war = {}
        ot_free = {}
        for t in range(NT):
            slot = t % 2
            p = t % 2
            w = [(c_dve, x_cons[slot][0]), (c_act, x_cons[slot][1])]
            nlx = P.emit("sync", lambda e, t=t, slot=slot: e.dma_start(
                out=XB[:, slot * D:(slot + 1) * D], in_=x_t[t]), waits=w,
                inc=d_sy, inc_amt=16)
            w = [(c_dve, yb_cons[slot][0]), (c_act, yb_cons[slot][1])]
            nly = P.emit("sync", lambda e, t=t, slot=slot: e.dma_start(
                out=YBF[:, slot * D:(slot + 1) * D], in_=y_t[t]), waits=w,
                inc=d_sy, inc_amt=16)
            sc, bi = ln2.apply(t)
            w = [(d_sy, nly), (c_dve, ln2.ready)]
            if p in yn_war:
                w.append(yn_war[p])
            na = P.emit("scalar", lambda e, p=p, sc=sc, bi=bi, slot=slot: e.activation(
                TMP[:, p * D:(p + 1) * D], YBF[:, slot * D:(slot + 1) * D],
                AF.Identity, bias=bi, scale=sc), waits=w, inc=c_act)
            yb_cons[slot] = (yb_cons[slot][0], na)
            w = [(d_sy, nlx), (c_act, na)]
            if p in ot_free:
                w.append(ot_free[p])
            no = P.emit("vector", lambda e, t=t, p=p, slot=slot: e.tensor_tensor(
                OT[:, p * D:(p + 1) * D], XB[:, slot * D:(slot + 1) * D],
                TMP[:, p * D:(p + 1) * D], ALU.add), waits=w, inc=c_dve)
            yn_war[p] = (c_dve, no)
            x_cons[slot] = (no, x_cons[slot][1])
            ns = P.emit("sync", lambda e, t=t, p=p: e.dma_start(
                out=o_t[t], in_=OT[:, p * D:(p + 1) * D]), waits=[(c_dve, no)],
                inc=d_sy, inc_amt=16)
            ot_free[p] = (d_sy, ns)
            _heat(P, c_pe, pa1, idt, (d_sy, nlx), (c_act, P.val(c_act)))

    return


def _l0ffn_inmaps(inputs):
    b1 = inputs["L0_ffn_b1"]
    b3 = inputs["L0_ffn_b3"]
    return {
        "f_w1": inputs["L0_ffn_w1"],
        "f_w3": inputs["L0_ffn_w3"],
        "f_w2": inputs["L0_ffn_w2"],
        "f_sw1": inputs["L0_ffn_sw1"],
        "f_sw3": inputs["L0_ffn_sw3"],
        "f_sw2": inputs["L0_ffn_sw2"],
        "f_b1r": np.ascontiguousarray(b1.reshape(8, 11, 128).transpose(2, 0, 1).reshape(128, 88)),
        "f_b3r": np.ascontiguousarray(b3.reshape(8, 11, 128).transpose(2, 0, 1).reshape(128, 88)),
        "f_b2m": inputs["L0_ffn_b2"],
        "f_sb1r": np.ascontiguousarray(inputs["L0_ffn_sb1"].reshape(11, 128).T),
        "f_sb3r": np.ascontiguousarray(inputs["L0_ffn_sb3"].reshape(11, 128).T),
        "f_sb2r": inputs["L0_ffn_sb2"].reshape(1, D),
        "f_gwt": np.ascontiguousarray(inputs["L0_ffn_gate_w"].T),
        "f_gbr": inputs["L0_ffn_gate_b"].reshape(1, 8),
    }






def emit_l0attn(nc, P, sems, x_in, out, ident, outer=None):
    c_pe, c_act, c_dve, d_sy, d_gp = sems
    wqt = nc.dram_tensor("a_wqt", [D, D], F32, kind="ExternalInput").ap()
    wvt = nc.dram_tensor("a_wvt", [D, D], F32, kind="ExternalInput").ap()
    wat = nc.dram_tensor("a_wat", [64, 64], F32, kind="ExternalInput").ap()
    wot = nc.dram_tensor("a_wot", [D, D], F32, kind="ExternalInput").ap()
    borow = nc.dram_tensor("a_borow", [1, D], F32, kind="ExternalInput").ap()
    abuf = nc.dram_tensor("a_abuf", [NT * 128, D], BF16, kind="Internal").ap()
    atb = nc.dram_tensor("a_atb", [D, NT * 128], BF16, kind="Internal").ap()

    x_t = x_in.rearrange("(t p) d -> t p d", p=128)
    o_t = out.rearrange("(t p) d -> t p d", p=128)
    a_t = abuf.rearrange("(t p) d -> t p d", p=128)
    at_w = atb.rearrange("(c r) t -> r c t", r=128)
    at_r = atb.rearrange("(h a) t -> a h t", a=64)
    wqt_r = wqt.rearrange("(dc p) q -> p dc q", p=128)
    wvt_r = wvt.rearrange("(dc p) q -> p dc q", p=128)
    wot_r = wot.rearrange("(ic p) q -> p ic q", p=128)

    with (outer if outer is not None else contextlib.ExitStack()) as stack:
        sb = lambda nm, sh, dt: stack.enter_context(nc.sbuf_tensor("a" + nm, sh, dt))
        ps = lambda nm, sh, dt: stack.enter_context(nc.psum_tensor("a" + nm, sh, dt))

        XB = sb("XB", [128, 2 * D], F32)
        XH = sb("XH", [128, 2 * D], BF16)
        XFT = sb("XFT", [128, 2 * D], BF16)
        WQT = sb("WQT", [128, 8 * D], BF16)
        WVT = sb("WVT", [128, 8 * D], BF16)
        WOT = sb("WOT", [128, 8 * D], BF16)
        WAT = sb("WAT", [128, 64], BF16)
        BOR = sb("BOR", [1, D], BF16)
        QKF = sb("QKF", [64, 2 * 2048], BF16)
        VEX = sb("VEX", [128, 2 * D], BF16)
        SCT = sb("SCT", [128, D], F32)
        ETL = sb("ETL", [128, D], F32)
        ATN = sb("ATN", [128, 2 * D], BF16)
        ATT = sb("ATT", [128, 2 * D], BF16)
        LRC = sb("LRC", [128, 16], F32)
        LRC2 = sb("LRC2", [128, 16], F32)
        RS = sb("RS", [64, 16], F32)
        OU2S = sb("OU2S", [64, 16 * 64], BF16)
        ATL = sb("ATL", [64, 2 * 2 * D], BF16)
        FOF = sb("FOF", [128, 2 * D], BF16)
        OB = sb("OB", [128, 2 * D], F32)
        ABL = sb("ABL", [128, 2 * D], BF16)
        YN = sb("YN", [128, 2 * D], F32)
        OT = sb("OT", [128, 2 * D], F32)
        onesf = sb("onesf", [1, 128], F32)
        onesb = sb("onesb", [1, 128], BF16)
        onec = sb("onec", [128, 1], BF16)
        eps128 = sb("eps128", [128, 1], F32)
        invd128 = sb("invd128", [128, 1], F32)
        neg1128 = sb("neg1128", [128, 1], F32)
        idt = sb("idt", [128, 128], BF16)
        sqscr = sb("sqscr", [128, D], F32)
        ptr = [ps("ptr0", [128, 128], BF16)]
        pA = ps("pA", [128, 512], F32)
        pB = ps("pB", [128, 512], F32)
        psc = [ps("psc0", [128, 512], F32), ps("psc1", [128, 512], F32)]
        psums = ps("psums", [64, 16], F32)
        OUACC = sb("OUACC", [64, 16 * 64], F32)
        SACC = sb("SACC", [64, 16], F32)

        ln1 = LNPhase(stack, nc, "aln1")
        ln2 = LNPhase(stack, nc, "aln2")

        ln1.emit_zero(P, c_dve)
        ln2.emit_zero(P, c_dve)
        P.emit("vector", lambda e: e.memset(OUACC[:], 0.0))
        acc_zero = P.emit("vector", lambda e: e.memset(SACC[:], 0.0), inc=c_dve)

        d_id = P.emit("sync", lambda e: e.dma_start(out=idt[:], in_=ident),
                      inc=d_sy, inc_amt=16)
        d_wq = P.emit("gpsimd", lambda e: e.dma_start(
            out=WQT[:].rearrange("p (dc q) -> p dc q", q=D), in_=wqt_r),
            inc=d_gp, inc_amt=16)
        d_wv = P.emit("gpsimd", lambda e: e.dma_start(
            out=WVT[:].rearrange("p (dc q) -> p dc q", q=D), in_=wvt_r),
            inc=d_gp, inc_amt=16)
        d_wo = P.emit("gpsimd", lambda e: e.dma_start(
            out=WOT[:].rearrange("p (ic q) -> p ic q", q=D), in_=wot_r),
            inc=d_gp, inc_amt=16)
        P.emit("gpsimd", lambda e: e.dma_start(out=WAT[0:64, :], in_=wat),
               inc=d_gp, inc_amt=16)
        d_wa = P.emit("gpsimd", lambda e: e.dma_start(out=WAT[64:128, :], in_=wat),
                      inc=d_gp, inc_amt=16)
        d_bo = P.emit("gpsimd", lambda e: e.dma_start(out=BOR[:], in_=borow),
                      inc=d_gp, inc_amt=16)
        P.emit("gpsimd", lambda e: e.memset(onesf[:], 1.0))
        P.emit("gpsimd", lambda e: e.memset(onesb[:], 1.0))
        P.emit("gpsimd", lambda e: e.memset(onec[:], 1.0))
        P.emit("gpsimd", lambda e: e.memset(eps128[:], LN_EPS))
        P.emit("gpsimd", lambda e: e.memset(invd128[:], INV_D))
        n_eps = P.emit("gpsimd", lambda e: e.memset(neg1128[:], -1.0), inc=d_gp)

        x_cons = {}
        for t in range(NT):
            slot = t % 2
            w = [(c_dve, x_cons[slot][0]) if slot in x_cons else None,
                 (c_act, x_cons[slot][1]) if slot in x_cons else None]
            nl = P.emit("sync", lambda e, t=t, slot=slot: e.dma_start(
                out=XB[:, slot * D:(slot + 1) * D], in_=x_t[t]), waits=w,
                inc=d_sy, inc_amt=16)
            src = XB[:, slot * D:(slot + 1) * D]
            nd = P.emit("vector", lambda e, src=src, t=t: e.reduce_sum(
                ln1.sall[:, t:t + 1], src, axis=AX),
                waits=[(d_sy, nl), (c_dve, ln1.q_zeroed)], inc=c_dve)
            ln1.stat_dves.append(nd)
            na = P.emit("scalar", lambda e, src=src, t=t: e.activation(
                sqscr[:], src, AF.Square, accum_out=ln1.qall[:, t:t + 1]),
                waits=[(d_sy, nl), (c_dve, ln1.q_zeroed)], inc=c_act)
            ln1.stat_acts.append(na)
            x_cons[slot] = (nd, na)
            _heat(P, c_pe, pA, idt, (d_sy, nl))
        ln1.emit_math(P, c_act, c_dve, eps128, invd128, neg1128, eps_wait=(d_gp, n_eps))

        xh_free = {}
        tr_free = None
        xft_cons = {}
        qkf_cons = {}
        vex_cons = {}
        atn_cons = {}
        att_store = {}
        sct_war = None
        etl_war = None
        lrc_war = None
        pa_free = None
        pb_free = None
        psc_free = [None, None]
        at_stores = []
        ou_last = [None, None]
        ps_last = None
        for t in range(NT):
            slot = t % 2
            p = t % 2
            w = [(c_dve, x_cons[slot][0]), (c_act, x_cons[slot][1])]
            nl = P.emit("sync", lambda e, t=t, slot=slot: e.dma_start(
                out=XB[:, slot * D:(slot + 1) * D], in_=x_t[t]), waits=w,
                inc=d_sy, inc_amt=16)
            src = XB[:, slot * D:(slot + 1) * D]
            w = [(d_sy, nl), (c_dve, ln1.ready)]
            if p in xh_free:
                w.append((c_pe, xh_free[p]))
            sc, bi = ln1.apply(t)
            nap = P.emit("scalar", lambda e, src=src, p=p, sc=sc, bi=bi: e.activation(
                XH[:, p * D:(p + 1) * D], src, AF.Identity, bias=bi, scale=sc),
                waits=w, inc=c_act)
            x_cons[slot] = (x_cons[slot][0], nap)
            last_tp = None
            last_ev = None
            for c in range(8):
                w = [(c_act, nap)] if c == 0 else []
                if tr_free is not None:
                    w.append((c_dve, tr_free))
                if t == 0 and c == 0:
                    w.append((d_sy, d_id))
                if c == 0 and p in xft_cons:
                    w.append(xft_cons[p])
                last_tp = P.emit("tensor", lambda e, p=p, c=c: e.transpose(
                    ptr[0][:], XH[:, p * D + c * 128: p * D + (c + 1) * 128],
                    idt[:]), waits=w, inc=c_pe)
                last_ev = P.emit("vector", lambda e, p=p, c=c: e.tensor_copy(
                    XFT[:, p * D + c * 128: p * D + (c + 1) * 128], ptr[0][:]),
                    waits=[(c_pe, last_tp)], inc=c_dve)
                tr_free = last_ev
            xh_free[p] = last_tp
            xft_ready = last_ev
            qk_evh = {}
            qk_evs = None
            for h in range(16):
                bank = pA if h % 2 == 0 else pB
                col = ((h // 2) % 2) * 128
                for dc in range(8):
                    w = []
                    if dc == 0:
                        w = [(c_dve, xft_ready), (d_gp, d_wq)]
                        if h < 2:
                            if h % 2 == 0 and pa_free is not None:
                                w.append(pa_free)
                            if h % 2 == 1 and pb_free is not None:
                                w.append(pb_free)
                        if h >= 2:
                            w.append((c_dve, qk_evh[h - 2]))
                        if p in qkf_cons and h < 2:
                            w.append(qkf_cons[p])
                    nmm = P.emit("tensor", lambda e, p=p, h=h, dc=dc, bank=bank, col=col: e.matmul(
                        bank[0:64, col:col + 128],
                        WQT[:, dc * D + h * 64: dc * D + h * 64 + 64],
                        XFT[:, p * D + dc * 128: p * D + (dc + 1) * 128],
                        start=(dc == 0), stop=(dc == 7)),
                        waits=w, inc=(c_pe if dc == 7 else None))
                qk_evs = P.emit("vector", lambda e, p=p, h=h, bank=bank, col=col: e.tensor_copy(
                    QKF[:, p * 2048 + h * 128: p * 2048 + (h + 1) * 128],
                    bank[0:64, col:col + 128]),
                    waits=[(c_pe, nmm)], inc=c_dve)
                qk_evh[h] = qk_evs
            pa_free = (c_dve, qk_evh[14])
            pb_free0 = (c_dve, qk_evh[15])
            v_evs = None
            for half in range(2):
                for dc in range(8):
                    w = []
                    if dc == 0:
                        w = [(c_dve, xft_ready), (d_gp, d_wv), pb_free0]
                        if pb_free is not None:
                            w.append(pb_free)
                        if half == 1:
                            w.append((c_dve, v_evs))
                    nmm = P.emit("tensor", lambda e, p=p, half=half, dc=dc: e.matmul(
                        pB[:],
                        XFT[:, p * D + dc * 128: p * D + (dc + 1) * 128],
                        WVT[:, dc * D + half * 512: dc * D + half * 512 + 512],
                        start=(dc == 0), stop=(dc == 7)),
                        waits=w, inc=(c_pe if dc == 7 else None))
                w = [(c_pe, nmm)]
                if p in vex_cons:
                    w.append(vex_cons[p])
                v_evs = P.emit("vector", lambda e, p=p, half=half: e.tensor_copy(
                    VEX[:, p * D + half * 512: p * D + half * 512 + 512], pB[:]),
                    waits=w, inc=c_dve)
            pb_free = (c_dve, v_evs)
            vex_ready = v_evs
            exp_acts = []
            for bank in range(2):
                nmm = None
                for hh in range(8):
                    h = bank * 8 + hh
                    w = [(c_dve, qk_evs), (d_gp, d_wa)]
                    if hh == 0 and psc_free[bank] is not None:
                        w.append(psc_free[bank])
                    nmm = P.emit("tensor", lambda e, p=p, h=h, hh=hh, bank=bank: e.matmul(
                        psc[bank][:, hh * 64:(hh + 1) * 64],
                        QKF[0:64, p * 2048 + h * 128: p * 2048 + (h + 1) * 128],
                        WAT[0:64, :], start=True, stop=True),
                        waits=(w if hh == 0 else []), inc=(c_pe if hh == 7 else None))
                w = [(c_pe, nmm)]
                if sct_war is not None:
                    w.append(sct_war)
                nex = P.emit("scalar", lambda e, bank=bank: e.activation(
                    ETL[:, bank * 512:(bank + 1) * 512], psc[bank][:], AF.Exp,
                    scale=0.125), waits=w, inc=c_act)
                exp_acts.append(nex)
                psc_free[bank] = (c_act, nex)
            sct_war = (c_act, exp_acts[-1])
            w = [(c_act, exp_acts[-1])]
            if lrc_war is not None:
                w.append(lrc_war)
            for hq in range(16):
                nls = P.emit("vector", lambda e, hq=hq: e.reduce_sum(
                    LRC[:, hq:hq + 1], ETL[:, hq * 64:(hq + 1) * 64], axis=AX),
                    waits=(w if hq == 0 else []), inc=(c_dve if hq == 15 else None))
            nlr = P.emit("vector", lambda e: e.reciprocal(LRC2[:], LRC[:]),
                         waits=[(c_dve, nls)], inc=c_dve)
            at_ops_d = None
            at_ops_a = None
            for h in range(16):
                lsl = LRC2[:, h:h + 1]
                esl = ETL[:, h * 64:(h + 1) * 64]
                dsl = ATN[:, p * D + h * 64: p * D + (h + 1) * 64]
                w = [(c_dve, nlr)]
                if h < 2 and p in atn_cons:
                    w.append(atn_cons[p])
                if h % 2 == 0:
                    at_ops_d = P.emit("vector", lambda e, dsl=dsl, esl=esl, lsl=lsl: e.tensor_scalar(
                        dsl, esl, lsl, None, ALU.mult), waits=w, inc=c_dve)
                else:
                    at_ops_a = P.emit("scalar", lambda e, dsl=dsl, esl=esl, lsl=lsl: e.activation(
                        dsl, esl, AF.Identity, scale=lsl), waits=w, inc=c_act)
            lrc_war = (c_dve, at_ops_d)
            atn_ready = [(c_dve, at_ops_d), (c_act, at_ops_a)]
            add_waits = [(c_dve, acc_zero)]
            for bank in range(2):
                nmm = None
                for hh in range(8):
                    h = bank * 8 + hh
                    w = list(atn_ready) + [(c_dve, vex_ready)]
                    if t == 0 and h == 0:
                        w.append((d_gp, n_eps))
                    if hh == 0:
                        w.append(psc_free[bank])
                    nmm = P.emit("tensor", lambda e, p=p, h=h, hh=hh, bank=bank: e.matmul(
                        psc[bank][0:64, hh * 64:(hh + 1) * 64],
                        ATN[:, p * D + h * 64: p * D + (h + 1) * 64],
                        VEX[:, p * D + h * 64: p * D + (h + 1) * 64],
                        start=True, stop=True),
                        waits=(w if hh == 0 else list(atn_ready)[:0]),
                        inc=(c_pe if hh == 7 else None))
                nadd = P.emit("vector", lambda e, bank=bank: e.tensor_tensor(
                    OUACC[:, bank * 512:(bank + 1) * 512],
                    OUACC[:, bank * 512:(bank + 1) * 512],
                    psc[bank][0:64, :], ALU.add),
                    waits=[(c_pe, nmm)] + add_waits, inc=c_dve)
                psc_free[bank] = (c_dve, nadd)
            nms = None
            for h in range(16):
                w = []
                if h == 0:
                    w = list(atn_ready)
                    if ps_last is not None:
                        w.append(ps_last)
                nms = P.emit("tensor", lambda e, p=p, h=h: e.matmul(
                    psums[:, h:h + 1],
                    ATN[:, p * D + h * 64: p * D + (h + 1) * 64],
                    onec[:], start=True, stop=True),
                    waits=w, inc=(c_pe if h == 15 else None))
            nsadd = P.emit("vector", lambda e: e.tensor_tensor(
                SACC[:], SACC[:], psums[0:64, :], ALU.add),
                waits=[(c_pe, nms)] + add_waits, inc=c_dve)
            ps_last = (c_dve, nsadd)
            last_ev2 = None
            for c in range(8):
                w = list(atn_ready) if c == 0 else []
                if tr_free is not None:
                    w.append((c_dve, tr_free))
                if c == 0 and p in att_store:
                    w.append(att_store[p])
                ntp = P.emit("tensor", lambda e, p=p, c=c: e.transpose(
                    ptr[0][:], ATN[:, p * D + c * 128: p * D + (c + 1) * 128],
                    idt[:]), waits=w, inc=c_pe)
                last_ev2 = P.emit("vector", lambda e, p=p, c=c: e.tensor_copy(
                    ATT[:, p * D + c * 128: p * D + (c + 1) * 128], ptr[0][:]),
                    waits=[(c_pe, ntp)], inc=c_dve)
                tr_free = last_ev2
            nst = P.emit("sync", lambda e, t=t, p=p: e.dma_start(
                out=at_w[:, :, t * 128:(t + 1) * 128],
                in_=ATT[:, p * D:(p + 1) * D].rearrange("p (c i) -> p c i", i=128)),
                waits=[(c_dve, last_ev2)], inc=d_sy, inc_amt=16)
            att_store[p] = (d_sy, nst)
            at_stores.append(nst)
            atn_cons[p] = (c_pe, P.val(c_pe))
            qkf_cons[p] = (c_pe, P.val(c_pe))
            vex_cons[p] = (c_pe, P.val(c_pe))
            xft_cons[p] = (c_pe, P.val(c_pe))
        nrc = P.emit("vector", lambda e: e.reciprocal(RS[:], SACC[:]),
                     waits=[ps_last, psc_free[0], psc_free[1]], inc=c_dve)
        prev = nrc
        for h in range(16):
            prev = P.emit("vector", lambda e, h=h: e.tensor_scalar(
                OU2S[:, h * 64:(h + 1) * 64], OUACC[:, h * 64:(h + 1) * 64],
                RS[:, h:h + 1], None, ALU.mult), waits=[(c_dve, prev)], inc=c_dve)
        ou2_done = (c_dve, prev)

        atl_cons = {}
        fof_cons = {}
        ob_war = {}
        pfo_free = [None, None]
        pab_free = [None, None]
        ab_stores = []
        for t in range(NT):
            p = t % 2
            w = [(d_sy, at_stores[t])]
            if p in atl_cons:
                w.append(atl_cons[p])
            nla = P.emit("sync", lambda e, t=t, p=p: e.dma_start(
                out=ATL[0:64, p * 2 * D:(p + 1) * 2 * D].rearrange(
                    "a (h i) -> a h i", i=128),
                in_=at_r[:, :, t * 128:(t + 1) * 128]), waits=w,
                inc=d_sy, inc_amt=16)
            evs = None
            ev_hist = {}
            for h in range(16):
                bank = h % 2
                col = ((h // 2) % 4) * 128
                w = [(d_sy, nla)]
                if h == 0:
                    w.append(ou2_done)
                if h < 2 and psc_free[bank] is not None:
                    w.append(psc_free[bank])
                if h >= 2:
                    w.append((c_dve, ev_hist[h - 2]))
                nmm = P.emit("tensor", lambda e, h=h, bank=bank, col=col, p=p: e.matmul(
                    psc[bank][0:64, col:col + 128],
                    OU2S[:, h * 64:(h + 1) * 64],
                    ATL[0:64, p * 2 * D + h * 128: p * 2 * D + (h + 1) * 128],
                    start=True, stop=True), waits=w, inc=c_pe)
                drow = (h % 2) * 64
                dcol = (h // 2) * 128
                w = [(c_pe, nmm)]
                if h < 2 and p in fof_cons:
                    w.append(fof_cons[p])
                evs = P.emit("vector", lambda e, h=h, bank=bank, col=col, drow=drow, dcol=dcol, p=p: e.tensor_copy(
                    FOF[drow:drow + 64, p * D + dcol: p * D + dcol + 128],
                    psc[bank][0:64, col:col + 128]),
                    waits=w, inc=c_dve)
                ev_hist[h] = evs
                psc_free[bank] = (c_dve, evs)
            fof_ready = evs
            ob_evs = []
            for half in range(2):
                bank = pA if half == 0 else pB
                for ic in range(8):
                    w = []
                    if ic == 0:
                        w = [(c_dve, fof_ready), (d_gp, d_wo)]
                        w.append(pa_free if half == 0 else pb_free)
                    P.emit("tensor", lambda e, p=p, half=half, ic=ic, bank=bank: e.matmul(
                        bank[:],
                        FOF[:, p * D + ic * 128: p * D + (ic + 1) * 128],
                        WOT[:, ic * D + half * 512: ic * D + half * 512 + 512],
                        start=(ic == 0), stop=False), waits=w)
                nmm = P.emit("tensor", lambda e, half=half, bank=bank: e.matmul(
                    bank[:], onesb[:], BOR[:, half * 512: half * 512 + 512],
                    start=False, stop=True), waits=[(d_gp, d_bo)], inc=c_pe)
                w = [(c_pe, nmm)]
                if p in ob_war:
                    w += list(ob_war[p])
                nev = P.emit("scalar", lambda e, p=p, half=half, bank=bank: e.activation(
                    OB[:, p * D + half * 512: p * D + half * 512 + 512], bank[:],
                    AF.Identity), waits=w, inc=c_act)
                if half == 0:
                    pa_free = (c_act, nev)
                else:
                    pb_free = (c_act, nev)
                ob_evs.append(nev)
            nd2 = P.emit("vector", lambda e, t=t, p=p: e.reduce_sum(
                ln2.sall[:, t:t + 1], OB[:, p * D:(p + 1) * D], axis=AX),
                waits=[(c_act, ob_evs[-1]), (c_dve, ln2.q_zeroed)], inc=c_dve)
            ln2.stat_dves.append(nd2)
            na2 = P.emit("scalar", lambda e, t=t, p=p: e.activation(
                sqscr[:], OB[:, p * D:(p + 1) * D], AF.Square,
                accum_out=ln2.qall[:, t:t + 1]),
                waits=[(c_act, ob_evs[-1]), (c_dve, ln2.q_zeroed)], inc=c_act)
            ln2.stat_acts.append(na2)
            nab = P.emit("scalar", lambda e, t=t, p=p: e.copy(
                ABL[:, p * D:(p + 1) * D], OB[:, p * D:(p + 1) * D]),
                waits=([ob_war[p][0]] if p in ob_war else []), inc=c_act)
            nst = P.emit("sync", lambda e, t=t, p=p: e.dma_start(
                out=a_t[t], in_=ABL[:, p * D:(p + 1) * D]),
                waits=[(c_act, nab)], inc=d_sy, inc_amt=16)
            ab_stores.append(nst)
            ob_war[p] = ((d_sy, nst), (c_dve, nd2))
            atl_cons[p] = (c_pe, P.val(c_pe))
            fof_cons[p] = (c_pe, P.val(c_pe))
        ln2.emit_math(P, c_act, c_dve, eps128, invd128, neg1128, eps_wait=(d_gp, n_eps))

        yn_war = {}
        ot_free = {}
        abl_cons = {}
        for t in range(NT):
            slot = t % 2
            p = t % 2
            w = [(c_dve, x_cons[slot][0]), (c_act, x_cons[slot][1])]
            nlx = P.emit("sync", lambda e, t=t, slot=slot: e.dma_start(
                out=XB[:, slot * D:(slot + 1) * D], in_=x_t[t]), waits=w,
                inc=d_sy, inc_amt=16)
            w = [(d_sy, ab_stores[t])]
            if p in abl_cons:
                w.append(abl_cons[p])
            else:
                w.append((c_act, P.val(c_act)))
            nly = P.emit("sync", lambda e, t=t, p=p: e.dma_start(
                out=ABL[:, p * D:(p + 1) * D], in_=a_t[t]), waits=w,
                inc=d_sy, inc_amt=16)
            sc, bi = ln2.apply(t)
            w = [(d_sy, nly), (c_dve, ln2.ready)]
            if p in yn_war:
                w.append(yn_war[p])
            na = P.emit("scalar", lambda e, p=p, sc=sc, bi=bi: e.activation(
                YN[:, p * D:(p + 1) * D], ABL[:, p * D:(p + 1) * D],
                AF.Identity, bias=bi, scale=sc), waits=w, inc=c_act)
            abl_cons[p] = (c_act, na)
            w = [(d_sy, nlx), (c_act, na)]
            if p in ot_free:
                w.append(ot_free[p])
            no = P.emit("vector", lambda e, t=t, p=p, slot=slot: e.tensor_tensor(
                OT[:, p * D:(p + 1) * D], XB[:, slot * D:(slot + 1) * D],
                YN[:, p * D:(p + 1) * D], ALU.add), waits=w, inc=c_dve)
            yn_war[p] = (c_dve, no)
            x_cons[slot] = (no, x_cons[slot][1])
            ns = P.emit("sync", lambda e, t=t, p=p: e.dma_start(
                out=o_t[t], in_=OT[:, p * D:(p + 1) * D]), waits=[(c_dve, no)],
                inc=d_sy, inc_amt=16)
            ot_free[p] = (d_sy, ns)
            _heat(P, c_pe, pA, idt, (d_sy, nlx), (c_act, P.val(c_act)))
    return


def _l0attn_inmaps(inputs):
    wemb = inputs["L0_attn_W_emb"]
    return {
        "a_wqt": np.ascontiguousarray(wemb[:1024, :].T),
        "a_wvt": np.ascontiguousarray(wemb[1024:, :].T),
        "a_wat": np.ascontiguousarray(inputs["L0_attn_W_anchor"].T),
        "a_wot": np.ascontiguousarray(inputs["L0_attn_W_out"].T),
        "a_borow": inputs["L0_attn_b_out"].reshape(1, D),
    }






def emit_l1attn(nc, P, sems, x_in, out, ident, outer=None):
    c_pe, c_act, c_dve, d_sy, d_gp = sems
    wu = nc.dram_tensor("h_wu", [D, D], F32, kind="ExternalInput").ap()
    wv = nc.dram_tensor("h_wv", [D, D], F32, kind="ExternalInput").ap()
    wo = nc.dram_tensor("h_wo", [D, D], F32, kind="ExternalInput").ap()
    bur = nc.dram_tensor("h_bur", [1, D], F32, kind="ExternalInput").ap()
    bvr = nc.dram_tensor("h_bvr", [1, D], F32, kind="ExternalInput").ap()
    bor = nc.dram_tensor("h_bor", [1, D], F32, kind="ExternalInput").ap()
    w1f = nc.dram_tensor("h_w1f", [64, 64], F32, kind="ExternalInput").ap()
    w2f = nc.dram_tensor("h_w2f", [64, 64], F32, kind="ExternalInput").ap()
    w3f = nc.dram_tensor("h_w3f", [64, D], F32, kind="ExternalInput").ap()
    b1c = nc.dram_tensor("h_b1c", [64, 1], F32, kind="ExternalInput").ap()
    b2c = nc.dram_tensor("h_b2c", [64, 1], F32, kind="ExternalInput").ap()
    b3r = nc.dram_tensor("h_b3r", [1, D], F32, kind="ExternalInput").ap()
    pet = nc.dram_tensor("h_pet", [64, NT * 128], BF16, kind="ExternalInput").ap()
    dect = nc.dram_tensor("h_dect", [128, NT], F32, kind="ExternalInput").ap()
    f1c = nc.dram_tensor("h_f1c", [32, 97], BF16, kind="ExternalInput").ap()
    cosf = nc.dram_tensor("h_cosf", [128, 33 * 128], BF16, kind="ExternalInput").ap()
    sinf = nc.dram_tensor("h_sinf", [128, 33 * 128], BF16, kind="ExternalInput").ap()
    nsinf = nc.dram_tensor("h_nsinf", [128, 33 * 128], BF16, kind="ExternalInput").ap()
    cosi = nc.dram_tensor("h_cosi", [128, 33 * 128], BF16, kind="ExternalInput").ap()
    sini = nc.dram_tensor("h_sini", [128, 33 * 128], BF16, kind="ExternalInput").ap()
    nsini = nc.dram_tensor("h_nsini", [128, 33 * 128], BF16, kind="ExternalInput").ap()
    abc = nc.dram_tensor("h_abc", [66, 32], BF16, kind="ExternalInput").ap()
    hbuf = nc.dram_tensor("h_hbuf", [NT * 128, D], BF16, kind="Internal").ap()
    ubuf = nc.dram_tensor("h_ubuf", [NT * 128, D], BF16, kind="Internal").ap()
    vbuf = nc.dram_tensor("h_vbuf", [NT * 128, D], BF16, kind="Internal").ap()
    dbuf = nc.dram_tensor("h_dbuf", [66, 128, D], BF16, kind="Internal").ap()
    ycbuf = nc.dram_tensor("h_ycbuf", [NT * 128, D], BF16, kind="Internal").ap()
    abuf = nc.dram_tensor("h_abuf", [NT * 128, D], BF16, kind="Internal").ap()

    x_t = x_in.rearrange("(t p) d -> t p d", p=128)
    o_t = out.rearrange("(t p) d -> t p d", p=128)
    h_t = hbuf.rearrange("(t p) d -> t p d", p=128)
    u_t = ubuf.rearrange("(t p) d -> t p d", p=128)
    v_t = vbuf.rearrange("(t p) d -> t p d", p=128)
    yc_t = ycbuf.rearrange("(t p) d -> t p d", p=128)
    a_t = abuf.rearrange("(t p) d -> t p d", p=128)
    h_f = hbuf.rearrange("(t1 t2) d -> t1 t2 d", t2=128)
    v_f = vbuf.rearrange("(t1 t2) d -> t1 t2 d", t2=128)
    yc_f = ycbuf.rearrange("(t1 t2) d -> t1 t2 d", t2=128)
    db_w = dbuf.rearrange("K t d -> t K d")
    wu_r = wu.rearrange("(dc p) j -> p dc j", p=128)
    wv_r = wv.rearrange("(dc p) j -> p dc j", p=128)
    wo_r = wo.rearrange("(ic p) j -> p ic j", p=128)

    with (outer if outer is not None else contextlib.ExitStack()) as stack:
        sb = lambda nm, sh, dt: stack.enter_context(nc.sbuf_tensor("h" + nm, sh, dt))
        ps = lambda nm, sh, dt: stack.enter_context(nc.psum_tensor("h" + nm, sh, dt))

        POOL = sb("POOL", [128, 24576], BF16)
        POOL2 = sb("POOL2", [128, 8448], F32)
        P2B = POOL2[:].bitcast(BF16)
        XB = sb("XB", [128, 2 * D], F32)
        XH = sb("XH", [128, 2 * D], BF16)
        XFT = sb("XFT", [128, 2 * D], BF16)
        UT = sb("UT", [128, 2 * D], BF16)
        VT = sb("VT", [128, 2 * D], BF16)
        AT1 = sb("AT1", [128, 64 * 97], BF16)
        HS = sb("HS", [128, 4224], BF16)
        DSB = sb("DSB", [128, 4224], BF16)
        ABL = sb("ABL", [128, 2 * D], BF16)
        F1CB = sb("F1CB", [32, 97], BF16)
        COSF = sb("COSF", [128, 4224], BF16)
        SINF = sb("SINF", [128, 4224], BF16)
        NSINF = sb("NSINF", [128, 4224], BF16)
        COSI = sb("COSI", [128, 4224], BF16)
        SINI = sb("SINI", [128, 4224], BF16)
        NSINI = sb("NSINI", [128, 4224], BF16)
        ABC = sb("ABC", [66, 32], BF16)
        DECT = sb("DECT", [128, NT], F32)
        W1C = sb("W1C", [64, 64], BF16)
        W2C = sb("W2C", [64, 64], BF16)
        B1C = sb("B1C", [64, 1], F32)
        B2C = sb("B2C", [64, 1], F32)
        B3R = sb("B3R", [1, D], BF16)
        BUR = sb("BUR", [1, D], BF16)
        BVR = sb("BVR", [1, D], BF16)
        BOR = sb("BOR", [1, D], BF16)
        onesb = sb("onesb", [1, 128], BF16)
        eps128 = sb("eps128", [128, 1], F32)
        invd128 = sb("invd128", [128, 1], F32)
        neg1128 = sb("neg1128", [128, 1], F32)
        idt = sb("idt", [128, 128], BF16)
        sqscr = sb("sqscr", [128, D], F32)
        ptr = [ps("ptr0", [128, 128], BF16)]
        pA = ps("pA", [128, 512], F32)
        pB = ps("pB", [128, 512], F32)
        pf = [ps("pf0", [128, 388], F32), ps("pf1", [128, 388], F32)]
        ps2 = [ps("ps2a", [128, 64], F32), ps("ps2b", [128, 64], F32)]
        pC = ps("pC", [32, 512], F32)

        ln1 = LNPhase(stack, nc, "hln1")
        ln2 = LNPhase(stack, nc, "hln2")

        ln1.emit_zero(P, c_dve)
        ln2.emit_zero(P, c_dve)

        ENG_SEMS = {"tensor": (c_pe, 1), "vector": (c_dve, 1), "scalar": (c_act, 1),
                    "sync": (d_sy, 16), "gpsimd": (d_gp, 16)}

        def barrier():
            bar = [(c_pe, P.val(c_pe)), (c_act, P.val(c_act)), (c_dve, P.val(c_dve)),
                   (d_sy, P.val(d_sy)), (d_gp, P.val(d_gp))]
            P.set_barrier(bar)

        pe1 = POOL[0:64, 0:4096]
        h1v = POOL[0:64, 4096:8192]
        h2v = POOL[0:64, 8192:12288]
        w3c = POOL[0:64, 12288:13312]
        hbt = POOL[:, 13312:15360]
        wub = POOL[:, 0:8192]
        wvb = POOL[:, 8192:16384]
        btv = POOL[0:32, 0:8192]
        dsw = POOL[0:66, 8192:16384]
        ycs = POOL[0:32, 16384:24576]
        wob = POOL[:, 0:8192]
        ulv = POOL[:, 8192:10240]
        yclv = POOL[:, 10240:12288]
        y2t = POOL[:, 12288:14336]
        y2f = POOL[:, 14336:16384]
        csv = P2B[:, 0:4224]
        ysv = P2B[:, 4224:8448]
        obv = POOL2[:, 4224:6272]
        ynv = POOL2[:, 6272:8320]
        otv = POOL2[:, 0:2048]

        small_loads = [(F1CB, f1c), (ABC, abc), (DECT, dect), (idt, ident),
                       (B1C, b1c), (B2C, b2c)]
        d_small = None
        for dst, s_ap in small_loads:
            d_small = P.emit("sync", lambda e, dst=dst, s_ap=s_ap: e.dma_start(
                out=dst[:], in_=s_ap), inc=d_sy, inc_amt=16)
        for dst, s_ap in [(COSF, cosf), (SINF, sinf), (NSINF, nsinf),
                          (COSI, cosi), (SINI, sini), (NSINI, nsini)]:
            d_small = P.emit("sync", lambda e, dst=dst, s_ap=s_ap: e.dma_start(
                out=dst[:], in_=s_ap), inc=d_sy, inc_amt=16)
        d_rows = None
        for dst, s_ap in [(B3R, b3r), (BUR, bur), (BVR, bvr), (BOR, bor),
                          (W1C, w1f), (W2C, w2f)]:
            d_rows = P.emit("gpsimd", lambda e, dst=dst, s_ap=s_ap: e.dma_start(
                out=dst[:], in_=s_ap), inc=d_gp, inc_amt=16)
        P.emit("gpsimd", lambda e: e.memset(onesb[:], 1.0))
        P.emit("gpsimd", lambda e: e.memset(eps128[:], LN_EPS))
        P.emit("gpsimd", lambda e: e.memset(invd128[:], INV_D))
        n_eps = P.emit("gpsimd", lambda e: e.memset(neg1128[:], -1.0), inc=d_gp)
        d_pe1 = P.emit("gpsimd", lambda e: e.dma_start(out=pe1, in_=pet),
                       inc=d_gp, inc_amt=16)
        d_w3 = P.emit("gpsimd", lambda e: e.dma_start(out=w3c, in_=w3f),
                      inc=d_gp, inc_amt=16)

        ev_prev = None
        for src_v, dst_v, wgt, bias in ((pe1, h1v, W1C, B1C), (h1v, h2v, W2C, B2C)):
            for cc in range(8):
                bank = pA if cc % 2 == 0 else pB
                w = [(d_sy, d_small), (d_gp, d_pe1), (d_gp, d_rows)]
                if cc < 2:
                    if ev_prev is not None:
                        w.append(ev_prev)
                else:
                    w.append(ev_hist2[cc - 2])
                nmm = P.emit("tensor", lambda e, src_v=src_v, cc=cc, wgt=wgt, bank=bank: e.matmul(
                    bank[0:64, :], wgt[:], src_v[:, cc * 512:(cc + 1) * 512],
                    start=True, stop=True), waits=w, inc=c_pe)
                nev = P.emit("scalar", lambda e, dst_v=dst_v, cc=cc, bias=bias, bank=bank: e.activation(
                    dst_v[:, cc * 512:(cc + 1) * 512], bank[0:64, :],
                    AF.Gelu_apprx_tanh, bias=bias[:]), waits=[(c_pe, nmm)], inc=c_act)
                if cc == 0:
                    ev_hist2 = {}
                ev_hist2[cc] = (c_act, nev)
            ev_prev = (c_act, nev)
        h2_done = (c_act, nev)
        hb_war = {}
        for t in range(NT):
            p = t % 2
            evs = []
            for half in range(2):
                bank = pA if half == 0 else pB
                w = [h2_done, (d_gp, d_w3)]
                if (t, half) != (0, 0) and (t > 1 or half > 0):
                    pass
                if t >= 1:
                    w.append(pbank_free[half])
                P.emit("tensor", lambda e, t=t, half=half, bank=bank: e.matmul(
                    bank[:], h2v[:, t * 128:(t + 1) * 128],
                    w3c[:, half * 512:(half + 1) * 512], start=True, stop=False),
                    waits=w)
                nmm = P.emit("tensor", lambda e, half=half, bank=bank: e.matmul(
                    bank[:], onesb[:], B3R[:, half * 512:(half + 1) * 512],
                    start=False, stop=True), waits=[(d_gp, d_rows)], inc=c_pe)
                w = [(c_pe, nmm)]
                if p in hb_war:
                    w.append(hb_war[p])
                nev = P.emit("vector", lambda e, t=t, p=p, half=half, bank=bank: e.tensor_scalar(
                    hbt[:, p * D + half * 512: p * D + half * 512 + 512], bank[:],
                    DECT[:, t:t + 1], None, ALU.mult), waits=w, inc=c_dve)
                if t == 0:
                    pbank_free = [None, None]
                pbank_free[half] = (c_dve, nev)
                evs.append(nev)
            nst = P.emit("sync", lambda e, t=t, p=p: e.dma_start(
                out=h_t[t], in_=hbt[:, p * D:(p + 1) * D]),
                waits=[(c_dve, evs[-1])], inc=d_sy, inc_amt=16)
            hb_war[p] = (d_sy, nst)
        barrier()

        x_cons = {}
        for t in range(NT):
            slot = t % 2
            w = [(c_dve, x_cons[slot][0]) if slot in x_cons else None,
                 (c_act, x_cons[slot][1]) if slot in x_cons else None]
            nl = P.emit("sync", lambda e, t=t, slot=slot: e.dma_start(
                out=XB[:, slot * D:(slot + 1) * D], in_=x_t[t]), waits=w,
                inc=d_sy, inc_amt=16)
            src = XB[:, slot * D:(slot + 1) * D]
            nd = P.emit("vector", lambda e, src=src, t=t: e.reduce_sum(
                ln1.sall[:, t:t + 1], src, axis=AX),
                waits=[(d_sy, nl), (c_dve, ln1.q_zeroed)], inc=c_dve)
            ln1.stat_dves.append(nd)
            na = P.emit("scalar", lambda e, src=src, t=t: e.activation(
                sqscr[:], src, AF.Square, accum_out=ln1.qall[:, t:t + 1]),
                waits=[(d_sy, nl), (c_dve, ln1.q_zeroed)], inc=c_act)
            ln1.stat_acts.append(na)
            x_cons[slot] = (nd, na)
            _heat(P, c_pe, pA, idt, (d_sy, nl))
        ln1.emit_math(P, c_act, c_dve, eps128, invd128, neg1128, eps_wait=(d_gp, n_eps))

        d_wu = P.emit("gpsimd", lambda e: e.dma_start(
            out=wub.rearrange("p (dc j) -> p dc j", j=D), in_=wu_r), inc=d_gp, inc_amt=16)
        d_wv = P.emit("gpsimd", lambda e: e.dma_start(
            out=wvb.rearrange("p (dc j) -> p dc j", j=D), in_=wv_r), inc=d_gp, inc_amt=16)

        xh_free = {}
        tr_free = [None, None]
        uv_war = {}
        pbank_free = [None, None]
        for t in range(NT):
            slot = t % 2
            p = t % 2
            w = [(c_dve, x_cons[slot][0]), (c_act, x_cons[slot][1])]
            nl = P.emit("sync", lambda e, t=t, slot=slot: e.dma_start(
                out=XB[:, slot * D:(slot + 1) * D], in_=x_t[t]), waits=w,
                inc=d_sy, inc_amt=16)
            src = XB[:, slot * D:(slot + 1) * D]
            w = [(d_sy, nl), (c_dve, ln1.ready)]
            if p in xh_free:
                w.append((c_pe, xh_free[p]))
            sc, bi = ln1.apply(t)
            nap = P.emit("scalar", lambda e, src=src, p=p, sc=sc, bi=bi: e.activation(
                XH[:, p * D:(p + 1) * D], src, AF.Identity, bias=bi, scale=sc),
                waits=w, inc=c_act)
            x_cons[slot] = (x_cons[slot][0], nap)
            last_tp = None
            last_ev = None
            for c in range(8):
                w = [(c_act, nap)] if c == 0 else []
                if tr_free[0] is not None:
                    w.append((c_dve, tr_free[0]))
                last_tp = P.emit("tensor", lambda e, p=p, c=c: e.transpose(
                    ptr[0][:], XH[:, p * D + c * 128: p * D + (c + 1) * 128],
                    idt[:]), waits=w, inc=c_pe)
                last_ev = P.emit("vector", lambda e, p=p, c=c: e.tensor_copy(
                    XFT[:, p * D + c * 128: p * D + (c + 1) * 128], ptr[0][:]),
                    waits=[(c_pe, last_tp)], inc=c_dve)
                tr_free[0] = last_ev
            xh_free[p] = last_tp
            xft_ready = last_ev
            for which, wgt, brow, dst, dbf, dwait in (
                    (0, wub, BUR, UT, u_t, d_wu), (1, wvb, BVR, VT, v_t, d_wv)):
                evs = []
                for half in range(2):
                    bank = pA if half == 0 else pB
                    for dc in range(8):
                        w = []
                        if dc == 0:
                            w = [(c_dve, xft_ready), (d_gp, dwait)]
                            if pbank_free[half] is not None:
                                w.append(pbank_free[half])
                        P.emit("tensor", lambda e, p=p, wgt=wgt, half=half, dc=dc, bank=bank: e.matmul(
                            bank[:], XFT[:, p * D + dc * 128: p * D + (dc + 1) * 128],
                            wgt[:, dc * D + half * 512: dc * D + half * 512 + 512],
                            start=(dc == 0), stop=False), waits=w)
                    nmm = P.emit("tensor", lambda e, brow=brow, half=half, bank=bank: e.matmul(
                        bank[:], onesb[:], brow[:, half * 512: half * 512 + 512],
                        start=False, stop=True), waits=[(d_gp, d_rows)], inc=c_pe)
                    w = [(c_pe, nmm)]
                    if (p, which) in uv_war:
                        w.append(uv_war[(p, which)])
                    nev = P.emit("scalar", lambda e, p=p, dst=dst, half=half, bank=bank: e.activation(
                        dst[:, p * D + half * 512: p * D + half * 512 + 512], bank[:],
                        AF.Identity), waits=w, inc=c_act)
                    pbank_free[half] = (c_act, nev)
                    evs.append(nev)
                nst = P.emit("sync", lambda e, t=t, p=p, dst=dst, dbf=dbf: e.dma_start(
                    out=dbf[t], in_=dst[:, p * D:(p + 1) * D]),
                    waits=[(c_act, evs[-1])], inc=d_sy, inc_amt=16)
                uv_war[(p, which)] = (d_sy, nst)
        barrier()

        def emit_fft_fwd(P, srcf, hc, dst_tile, bt_war, at_war, dst_war, ps2_war):
            d0 = hc * 64
            nbl = P.emit("sync", lambda e: e.dma_start(
                out=btv.rearrange("q (a b) -> q a b", b=64),
                in_=srcf[:, :, d0:d0 + 64]), waits=[bt_war], inc=d_sy, inc_amt=16)
            ev1 = {}
            nev = None
            s1_last = None
            for d in range(64):
                bank = (d // 4) % 2
                w = []
                if d % 4 == 0:
                    w = [(d_sy, nbl), (d_sy, d_small)]
                    if ev1.get(bank) is not None:
                        w.append((c_dve, ev1[bank]))
                    elif at_war is not None and d < 8:
                        w.append(at_war)
                nmm = P.emit("tensor", lambda e, d=d, bank=bank: e.matmul(
                    pf[bank][:, (d % 4) * 97:(d % 4) * 97 + 97],
                    btv[:, d::64][0:32, :],
                    F1CB[:], start=True, stop=True),
                    waits=w, inc=(c_pe if d % 4 == 3 else None))
                if d % 4 == 3:
                    s1_last = nmm
                    nev = P.emit("vector", lambda e, d=d, bank=bank: e.tensor_copy(
                        AT1[:, (d - 3) * 97:(d + 1) * 97], pf[bank][:]),
                        waits=[(c_pe, nmm)], inc=c_dve)
                    ev1[bank] = nev
            at_done = nev
            ev2 = {}
            nevc = None
            for k1 in range(33):
                for ri in range(2):
                    bank = (k1 * 2 + ri) % 2
                    if ri == 0:
                        l1, l2 = COSF, SINF
                        r1o, r2o = 0, 64
                    else:
                        l1, l2 = COSF, NSINF
                        r1o, r2o = 64, 0
                    w = [(c_dve, at_done)]
                    if ev2.get(bank) is not None:
                        w.append((c_act, ev2[bank]))
                    elif ps2_war is not None:
                        w.append(ps2_war)
                    P.emit("tensor", lambda e, k1=k1, l1=l1, r1o=r1o, bank=bank: e.matmul(
                        ps2[bank][:, 0:64], l1[:, k1 * 128:(k1 + 1) * 128],
                        AT1[:, r1o + k1::97][:, 0:64], start=True, stop=False),
                        waits=w)
                    nmm = P.emit("tensor", lambda e, k1=k1, l2=l2, r2o=r2o, bank=bank: e.matmul(
                        ps2[bank][:, 0:64], l2[:, k1 * 128:(k1 + 1) * 128],
                        AT1[:, r2o + k1::97][:, 0:64], start=False, stop=True),
                        inc=c_pe)
                    w = [(c_pe, nmm), dst_war]
                    nevc = P.emit("scalar", lambda e, k1=k1, ri=ri, bank=bank, dst_tile=dst_tile: e.copy(
                        dst_tile[:, k1 * 128 + ri * 64: k1 * 128 + ri * 64 + 64],
                        ps2[bank][:, 0:64]), waits=w, inc=c_act)
                    ev2[bank] = nevc
            return (c_pe, s1_last), (c_dve, at_done), (c_act, nevc)

        bt_war = None
        hs_done = None
        dsw_war = None
        ycs_war = None
        yc_stores = []
        at_war = None
        cs_war = None
        ys_war = None
        ds_war = None
        for hc in range(16):
            d0 = hc * 64
            bt_war, at_war, hs_done = emit_fft_fwd(P, h_f, hc, HS, bt_war, at_war,
                                                   cs_war, ds_war)
            bt_war, at_war, cs_done = emit_fft_fwd(P, v_f, hc, csv, bt_war, at_war,
                                                   cs_war, hs_done)
            def plane(tile, ri):
                return tile[:].rearrange("p (k r d) -> p k r d", r=2, d=64)[:, :, ri, :]
            w0 = [cs_done, hs_done]
            if ys_war is not None:
                w0.append(ys_war)
            if ds_war is not None:
                w0.append(ds_war)
            tmp1 = DSB[:, 0:2112]
            tmp2 = DSB[:, 2112:4224]
            n_ = P.emit("vector", lambda e, tmp1=tmp1: e.tensor_tensor(
                tmp1, plane(csv, 0), plane(HS, 0), ALU.mult), waits=w0, inc=c_dve)
            n_ = P.emit("vector", lambda e, tmp2=tmp2: e.tensor_tensor(
                tmp2, plane(csv, 1), plane(HS, 1), ALU.mult),
                waits=[(c_dve, n_)], inc=c_dve)
            n_ = P.emit("vector", lambda e, tmp1=tmp1, tmp2=tmp2: e.tensor_tensor(
                plane(ysv, 0), tmp1, tmp2, ALU.subtract), waits=[(c_dve, n_)], inc=c_dve)
            n_ = P.emit("vector", lambda e, tmp1=tmp1: e.tensor_tensor(
                tmp1, plane(csv, 0), plane(HS, 1), ALU.mult), waits=[(c_dve, n_)], inc=c_dve)
            n_ = P.emit("vector", lambda e, tmp2=tmp2: e.tensor_tensor(
                tmp2, plane(csv, 1), plane(HS, 0), ALU.mult), waits=[(c_dve, n_)], inc=c_dve)
            n6_ = P.emit("vector", lambda e, tmp1=tmp1, tmp2=tmp2: e.tensor_tensor(
                plane(ysv, 1), tmp1, tmp2, ALU.add), waits=[(c_dve, n_)], inc=c_dve)
            ys_done = (c_dve, n6_)
            ys_war = ys_done
            cs_war = ys_done
            ev2 = {}
            nevd = None
            for k1 in range(33):
                for ri in range(2):
                    bank = (k1 * 2 + ri) % 2
                    if ri == 0:
                        l1, l2 = COSI, NSINI
                        s1, s2 = 0, 1
                    else:
                        l1, l2 = COSI, SINI
                        s1, s2 = 1, 0
                    w = [ys_done]
                    if ev2.get(bank) is not None:
                        w.append((c_dve, ev2[bank]))
                    P.emit("tensor", lambda e, k1=k1, l1=l1, s1=s1, bank=bank: e.matmul(
                        ps2[bank][:, 0:64], l1[:, k1 * 128:(k1 + 1) * 128],
                        ysv[:, k1 * 128 + s1 * 64: k1 * 128 + s1 * 64 + 64],
                        start=True, stop=False), waits=w)
                    nmm = P.emit("tensor", lambda e, k1=k1, l2=l2, s2=s2, bank=bank: e.matmul(
                        ps2[bank][:, 0:64], l2[:, k1 * 128:(k1 + 1) * 128],
                        ysv[:, k1 * 128 + s2 * 64: k1 * 128 + s2 * 64 + 64],
                        start=False, stop=True), inc=c_pe)
                    w = [(c_pe, nmm)]
                    if ds_war is not None and k1 == 0 and ri == 0:
                        w.append(ds_war)
                    nevd = P.emit("vector", lambda e, k1=k1, ri=ri, bank=bank: e.tensor_copy(
                        DSB[:, (ri * 33 + k1) * 64:(ri * 33 + k1) * 64 + 64],
                        ps2[bank][:, 0:64]), waits=w, inc=c_dve)
                    ev2[bank] = nevd
            ds_done = (c_dve, nevd)
            nds = P.emit("sync", lambda e, hc=hc: e.dma_start(
                out=db_w[:, :, hc * 64:(hc + 1) * 64],
                in_=DSB[:].rearrange("p (K d) -> p K d", d=64)),
                waits=[ds_done], inc=d_sy, inc_amt=16)
            ds_war = (d_sy, nds)
            w = [(d_sy, nds)]
            if dsw_war is not None:
                w.append(dsw_war)
            ndsw = P.emit("sync", lambda e, hc=hc: e.dma_start(
                out=dsw.rearrange("q (a b) -> q a b", b=64),
                in_=dbuf[:, :, hc * 64:(hc + 1) * 64]), waits=w, inc=d_sy, inc_amt=16)
            nyev = None
            evc = {}
            for ch in range(16):
                w = [(d_sy, ndsw), (d_sy, d_small)]
                if evc.get(0) is not None:
                    w.append((c_dve, evc[0]))
                if ycs_war is not None and ch == 0:
                    w.append(ycs_war)
                nmm = P.emit("tensor", lambda e, ch=ch: e.matmul(
                    pC[:], ABC[:], dsw[:, ch * 512:(ch + 1) * 512],
                    start=True, stop=True), waits=w, inc=c_pe)
                nyev = P.emit("vector", lambda e, ch=ch: e.tensor_copy(
                    ycs[:, ch * 512:(ch + 1) * 512], pC[:]),
                    waits=[(c_pe, nmm)], inc=c_dve)
                evc[0] = nyev
            nyst = P.emit("sync", lambda e, hc=hc: e.dma_start(
                out=yc_f[:, :, hc * 64:(hc + 1) * 64],
                in_=ycs.rearrange("q (a b) -> q a b", b=64)),
                waits=[(c_dve, nyev)], inc=d_sy, inc_amt=16)
            yc_stores.append(nyst)
            ycs_war = (d_sy, nyst)
            dsw_war = (c_pe, P.val(c_pe))
        barrier()

        d_wo = P.emit("gpsimd", lambda e: e.dma_start(
            out=wob.rearrange("p (ic j) -> p ic j", j=D), in_=wo_r), inc=d_gp, inc_amt=16)
        ul_war = {}
        y2_war = {}
        ob_war = {}
        ab_stores = []
        pbank_free = [None, None]
        tr_free = [None, None]
        for t in range(NT):
            p = t % 2
            w = [(d_sy, P.val(d_sy))] if t == 0 else []
            if p in ul_war:
                w.append(ul_war[p])
            nlu = P.emit("sync", lambda e, t=t, p=p: e.dma_start(
                out=ulv[:, p * D:(p + 1) * D], in_=u_t[t]), waits=w, inc=d_sy, inc_amt=16)
            nly = P.emit("sync", lambda e, t=t, p=p: e.dma_start(
                out=yclv[:, p * D:(p + 1) * D], in_=yc_t[t]), inc=d_sy, inc_amt=16)
            w = [(d_sy, nly)]
            if p in y2_war:
                w.append(y2_war[p])
            ny2 = P.emit("vector", lambda e, p=p: e.tensor_tensor(
                y2t[:, p * D:(p + 1) * D], ulv[:, p * D:(p + 1) * D],
                yclv[:, p * D:(p + 1) * D], ALU.mult), waits=w, inc=c_dve)
            ul_war[p] = (c_dve, ny2)
            last_tp = None
            last_ev = None
            for c in range(8):
                w = [(c_dve, ny2)] if c == 0 else []
                if tr_free[0] is not None:
                    w.append((c_dve, tr_free[0]))
                if c == 0 and p in y2_war:
                    w.append(y2_war[p])
                last_tp = P.emit("tensor", lambda e, p=p, c=c: e.transpose(
                    ptr[0][:], y2t[:, p * D + c * 128: p * D + (c + 1) * 128],
                    idt[:]), waits=w, inc=c_pe)
                last_ev = P.emit("vector", lambda e, p=p, c=c: e.tensor_copy(
                    y2f[:, p * D + c * 128: p * D + (c + 1) * 128], ptr[0][:]),
                    waits=[(c_pe, last_tp)], inc=c_dve)
                tr_free[0] = last_ev
            y2_war[p] = (c_pe, last_tp)
            y2f_ready = last_ev
            ob_evs = []
            for half in range(2):
                bank = pA if half == 0 else pB
                for ic in range(8):
                    w = []
                    if ic == 0:
                        w = [(c_dve, y2f_ready), (d_gp, d_wo)]
                        if pbank_free[half] is not None:
                            w.append(pbank_free[half])
                    P.emit("tensor", lambda e, p=p, half=half, ic=ic, bank=bank: e.matmul(
                        bank[:], y2f[:, p * D + ic * 128: p * D + (ic + 1) * 128],
                        wob[:, ic * D + half * 512: ic * D + half * 512 + 512],
                        start=(ic == 0), stop=False), waits=w)
                nmm = P.emit("tensor", lambda e, half=half, bank=bank: e.matmul(
                    bank[:], onesb[:], BOR[:, half * 512: half * 512 + 512],
                    start=False, stop=True), waits=[(d_gp, d_rows)], inc=c_pe)
                w = [(c_pe, nmm)]
                if p in ob_war:
                    w += list(ob_war[p])
                nev = P.emit("scalar", lambda e, p=p, half=half, bank=bank: e.activation(
                    obv[:, p * D + half * 512: p * D + half * 512 + 512], bank[:],
                    AF.Identity), waits=w, inc=c_act)
                pbank_free[half] = (c_act, nev)
                ob_evs.append(nev)
            nd2 = P.emit("vector", lambda e, t=t, p=p: e.reduce_sum(
                ln2.sall[:, t:t + 1], obv[:, p * D:(p + 1) * D], axis=AX),
                waits=[(c_act, ob_evs[-1]), (c_dve, ln2.q_zeroed)], inc=c_dve)
            ln2.stat_dves.append(nd2)
            na2 = P.emit("scalar", lambda e, t=t, p=p: e.activation(
                sqscr[:], obv[:, p * D:(p + 1) * D], AF.Square,
                accum_out=ln2.qall[:, t:t + 1]),
                waits=[(c_act, ob_evs[-1]), (c_dve, ln2.q_zeroed)], inc=c_act)
            ln2.stat_acts.append(na2)
            nab = P.emit("scalar", lambda e, t=t, p=p: e.copy(
                ABL[:, p * D:(p + 1) * D], obv[:, p * D:(p + 1) * D]),
                waits=([ob_war[p][0]] if p in ob_war else []), inc=c_act)
            nst = P.emit("sync", lambda e, t=t, p=p: e.dma_start(
                out=a_t[t], in_=ABL[:, p * D:(p + 1) * D]),
                waits=[(c_act, nab)], inc=d_sy, inc_amt=16)
            ab_stores.append(nst)
            ob_war[p] = ((d_sy, nst), (c_dve, nd2))
        ln2.emit_math(P, c_act, c_dve, eps128, invd128, neg1128, eps_wait=(d_gp, n_eps))

        yn_war = {}
        ot_free = {}
        abl_cons = {}
        for t in range(NT):
            slot = t % 2
            p = t % 2
            w = [(c_dve, x_cons[slot][0]), (c_act, x_cons[slot][1])]
            if t < 2:
                w.append((c_dve, P.val(c_dve)))
            nlx = P.emit("sync", lambda e, t=t, slot=slot: e.dma_start(
                out=XB[:, slot * D:(slot + 1) * D], in_=x_t[t]), waits=w,
                inc=d_sy, inc_amt=16)
            w = [(d_sy, ab_stores[t])]
            if p in abl_cons:
                w.append(abl_cons[p])
            else:
                w.append((c_act, P.val(c_act)))
            nly = P.emit("sync", lambda e, t=t, p=p: e.dma_start(
                out=ABL[:, p * D:(p + 1) * D], in_=a_t[t]), waits=w,
                inc=d_sy, inc_amt=16)
            sc, bi = ln2.apply(t)
            w = [(d_sy, nly), (c_dve, ln2.ready)]
            if p in yn_war:
                w.append(yn_war[p])
            na = P.emit("scalar", lambda e, p=p, sc=sc, bi=bi: e.activation(
                ynv[:, p * D:(p + 1) * D], ABL[:, p * D:(p + 1) * D],
                AF.Identity, bias=bi, scale=sc), waits=w, inc=c_act)
            abl_cons[p] = (c_act, na)
            w = [(d_sy, nlx), (c_act, na)]
            if p in ot_free:
                w.append(ot_free[p])
            no = P.emit("vector", lambda e, t=t, p=p, slot=slot: e.tensor_tensor(
                otv[:, p * D:(p + 1) * D], XB[:, slot * D:(slot + 1) * D],
                ynv[:, p * D:(p + 1) * D], ALU.add), waits=w, inc=c_dve)
            yn_war[p] = (c_dve, no)
            x_cons[slot] = (no, x_cons[slot][1])
            ns = P.emit("sync", lambda e, t=t, p=p: e.dma_start(
                out=o_t[t], in_=otv[:, p * D:(p + 1) * D]), waits=[(c_dve, no)],
                inc=d_sy, inc_amt=16)
            ot_free[p] = (d_sy, ns)
            _heat(P, c_pe, pA, idt, (d_sy, nlx), (c_act, P.val(c_act)))

    return


def _l1attn_consts():
    N = 8192
    t1 = np.arange(32)
    k1r = np.arange(33)
    F1C = np.zeros((32, 97), dtype=np.float32)
    F1C[:, 0:33] = np.cos(2 * np.pi * np.outer(t1, k1r) / 64)
    F1C[:, 64:97] = -np.sin(2 * np.pi * np.outer(t1, k1r) / 64)
    t2 = np.arange(128)
    k2 = np.arange(128)
    cosf = np.zeros((128, 33 * 128), dtype=np.float32)
    sinf = np.zeros((128, 33 * 128), dtype=np.float32)
    cosi = np.zeros((128, 33 * 128), dtype=np.float32)
    sini = np.zeros((128, 33 * 128), dtype=np.float32)
    for k1 in range(33):
        th = 2 * np.pi * np.outer(t2, (k1 + 64 * k2)) / N
        cosf[:, k1 * 128:(k1 + 1) * 128] = np.cos(th)
        sinf[:, k1 * 128:(k1 + 1) * 128] = np.sin(th)
        thi = th.T
        cosi[:, k1 * 128:(k1 + 1) * 128] = np.cos(thi)
        sini[:, k1 * 128:(k1 + 1) * 128] = np.sin(thi)
    ABc = np.zeros((66, 32), dtype=np.float32)
    for k1 in range(33):
        c = 1.0 if k1 in (0, 32) else 2.0
        ABc[k1, :] = c * np.cos(2 * np.pi * t1 * k1 / 64) / N
        if k1 not in (0, 32):
            ABc[33 + k1, :] = -2.0 * np.sin(2 * np.pi * t1 * k1 / 64) / N
    L, FO = 4096, 64
    positions = np.arange(L, dtype=np.float32)[:, None]
    div_term = np.exp(np.arange(0, FO, 2, dtype=np.float32) * -(np.log(10000.0) / FO))
    pe = np.zeros((L, FO), np.float32)
    pe[:, 0::2] = np.sin(positions * div_term)
    pe[:, 1::2] = np.cos(positions * div_term)
    dec = np.exp(-0.01 * np.arange(L, dtype=np.float32)).reshape(32, 128).T
    bf = ml_dtypes.bfloat16
    return {
        "f1c": F1C.astype(bf), "cosf": cosf.astype(bf), "sinf": sinf.astype(bf),
        "nsinf": (-sinf).astype(bf), "cosi": cosi.astype(bf), "sini": sini.astype(bf),
        "nsini": (-sini).astype(bf), "abc": ABc.astype(bf),
        "pet": np.ascontiguousarray(pe.T).astype(bf),
        "dect": np.ascontiguousarray(dec),
    }


_L1A_CONSTS = None


def _l1attn_inmaps(inputs):
    global _L1A_CONSTS
    if _L1A_CONSTS is None:
        _L1A_CONSTS = _l1attn_consts()
    m = {
        "h_wu": inputs["L1_attn_Wu"],
        "h_wv": inputs["L1_attn_Wv"],
        "h_wo": inputs["L1_attn_Wo"],
        "h_bur": inputs["L1_attn_bu"].reshape(1, D),
        "h_bvr": inputs["L1_attn_bv"].reshape(1, D),
        "h_bor": inputs["L1_attn_bo"].reshape(1, D),
        "h_w1f": inputs["L1_attn_W1"],
        "h_w2f": inputs["L1_attn_W2"],
        "h_w3f": inputs["L1_attn_W3"],
        "h_b1c": inputs["L1_attn_b1"].reshape(64, 1),
        "h_b2c": inputs["L1_attn_b2"].reshape(64, 1),
        "h_b3r": inputs["L1_attn_b3"].reshape(1, D),
    }
    m.update({"h_" + k: v for k, v in _L1A_CONSTS.items()})
    return m

def _heat(P, c_pe, bank, idt, wait, wait2=None):
    P.emit("tensor", lambda e: e.matmul(bank[0:1, 0:1], idt[0:1, 0:1],
                                        idt[0:1, 0:1], start=True, stop=True),
           waits=[wait, wait2], inc=c_pe)


def _bar(P, sems):
    c_pe, c_act, c_dve, d_sy, d_gp = sems
    P.set_barrier([(c_pe, P.val(c_pe)), (c_act, P.val(c_act)), (c_dve, P.val(c_dve)),
                   (d_sy, P.val(d_sy)), (d_gp, P.val(d_gp))])


def _finalize(nc, P, sems):
    c_pe, c_act, c_dve, d_sy, d_gp = sems
    P.emit("sync", lambda e: e.nop(), waits=[(d_sy, P.val(d_sy))])
    P.emit("gpsimd", lambda e: e.nop(), waits=[(d_gp, P.val(d_gp))])
    with nc.Block() as block:
        P.run(block)


def build_fused(dbg=False):
    nc = bass.Bass("TRN2", target_bir_lowering=False, debug=False)
    kind = "ExternalOutput" if dbg else "Internal"
    x0 = nc.dram_tensor("x", [NT * 128, D], F32, kind="ExternalInput").ap()
    out = nc.dram_tensor("out", [NT * 128, D], F32, kind="ExternalOutput").ap()
    ident = nc.dram_tensor("ident", [128, 128], BF16, kind="ExternalInput").ap()
    x1b = nc.dram_tensor("x1b", [NT * 128, D], F32, kind=kind).ap()
    x2b = nc.dram_tensor("x2b", [NT * 128, D], F32, kind=kind).ap()
    x3b = nc.dram_tensor("x3b", [NT * 128, D], F32, kind=kind).ap()
    P = Prog()
    with contextlib.ExitStack() as stack:
        sems = tuple(stack.enter_context(nc.semaphore(nm))
                     for nm in ("c_pe", "c_act", "c_dve", "d_sy", "d_gp"))
        emit_l0attn(nc, P, sems, x0, x1b, ident)
        _bar(P, sems)
        emit_l0ffn(nc, P, sems, x1b, x2b, ident)
        _bar(P, sems)
        emit_l1attn(nc, P, sems, x2b, x3b, ident)
        _bar(P, sems)
        emit_l1ffn(nc, P, sems, x3b, out, ident)
        _finalize(nc, P, sems)
    return nc


def _fused_inmaps(x_core, inputs):
    m = {"x": np.ascontiguousarray(x_core),
         "ident": np.eye(128, dtype=ml_dtypes.bfloat16)}
    m.update(_l0attn_inmaps(inputs))
    m.update(_l0ffn_inmaps(inputs))
    m.update(_l1attn_inmaps(inputs))
    m.update(_l1ffn_inmaps(inputs))
    return m


def _run_block(emit_fn, inmaps_fn, x_full, inputs, n_cores=8):
    nc = bass.Bass("TRN2", target_bir_lowering=False, debug=False)
    x0 = nc.dram_tensor("x", [NT * 128, D], F32, kind="ExternalInput").ap()
    out = nc.dram_tensor("out", [NT * 128, D], F32, kind="ExternalOutput").ap()
    ident = nc.dram_tensor("ident", [128, 128], BF16, kind="ExternalInput").ap()
    P = Prog()
    with contextlib.ExitStack() as stack:
        sems = tuple(stack.enter_context(nc.semaphore(nm))
                     for nm in ("c_pe", "c_act", "c_dve", "d_sy", "d_gp"))
        inner = contextlib.ExitStack()
        stack.enter_context(inner)
        emit_fn(nc, P, sems, x0, out, ident, outer=inner)
        _finalize(nc, P, sems)
    in_maps = []
    for c in range(n_cores):
        m = {"x": np.ascontiguousarray(x_full[c]),
             "ident": np.eye(128, dtype=ml_dtypes.bfloat16)}
        m.update(inmaps_fn(inputs))
        in_maps.append(m)
    res = run_bass_kernel_spmd(nc, in_maps, core_ids=list(range(n_cores)))
    return np.stack([r["out"] for r in res.results], axis=0)


def run_l0attn(x_full, inputs, n_cores=8):
    return _run_block(emit_l0attn, _l0attn_inmaps, x_full, inputs, n_cores)


def run_l0ffn(x_full, inputs, n_cores=8):
    return _run_block(emit_l0ffn, _l0ffn_inmaps, x_full, inputs, n_cores)


def run_l1attn(x_full, inputs, n_cores=8):
    return _run_block(emit_l1attn, _l1attn_inmaps, x_full, inputs, n_cores)


def run_l1ffn(x_full, inputs, n_cores=8):
    return _run_block(emit_l1ffn, _l1ffn_inmaps, x_full, inputs, n_cores)


def kernel(**inputs):
    x = inputs["L0_attn_x"]
    nc = build_fused()
    in_maps = [_fused_inmaps(x[c], inputs) for c in range(8)]
    res = run_bass_kernel_spmd(nc, in_maps, core_ids=list(range(8)))
    return np.stack([r["out"] for r in res.results], axis=0)


# version 09
# speedup vs baseline: 1.1240x; 1.0443x over previous
import contextlib
import numpy as np
import ml_dtypes
import concourse.bass as bass
import concourse.mybir as mybir
from concourse.bass_utils import run_bass_kernel_spmd

F32 = mybir.dt.float32
BF16 = mybir.dt.bfloat16
FP8 = mybir.dt.float8e4
AF = mybir.ActivationFunctionType
ALU = mybir.AluOpType
AX = mybir.AxisListType.X

NT = 32
_DBG_INTERNAL = False
D = 1024
INV_D = 1.0 / D
LN_EPS = 1e-5


class Prog:
    ENGINES = ("tensor", "vector", "scalar", "gpsimd", "sync")

    def __init__(self):
        self.ops = {e: [] for e in self.ENGINES}
        self.cnt = {}

    def val(self, sem):
        return self.cnt.get(id(sem), 0)

    def emit(self, eng, build, waits=(), inc=None, inc_amt=1):
        n = None
        if inc is not None:
            k = id(inc)
            self.cnt[k] = self.cnt.get(k, 0) + inc_amt
            n = self.cnt[k]
        pend = getattr(self, "pending", None)
        extra = pend.pop(eng, []) if pend else []
        waits = tuple((s, v) for w in list(waits) + extra if w is not None
                      for (s, v) in [w] if v and v > 0)

        def f(e, build=build, waits=waits, inc=inc, amt=inc_amt):
            for s, v in waits:
                e.wait_ge(s, v)
            inst = build(e)
            if inc is not None:
                inst.then_inc(inc, amt)
        self.ops[eng].append(f)
        return n

    def set_barrier(self, bar):
        self.pending = {e: list(bar) for e in self.ENGINES}

    def run(self, block):
        for eng in self.ENGINES:
            dec = getattr(block, eng)

            def body(e, eng=eng):
                for f in self.ops[eng]:
                    f(e)
            dec(body)


class LNPhase:

    def __init__(self, stack, nc, name):
        t = lambda nm: stack.enter_context(nc.sbuf_tensor(name + nm, [128, NT], F32))
        self.sall = t("_s")
        self.qall = t("_q")
        self.mall = t("_m")
        self.rall = t("_r")
        self.ball = t("_b")
        self.scr1 = t("_t1")
        self.scr2 = t("_t2")
        self.stat_acts = []
        self.stat_dves = []
        self.ready = None
        self.q_zeroed = None

    def emit_zero(self, P, c_dve):
        self.q_zeroed = P.emit("vector", lambda e: e.memset(self.qall[:], 0.0), inc=c_dve)

    def emit_math(self, P, c_act, c_dve, eps128, invd128, neg1128, eps_wait=None):
        last_sq = self.stat_acts[-1]
        last_rd = self.stat_dves[-1]
        n_m = P.emit("vector", lambda e: e.tensor_scalar(
            self.mall[:], self.sall[:], invd128[:], None, ALU.mult),
            waits=[eps_wait, (c_dve, last_rd)], inc=c_dve)
        n_s1 = P.emit("vector", lambda e: e.tensor_tensor(
            self.scr1[:], self.mall[:], self.mall[:], ALU.mult),
            waits=[(c_dve, n_m)], inc=c_dve)
        n_s2 = P.emit("vector", lambda e: e.tensor_scalar(
            self.scr2[:], self.qall[:], invd128[:], None, ALU.mult),
            waits=[(c_act, last_sq)], inc=c_dve)
        nv = P.emit("vector", lambda e: e.tensor_tensor(
            self.scr2[:], self.scr2[:], self.scr1[:], ALU.subtract),
            waits=[(c_dve, n_s1), (c_dve, n_s2)], inc=c_dve)
        n_ln = P.emit("scalar", lambda e: e.activation(
            self.scr1[:], self.scr2[:], AF.Ln, bias=eps128[:]),
            waits=[(c_dve, nv), eps_wait], inc=c_act)
        nr = P.emit("scalar", lambda e: e.activation(
            self.rall[:], self.scr1[:], AF.Exp, scale=-0.5),
            waits=[(c_act, n_ln)], inc=c_act)
        self.ready = P.emit("vector", lambda e: e.scalar_tensor_tensor(
            self.ball[:], self.mall[:], neg1128[:], self.rall[:], ALU.mult, ALU.mult),
            waits=[(c_act, nr), (c_dve, n_m)], inc=c_dve)

    def apply(self, t):
        return self.rall[:, t:t + 1], self.ball[:, t:t + 1]


def emit_l1ffn(nc, P, sems, x_in, out, ident, outer=None):
    c_pe, c_act, c_dve, d_sy, d_gp = sems
    kern = nc.dram_tensor("k_kern", [D, D], BF16, kind="ExternalInput").ap()
    s_in = nc.dram_tensor("k_s_in", [1, 1], F32, kind="ExternalInput").ap()
    s_k = nc.dram_tensor("k_s_k", [1, 1], F32, kind="ExternalInput").ap()
    bias = nc.dram_tensor("k_bias", [1, D], BF16, kind="ExternalInput").ap()
    _k = "ExternalOutput" if _DBG_INTERNAL else "Internal"
    ybuf = nc.dram_tensor("k_ybuf", [NT * 128, D], BF16, kind=_k).ap()
    if _DBG_INTERNAL:
        kdbg = nc.dram_tensor("k_dbg", [5, 128], F32, kind="ExternalOutput").ap()

    x_t = x_in.rearrange("(t p) d -> t p d", p=128)
    y_t = ybuf.rearrange("(t p) d -> t p d", p=128)
    o_t = out.rearrange("(t p) d -> t p d", p=128)

    with (outer if outer is not None else contextlib.ExitStack()) as stack:
        sb = lambda nm, sh, dt: stack.enter_context(nc.sbuf_tensor("k" + nm, sh, dt))
        ps = lambda nm, sh, dt: stack.enter_context(nc.psum_tensor("k" + nm, sh, dt))

        XB = sb("XB", [128, 4 * D], F32)
        KB = sb("KB", [128, 8 * D], BF16)
        K8 = sb("K8", [128, 8 * D], FP8)
        kdiv = sb("kdiv", [128, 2 * D], F32)
        onesf = sb("onesf", [1, 128], F32)
        onesb = sb("onesb", [1, 128], BF16)
        sA = sb("sA", [1, 1], F32)
        sB = sb("sB", [1, 1], F32)
        btrow = sb("btrow", [1, D], BF16)
        s128 = sb("s128", [128, 1], F32)
        k128 = sb("k128", [128, 1], F32)
        ss128 = sb("ss128", [128, 1], F32)
        rs128 = sb("rs128", [128, 1], F32)
        rk128 = sb("rk128", [128, 1], F32)
        eps128 = sb("eps128", [128, 1], F32)
        invd128 = sb("invd128", [128, 1], F32)
        neg1128 = sb("neg1128", [128, 1], F32)
        biasbc = sb("biasbc", [128, D], BF16)
        idt = sb("idt", [128, 128], BF16)
        sqscr = sb("sqscr", [128, D], F32)
        XH = sb("XH", [128, 2 * D], BF16)
        XF = sb("XF", [128, 2 * D], BF16)
        xdiv = sb("xdiv", [128, 2 * D], F32)
        X8 = sb("X8", [128, 2 * D], FP8)
        YB = sb("YB", [128, 2 * D], BF16)
        YN = sb("YN", [128, 2 * D], F32)
        OT = sb("OT", [128, 2 * D], F32)
        ptr = [ps("ptr0", [128, 128], BF16), ps("ptr1", [128, 128], BF16)]
        py = [ps("py0", [128, 512], F32), ps("py1", [128, 512], F32)]

        ln1 = LNPhase(stack, nc, "kln1")
        ln2 = LNPhase(stack, nc, "kln2")

        ln1.emit_zero(P, c_dve)
        ln2.emit_zero(P, c_dve)
        for c in range(8):
            d_kb = P.emit("sync", lambda e, c=c: e.dma_start(
                out=KB[:, c * D:(c + 1) * D],
                in_=kern[c * 128:(c + 1) * 128, :]), inc=d_sy, inc_amt=16)
        d_sa = P.emit("sync", lambda e: e.dma_start(out=sA[:], in_=s_in), inc=d_sy, inc_amt=16)
        d_sb = P.emit("sync", lambda e: e.dma_start(out=sB[:], in_=s_k), inc=d_sy, inc_amt=16)
        d_bt = P.emit("sync", lambda e: e.dma_start(out=btrow[:], in_=bias), inc=d_sy, inc_amt=16)
        d_id = P.emit("sync", lambda e: e.dma_start(out=idt[:], in_=ident), inc=d_sy, inc_amt=16)
        P.emit("gpsimd", lambda e: e.memset(onesf[:], 1.0))
        P.emit("gpsimd", lambda e: e.memset(onesb[:], 1.0))
        P.emit("gpsimd", lambda e: e.memset(eps128[:], LN_EPS))
        P.emit("gpsimd", lambda e: e.memset(invd128[:], INV_D))
        n_eps = P.emit("gpsimd", lambda e: e.memset(neg1128[:], -1.0), inc=d_gp)

        n1 = P.emit("tensor", lambda e: e.matmul(py[0][:, 0:1], onesf[:], sA[:],
                                                 start=True, stop=True),
                    waits=[(d_sy, d_sa), (d_gp, n_eps)], inc=c_pe)
        ns128 = P.emit("vector", lambda e: e.tensor_copy(s128[:], py[0][:, 0:1]),
                       waits=[(c_pe, n1)], inc=c_dve)
        n2 = P.emit("tensor", lambda e: e.matmul(py[0][:, 4:5], onesf[:], sB[:],
                                                 start=True, stop=True),
                    waits=[(d_sy, d_sb), (c_dve, ns128)], inc=c_pe)
        n3 = P.emit("vector", lambda e: e.tensor_copy(k128[:], py[0][:, 4:5]),
                    waits=[(c_pe, n2)], inc=c_dve)
        n_ss = P.emit("vector", lambda e: e.tensor_tensor(ss128[:], s128[:], k128[:], ALU.mult),
                      waits=[(c_dve, n3)], inc=c_dve)
        P.emit("vector", lambda e: e.reciprocal(rs128[:], s128[:]), waits=[(c_dve, ns128)])
        n4 = P.emit("vector", lambda e: e.reciprocal(rk128[:], k128[:]),
                    waits=[(c_dve, n3)], inc=c_dve)
        nb0 = P.emit("tensor", lambda e: e.matmul(py[1][:], onesb[:], btrow[:, 0:512],
                                                  start=True, stop=True),
                     waits=[(d_sy, d_bt), (d_gp, n_eps)], inc=c_pe)
        nb0e = P.emit("scalar", lambda e: e.activation(biasbc[:, 0:512], py[1][:], AF.Identity),
                      waits=[(c_pe, nb0)], inc=c_act)
        nb1 = P.emit("tensor", lambda e: e.matmul(py[1][:], onesb[:], btrow[:, 512:1024],
                                                  start=True, stop=True),
                     waits=[(c_act, nb0e)], inc=c_pe)
        nb1e = P.emit("scalar", lambda e: e.activation(biasbc[:, 512:1024], py[1][:], AF.Identity),
                      waits=[(c_pe, nb1)], inc=c_act)

        k8_last = None
        k8_divs = {}
        for c in range(8):
            p = c % 2
            w = [(c_dve, n4), (d_sy, d_kb)]
            if c >= 2:
                w.append((c_act, k8_divs[p]))
            ndv = P.emit("vector", lambda e, c=c, p=p: e.tensor_scalar(
                kdiv[:, p * D:(p + 1) * D], KB[:, c * D:(c + 1) * D],
                rk128[:], None, ALU.mult), waits=w, inc=c_dve)
            k8_last = P.emit("scalar", lambda e, c=c, p=p: e.copy(
                K8[:, c * D:(c + 1) * D], kdiv[:, p * D:(p + 1) * D]),
                waits=[(c_dve, ndv)], inc=c_act)
            k8_divs[p] = k8_last

        if _DBG_INTERNAL:
            P.emit("gpsimd", lambda e: e.dma_start(
                out=kdbg[0:1, :].rearrange('a b -> b a'), in_=s128[:]), waits=[(c_dve, n4)], inc=d_gp, inc_amt=16)
            P.emit("gpsimd", lambda e: e.dma_start(
                out=kdbg[1:2, :].rearrange('a b -> b a'), in_=k128[:]), inc=d_gp, inc_amt=16)
            P.emit("gpsimd", lambda e: e.dma_start(
                out=kdbg[2:3, :].rearrange('a b -> b a'), in_=rs128[:]), inc=d_gp, inc_amt=16)
            P.emit("gpsimd", lambda e: e.dma_start(
                out=kdbg[3:4, :].rearrange('a b -> b a'), in_=rk128[:]), inc=d_gp, inc_amt=16)
            P.emit("gpsimd", lambda e: e.dma_start(
                out=kdbg[4:5, :].rearrange('a b -> b a'), in_=ss128[:]), waits=[(c_dve, P.val(c_dve))],
                inc=d_gp, inc_amt=16)

        _heat(P, c_pe, py[0], idt, (c_dve, n3))
        x_cons = {}
        for t in range(NT):
            slot = t % 4
            w = [ (c_dve, x_cons[slot][0]) if slot in x_cons else None,
                  (c_act, x_cons[slot][1]) if slot in x_cons else None ]
            nl = P.emit("sync", lambda e, t=t, slot=slot: e.dma_start(
                out=XB[:, slot * D:(slot + 1) * D], in_=x_t[t]), waits=w,
                inc=d_sy, inc_amt=16)
            src = XB[:, slot * D:(slot + 1) * D]
            nd = P.emit("vector", lambda e, src=src, t=t: e.reduce_sum(
                ln1.sall[:, t:t + 1], src, axis=AX), waits=[(d_sy, nl)], inc=c_dve)
            ln1.stat_dves.append(nd)
            na = P.emit("scalar", lambda e, src=src, t=t: e.activation(
                sqscr[:], src, AF.Square, accum_out=ln1.qall[:, t:t + 1]),
                waits=[(d_sy, nl), (c_dve, ln1.q_zeroed)], inc=c_act)
            ln1.stat_acts.append(na)
            x_cons[slot] = (nd, na)
            _heat(P, c_pe, py[0], idt, (d_sy, nl))

        ln1.emit_math(P, c_act, c_dve, eps128, invd128, neg1128, eps_wait=(d_gp, n_eps))

        xh_free = {}
        xdiv_free = {}
        x8_free = {}
        yb_cons = {}
        py_free = [(c_dve, n3), (c_act, nb1e)]
        tr_free = [None, None]
        for t in range(NT):
            slot = t % 4
            p = t % 2
            w = [(c_dve, x_cons[slot][0]), (c_act, x_cons[slot][1])]
            nl = P.emit("sync", lambda e, t=t, slot=slot: e.dma_start(
                out=XB[:, slot * D:(slot + 1) * D], in_=x_t[t]), waits=w,
                inc=d_sy, inc_amt=16)
            src = XB[:, slot * D:(slot + 1) * D]
            w = [(d_sy, nl), (c_dve, ln1.ready)]
            if p in xh_free:
                w.append((c_pe, xh_free[p]))
            sc, bi = ln1.apply(t)
            nap = P.emit("scalar", lambda e, src=src, p=p, sc=sc, bi=bi: e.activation(
                XH[:, p * D:(p + 1) * D], src, AF.Identity, bias=bi, scale=sc),
                waits=w, inc=c_act)
            x_cons[slot] = (x_cons[slot][0], nap)
            last_tp = None
            for c in range(8):
                tp = c % 2
                w = [(c_act, nap)] if c < 2 else []
                if tr_free[tp] is not None:
                    w.append((c_dve, tr_free[tp]))
                if t == 0 and c == 0:
                    w.append((d_sy, d_id))
                last_tp = P.emit("tensor", lambda e, p=p, c=c, tp=tp: e.transpose(
                    ptr[tp][:], XH[:, p * D + c * 128: p * D + (c + 1) * 128],
                    idt[:]), waits=w, inc=c_pe)
                tr_free[tp] = P.emit("vector", lambda e, p=p, c=c, tp=tp: e.tensor_copy(
                    XF[:, p * D + c * 128: p * D + (c + 1) * 128], ptr[tp][:]),
                    waits=[(c_pe, last_tp)], inc=c_dve)
            xh_free[p] = last_tp
            w = [(c_dve, n4), (c_dve, tr_free[1])]
            if p in xdiv_free:
                w.append((c_act, xdiv_free[p]))
            nd = P.emit("vector", lambda e, p=p: e.tensor_scalar(
                xdiv[:, p * D:(p + 1) * D], XF[:, p * D:(p + 1) * D],
                rs128[:], None, ALU.mult), waits=w, inc=c_dve)
            w = [(c_dve, nd)]
            if p in x8_free:
                w.append((c_pe, x8_free[p]))
            n8 = P.emit("scalar", lambda e, p=p: e.copy(
                X8[:, p * D:(p + 1) * D], xdiv[:, p * D:(p + 1) * D]),
                waits=w, inc=c_act)
            xdiv_free[p] = n8
            mm_groups = []
            for half in range(2):
                nmm = None
                for c in range(8):
                    w = []
                    if c == 0:
                        w = [(c_act, n8), (c_act, k8_last), py_free[half]]
                    nmm = P.emit("tensor", lambda e, p=p, half=half, c=c: e.matmul(
                        py[half][:], X8[:, p * D + c * 128: p * D + (c + 1) * 128],
                        K8[:, c * D + half * 512: c * D + half * 512 + 512],
                        start=(c == 0), stop=(c == 7)), waits=w,
                        inc=(c_pe if c == 7 else None))
                mm_groups.append(nmm)
            x8_free[p] = mm_groups[-1]
            yevs = []
            for half in range(2):
                w = [(c_pe, mm_groups[half]), (c_dve, n_ss)]
                if p in yb_cons:
                    w += [yb_cons[p][0], yb_cons[p][1], yb_cons[p][2]]
                ne = P.emit("scalar", lambda e, p=p, half=half: e.activation(
                    YB[:, p * D + half * 512: p * D + half * 512 + 512],
                    py[half][:], AF.Identity, scale=ss128[:]),
                    waits=w, inc=c_act)
                py_free[half] = (c_act, ne)
                yevs.append(ne)
            nb = P.emit("vector", lambda e, p=p: e.tensor_tensor(
                YB[:, p * D:(p + 1) * D], YB[:, p * D:(p + 1) * D],
                biasbc[:], ALU.add), waits=[(c_act, yevs[-1]), (c_act, nb1e)], inc=c_dve)
            ns = P.emit("sync", lambda e, t=t, p=p: e.dma_start(
                out=y_t[t], in_=YB[:, p * D:(p + 1) * D]), waits=[(c_dve, nb)],
                inc=d_sy, inc_amt=16)
            nd2 = P.emit("vector", lambda e, t=t, p=p: e.reduce_sum(
                ln2.sall[:, t:t + 1], YB[:, p * D:(p + 1) * D], axis=AX),
                waits=[(c_dve, nb)], inc=c_dve)
            ln2.stat_dves.append(nd2)
            na2 = P.emit("scalar", lambda e, t=t, p=p: e.activation(
                sqscr[:], YB[:, p * D:(p + 1) * D], AF.Square,
                accum_out=ln2.qall[:, t:t + 1]),
                waits=[(c_dve, nb), (c_dve, ln2.q_zeroed)], inc=c_act)
            ln2.stat_acts.append(na2)
            yb_cons[p] = ((c_dve, nd2), (c_act, na2), (d_sy, ns))

        ln2.emit_math(P, c_act, c_dve, eps128, invd128, neg1128, eps_wait=(d_gp, n_eps))

        ot_free = {}
        yn_war = {}
        for t in range(NT):
            slot = t % 4
            p = t % 2
            w = [(c_dve, x_cons[slot][0]), (c_act, x_cons[slot][1])]
            nlx = P.emit("sync", lambda e, t=t, slot=slot: e.dma_start(
                out=XB[:, slot * D:(slot + 1) * D], in_=x_t[t]), waits=w,
                inc=d_sy, inc_amt=16)
            w = [yb_cons[p][0], yb_cons[p][1], yb_cons[p][2]]
            nly = P.emit("sync", lambda e, t=t, p=p: e.dma_start(
                out=YB[:, p * D:(p + 1) * D], in_=y_t[t]), waits=w,
                inc=d_sy, inc_amt=16)
            sc, bi = ln2.apply(t)
            w = [(d_sy, nly), (c_dve, ln2.ready)]
            if p in yn_war:
                w.append(yn_war[p])
            na = P.emit("scalar", lambda e, p=p, sc=sc, bi=bi: e.activation(
                YN[:, p * D:(p + 1) * D], YB[:, p * D:(p + 1) * D],
                AF.Identity, bias=bi, scale=sc), waits=w, inc=c_act)
            yb_cons[p] = ((c_act, na), None, None)
            w = [(d_sy, nlx), (c_act, na)]
            if p in ot_free:
                w.append(ot_free[p][1])
            no = P.emit("vector", lambda e, t=t, p=p, slot=slot: e.tensor_tensor(
                OT[:, p * D:(p + 1) * D], XB[:, slot * D:(slot + 1) * D],
                YN[:, p * D:(p + 1) * D], ALU.add), waits=w, inc=c_dve)
            yn_war[p] = (c_dve, no)
            x_cons[slot] = (no, x_cons[slot][1])
            ns = P.emit("sync", lambda e, t=t, p=p: e.dma_start(
                out=o_t[t], in_=OT[:, p * D:(p + 1) * D]), waits=[(c_dve, no)],
                inc=d_sy, inc_amt=16)
            ot_free[p] = (no, (d_sy, ns))
            _heat(P, c_pe, py[0], idt, (d_sy, nlx), (c_act, P.val(c_act)))

    return


def _l1ffn_inmaps(inputs):
    return {
        "k_kern": np.ascontiguousarray(inputs["L1_ffn_kernel"]),
        "k_s_in": inputs["L1_ffn_input_scale"].reshape(1, 1),
        "k_s_k": inputs["L1_ffn_kernel_scale"].reshape(1, 1),
        "k_bias": inputs["L1_ffn_bias"].reshape(1, D),
    }






def emit_l0ffn(nc, P, sems, x_in, out, ident, outer=None):
    c_pe, c_act, c_dve, d_sy, d_gp = sems
    H = 1408
    NHC = 11
    w1 = nc.dram_tensor("f_w1", [8, D, H], F32, kind="ExternalInput").ap()
    w3 = nc.dram_tensor("f_w3", [8, D, H], F32, kind="ExternalInput").ap()
    w2 = nc.dram_tensor("f_w2", [8, H, D], F32, kind="ExternalInput").ap()
    sw1 = nc.dram_tensor("f_sw1", [D, H], F32, kind="ExternalInput").ap()
    sw3 = nc.dram_tensor("f_sw3", [D, H], F32, kind="ExternalInput").ap()
    sw2 = nc.dram_tensor("f_sw2", [H, D], F32, kind="ExternalInput").ap()
    b1r = nc.dram_tensor("f_b1r", [128, 8 * NHC], F32, kind="ExternalInput").ap()
    b3r = nc.dram_tensor("f_b3r", [128, 8 * NHC], F32, kind="ExternalInput").ap()
    b2m = nc.dram_tensor("f_b2m", [8, D], F32, kind="ExternalInput").ap()
    sb1r = nc.dram_tensor("f_sb1r", [128, NHC], F32, kind="ExternalInput").ap()
    sb3r = nc.dram_tensor("f_sb3r", [128, NHC], F32, kind="ExternalInput").ap()
    sb2r = nc.dram_tensor("f_sb2r", [1, D], F32, kind="ExternalInput").ap()
    gwt = nc.dram_tensor("f_gwt", [D, 8], F32, kind="ExternalInput").ap()
    gbr = nc.dram_tensor("f_gbr", [1, 8], F32, kind="ExternalInput").ap()
    _k = "ExternalOutput" if _DBG_INTERNAL else "Internal"
    ybuf = nc.dram_tensor("f_ybuf", [NT * 128, D], F32, kind=_k).ap()
    xhfb = nc.dram_tensor("f_xhfb", [8, 128, NT * 128], BF16, kind=_k).ap()

    x_t = x_in.rearrange("(t p) d -> t p d", p=128)
    o_t = out.rearrange("(t p) d -> t p d", p=128)
    y_t = ybuf.rearrange("(t p) d -> t p d", p=128)
    y_r = ybuf.rearrange("(T p) d -> p T d", p=128)
    xh_w = xhfb.rearrange("c p n -> p c n")
    w1_r = w1.rearrange("e (dc p) h -> e p dc h", p=128)
    w3_r = w3.rearrange("e (dc p) h -> e p dc h", p=128)
    w2_r = w2.rearrange("e (hc p) d -> e p hc d", p=128)
    sw1_r = sw1.rearrange("(dc p) h -> p dc h", p=128)
    sw3_r = sw3.rearrange("(dc p) h -> p dc h", p=128)
    sw2_r = sw2.rearrange("(hc p) d -> p hc d", p=128)
    gwt_r = gwt.rearrange("(dc p) e -> p dc e", p=128)

    with contextlib.ExitStack() as stack:
        sb = lambda nm, sh, dt: stack.enter_context(nc.sbuf_tensor("f" + nm, sh, dt))
        ps = lambda nm, sh, dt: stack.enter_context(nc.psum_tensor("f" + nm, sh, dt))

        XB = sb("XB", [128, 2 * D], F32)
        XH = sb("XH", [128, 2 * D], BF16)
        XFT = sb("XFT", [128, 2 * D], BF16)
        XSTR = sb("XSTR", [128, 2 * 8 * 512], BF16)
        W1B = sb("W1B", [128, 8 * H], BF16)
        W3B = sb("W3B", [128, 8 * H], BF16)
        W2B = sb("W2B", [128, NHC * D], BF16)
        HT = sb("HT", [128, NHC * 512], BF16)
        GS1 = sb("GS1", [128, 512], F32)
        GS3 = sb("GS3", [128, 512], F32)
        PSC = sb("PSC", [128, 512], F32)
        OESB = sb("OESB", [128, 4 * D], F32)
        B1S = sb("B1S", [128, 8 * NHC], F32)
        B3S = sb("B3S", [128, 8 * NHC], F32)
        SB1S = sb("SB1S", [128, NHC], F32)
        SB3S = sb("SB3S", [128, NHC], F32)
        B2T = sb("B2T", [8, D], BF16)
        SB2R = sb("SB2R", [1, D], BF16)
        GWTB = sb("GWTB", [128, 8 * 8], BF16)
        GBR1 = sb("GBR1", [1, 8], F32)
        GBB = sb("GBB", [128, 8], F32)
        GT = sb("GT", [128, 2 * 8], F32)
        GE = sb("GE", [128, 8], F32)
        GP = sb("GP", [128, 8], F32)
        GBT = sb("GBT", [128, 8], F32)
        GSC = sb("GSC", [128, 8], F32)
        GM = sb("GM", [128, 1], F32)
        RW = sb("RW", [128, NT * 8], F32)
        RWB = sb("RWB", [128, 8], BF16)
        RWF = sb("RWF", [8, NT * 128], BF16)
        YBF = sb("YBF", [128, 2 * D], F32)
        TMP = sb("TMP", [128, 2 * D], F32)
        OT = sb("OT", [128, 2 * D], F32)
        onesf = sb("onesf", [1, 128], F32)
        onesb = sb("onesb", [1, 128], BF16)
        eps128 = sb("eps128", [128, 1], F32)
        invd128 = sb("invd128", [128, 1], F32)
        neg1128 = sb("neg1128", [128, 1], F32)
        negbig = sb("negbig", [128, 1], F32)
        idt = sb("idt", [128, 128], BF16)
        sqscr = sb("sqscr", [128, D], F32)
        pa1 = ps("pa1", [128, 512], F32)
        pa3 = ps("pa3", [128, 512], F32)
        po = [ps("po0", [128, 512], F32), ps("po1", [128, 512], F32)]
        ptr = [ps("ptr0", [128, 128], BF16), ps("ptr1", [128, 128], BF16)]
        pg = ps("pg", [128, 8], F32)

        ln1 = LNPhase(stack, nc, "fln1")
        ln2 = LNPhase(stack, nc, "fln2")
        ln1.emit_zero(P, c_dve)
        ln2.emit_zero(P, c_dve)

        d_small = None
        for (dst, src_ap) in [(B1S, b1r), (B3S, b3r), (SB1S, sb1r), (SB3S, sb3r),
                              (GBR1, gbr), (idt, ident)]:
            d_small = P.emit("sync", lambda e, dst=dst, src_ap=src_ap: e.dma_start(
                out=dst[:], in_=src_ap), inc=d_sy, inc_amt=16)
        d_b2 = P.emit("gpsimd", lambda e: e.dma_start(out=B2T[:], in_=b2m),
                      inc=d_gp, inc_amt=16)
        d_sb2 = P.emit("gpsimd", lambda e: e.dma_start(out=SB2R[:], in_=sb2r),
                       inc=d_gp, inc_amt=16)
        d_gw = P.emit("gpsimd", lambda e: e.dma_start(
            out=GWTB[:].rearrange("p (dc e) -> p dc e", e=8), in_=gwt_r),
            inc=d_gp, inc_amt=16)
        P.emit("gpsimd", lambda e: e.memset(onesf[:], 1.0))
        P.emit("gpsimd", lambda e: e.memset(onesb[:], 1.0))
        P.emit("gpsimd", lambda e: e.memset(eps128[:], LN_EPS))
        P.emit("gpsimd", lambda e: e.memset(invd128[:], INV_D))
        P.emit("gpsimd", lambda e: e.memset(negbig[:], -1e30))
        n_eps = P.emit("gpsimd", lambda e: e.memset(neg1128[:], -1.0), inc=d_gp)
        ngb = P.emit("tensor", lambda e: e.matmul(pg[:], onesf[:], GBR1[:],
                                                  start=True, stop=True),
                     waits=[(d_sy, d_small), (d_gp, n_eps)], inc=c_pe)
        ngbe = P.emit("vector", lambda e: e.tensor_copy(GBB[:], pg[:]),
                      waits=[(c_pe, ngb)], inc=c_dve)

        x_cons = {}
        for t in range(NT):
            slot = t % 2
            w = [(c_dve, x_cons[slot][0]) if slot in x_cons else None,
                 (c_act, x_cons[slot][1]) if slot in x_cons else None]
            nl = P.emit("sync", lambda e, t=t, slot=slot: e.dma_start(
                out=XB[:, slot * D:(slot + 1) * D], in_=x_t[t]), waits=w,
                inc=d_sy, inc_amt=16)
            src = XB[:, slot * D:(slot + 1) * D]
            nd = P.emit("vector", lambda e, src=src, t=t: e.reduce_sum(
                ln1.sall[:, t:t + 1], src, axis=AX),
                waits=[(d_sy, nl), (c_dve, ln1.q_zeroed)], inc=c_dve)
            ln1.stat_dves.append(nd)
            na = P.emit("scalar", lambda e, src=src, t=t: e.activation(
                sqscr[:], src, AF.Square, accum_out=ln1.qall[:, t:t + 1]),
                waits=[(d_sy, nl), (c_dve, ln1.q_zeroed)], inc=c_act)
            ln1.stat_acts.append(na)
            x_cons[slot] = (nd, na)
        ln1.emit_math(P, c_act, c_dve, eps128, invd128, neg1128, eps_wait=(d_gp, n_eps))

        xh_free = {}
        xft_store = {}
        tr_free = [None, None]
        xh_stores = []
        for t in range(NT):
            slot = t % 2
            p = t % 2
            w = [(c_dve, x_cons[slot][0]), (c_act, x_cons[slot][1])]
            nl = P.emit("sync", lambda e, t=t, slot=slot: e.dma_start(
                out=XB[:, slot * D:(slot + 1) * D], in_=x_t[t]), waits=w,
                inc=d_sy, inc_amt=16)
            src = XB[:, slot * D:(slot + 1) * D]
            w = [(d_sy, nl), (c_dve, ln1.ready)]
            if p in xh_free:
                w.append((c_pe, xh_free[p]))
            sc, bi = ln1.apply(t)
            nap = P.emit("scalar", lambda e, src=src, p=p, sc=sc, bi=bi: e.activation(
                XH[:, p * D:(p + 1) * D], src, AF.Identity, bias=bi, scale=sc),
                waits=w, inc=c_act)
            x_cons[slot] = (x_cons[slot][0], nap)
            last_tp = None
            last_ev = None
            for c in range(8):
                tp = c % 2
                w = [(c_act, nap)] if c < 2 else []
                if tr_free[tp] is not None:
                    w.append((c_dve, tr_free[tp]))
                if t == 0 and c == 0:
                    w.append((d_sy, d_small))
                if c < 2 and p in xft_store:
                    w.append(xft_store[p])
                last_tp = P.emit("tensor", lambda e, p=p, c=c, tp=tp: e.transpose(
                    ptr[tp][:], XH[:, p * D + c * 128: p * D + (c + 1) * 128],
                    idt[:]), waits=w, inc=c_pe)
                last_ev = P.emit("vector", lambda e, p=p, c=c, tp=tp: e.tensor_copy(
                    XFT[:, p * D + c * 128: p * D + (c + 1) * 128], ptr[tp][:]),
                    waits=[(c_pe, last_tp)], inc=c_dve)
                tr_free[tp] = last_ev
            xh_free[p] = last_tp
            nst = P.emit("sync", lambda e, t=t, p=p: e.dma_start(
                out=xh_w[:, :, t * 128:(t + 1) * 128],
                in_=XFT[:, p * D:(p + 1) * D].rearrange("p (c i) -> p c i", i=128)),
                waits=[(c_dve, last_ev)], inc=d_sy, inc_amt=16)
            xft_store[p] = (d_sy, nst)
            xh_stores.append(nst)
        xh_all = xh_stores[-1]

        gt_war = {}
        rwf_done = []
        xstr_war = {}
        xstr_ld = {}
        dve_chain = None
        rw_tp = None
        for tc in range(8):
            sp = tc % 2
            w = [(d_sy, xh_all)]
            if sp in xstr_war:
                w.append(xstr_war[sp])
            nxl = P.emit("sync", lambda e, tc=tc, sp=sp: e.dma_start(
                out=XSTR[:, sp * 4096:(sp + 1) * 4096].rearrange("p (c n) -> p c n", n=512),
                in_=xh_w[:, :, tc * 512:(tc + 1) * 512]), waits=w,
                inc=d_sy, inc_amt=16)
            xstr_ld[tc] = nxl
            for tsub in range(4):
                t = 4 * tc + tsub
                gp_ = t % 2
                w = [(d_sy, nxl), (d_gp, d_gw)]
                if gp_ in gt_war:
                    w.append(gt_war[gp_])
                nmm = None
                for dc in range(8):
                    nmm = P.emit("tensor", lambda e, sp=sp, tsub=tsub, dc=dc: e.matmul(
                        pg[:],
                        XSTR[:, sp * 4096 + dc * 512 + tsub * 128: sp * 4096 + dc * 512 + (tsub + 1) * 128],
                        GWTB[:, dc * 8:(dc + 1) * 8],
                        start=(dc == 0), stop=(dc == 7)),
                        waits=(w if dc == 0 else []), inc=(c_pe if dc == 7 else None))
                ng = P.emit("scalar", lambda e, gp_=gp_: e.activation(
                    GT[:, gp_ * 8:(gp_ + 1) * 8], pg[:], AF.Identity),
                    waits=[(c_pe, nmm)], inc=c_act)
                gt_war[gp_] = (c_act, ng)
                gtv = GT[:, gp_ * 8:(gp_ + 1) * 8]
                def dve(bld, extra=()):
                    nonlocal dve_chain
                    wts = [(c_dve, dve_chain)] if dve_chain else []
                    dve_chain = P.emit("vector", bld, waits=list(extra) + wts, inc=c_dve)
                    return dve_chain
                dve(lambda e, gtv=gtv: e.reduce_max(GM[:], gtv, axis=AX),
                    extra=[(c_act, ng)])
                dve(lambda e, gtv=gtv: e.tensor_scalar(GSC[:], gtv, GM[:], None, ALU.subtract))
                ne = P.emit("scalar", lambda e: e.activation(GE[:], GSC[:], AF.Exp),
                            waits=[(c_dve, dve_chain)], inc=c_act)
                dve(lambda e: e.reduce_sum(GM[:], GE[:], axis=AX), extra=[(c_act, ne)])
                dve(lambda e: e.reciprocal(GM[:], GM[:]))
                dve(lambda e: e.tensor_scalar(GP[:], GE[:], GM[:], None, ALU.mult))
                dve(lambda e: e.tensor_tensor(GBT[:], GP[:], GBB[:], ALU.add),
                    extra=[(c_dve, ngbe)])
                dve(lambda e: e.reduce_max(GM[:], GBT[:], axis=AX))
                dve(lambda e: e.tensor_scalar(GSC[:], GBT[:], GM[:], None, ALU.is_ge))
                dve(lambda e: e.scalar_tensor_tensor(GSC[:], GSC[:], negbig[:], GBT[:],
                                                     ALU.mult, ALU.add))
                dve(lambda e: e.reduce_max(GM[:], GSC[:], axis=AX))
                dve(lambda e: e.tensor_scalar(GSC[:], GBT[:], GM[:], None, ALU.is_ge))
                nrw = dve(lambda e, t=t: e.tensor_tensor(
                    RW[:, t * 8:(t + 1) * 8], GP[:], GSC[:], ALU.mult))
                w = [(c_dve, nrw)]
                if rw_tp is not None:
                    w.append((c_pe, rw_tp))
                nrb = P.emit("scalar", lambda e, t=t: e.copy(RWB[:], RW[:, t * 8:(t + 1) * 8]),
                             waits=w, inc=c_act)
                w = [(c_act, nrb)]
                if tr_free[0] is not None:
                    w.append((c_dve, tr_free[0]))
                rw_tp = P.emit("tensor", lambda e: e.transpose(
                    ptr[0][0:8, :], RWB[:], idt[:]), waits=w, inc=c_pe)
                nrf = P.emit("vector", lambda e, t=t: e.tensor_copy(
                    RWF[:, t * 128:(t + 1) * 128], ptr[0][0:8, :]),
                    waits=[(c_pe, rw_tp)], inc=c_dve)
                tr_free[0] = nrf
                rwf_done.append(nrf)
            xstr_war[sp] = (c_pe, P.val(c_pe))

        init_done = []
        yin_war = {}
        pai = [pa1, pa3]
        pai_free = [None, None]
        for t in range(NT):
            p = t % 2
            evs = []
            for half in range(2):
                bank = pai[half]
                w = [(c_dve, rwf_done[t]), (d_gp, d_sb2), (d_gp, d_b2)]
                if pai_free[half] is not None:
                    w.append(pai_free[half])
                P.emit("tensor", lambda e, t=t, half=half, bank=bank: e.matmul(
                    bank[:], RWF[:, t * 128:(t + 1) * 128],
                    B2T[:, half * 512:(half + 1) * 512], start=True, stop=False),
                    waits=w)
                nm2 = P.emit("tensor", lambda e, half=half, bank=bank: e.matmul(
                    bank[:], onesb[:], SB2R[:, half * 512:(half + 1) * 512],
                    start=False, stop=True), inc=c_pe)
                w = [(c_pe, nm2)]
                if p in yin_war:
                    w.append(yin_war[p])
                nev = P.emit("scalar", lambda e, p=p, half=half, bank=bank: e.activation(
                    TMP[:, p * D + half * 512: p * D + half * 512 + 512],
                    bank[:], AF.Identity), waits=w, inc=c_act)
                pai_free[half] = (c_act, nev)
                evs.append(nev)
            nst = P.emit("sync", lambda e, t=t, p=p: e.dma_start(
                out=y_t[t], in_=TMP[:, p * D:(p + 1) * D]),
                waits=[(c_act, evs[-1])], inc=d_sy, inc_amt=16)
            yin_war[p] = (d_sy, nst)
            init_done.append(nst)
        init_all = init_done[-1]

        region_last = [None] * 8
        oesb_war = None
        last_a_mm = None
        last_o_mm = None
        g1_war = pai_free[0]
        g3_war = pai_free[1]
        pp_war = None
        ht_war = [None] * 8
        po_free = [None, None]
        a_mm_hc = {}
        for e_i in range(9):
            shared = (e_i == 8)
            s1 = sw1_r if shared else w1_r[e_i]
            s3 = sw3_r if shared else w3_r[e_i]
            s2 = sw2_r if shared else w2_r[e_i]
            wl_hc = {}
            for hc in range(NHC):
                w = []
                if hc in a_mm_hc:
                    w.append((c_pe, a_mm_hc[hc]))
                P.emit("gpsimd", lambda e, s1=s1, hc=hc: e.dma_start(
                    out=W1B[:].rearrange("p (dc h) -> p dc h", h=H)[:, :, hc * 128:(hc + 1) * 128],
                    in_=s1[:, :, hc * 128:(hc + 1) * 128]), waits=w,
                    inc=d_gp, inc_amt=16)
                wl_hc[hc] = P.emit("gpsimd", lambda e, s3=s3, hc=hc: e.dma_start(
                    out=W3B[:].rearrange("p (dc h) -> p dc h", h=H)[:, :, hc * 128:(hc + 1) * 128],
                    in_=s3[:, :, hc * 128:(hc + 1) * 128]),
                    inc=d_gp, inc_amt=16)
            w2w = []
            if last_o_mm is not None:
                w2w.append((c_pe, last_o_mm))
            wl = P.emit("gpsimd", lambda e, s2=s2: e.dma_start(
                out=W2B[:].rearrange("p (hc d) -> p hc d", d=D), in_=s2), waits=w2w,
                inc=d_gp, inc_amt=16)
            for tc in range(8):
                sp = tc % 2
                w = [(d_sy, xh_all)]
                if sp in xstr_war:
                    w.append(xstr_war[sp])
                nxl = P.emit("sync", lambda e, tc=tc, sp=sp: e.dma_start(
                    out=XSTR[:, sp * 4096:(sp + 1) * 4096].rearrange(
                        "p (c n) -> p c n", n=512),
                    in_=xh_w[:, :, tc * 512:(tc + 1) * 512]), waits=w,
                    inc=d_sy, inc_amt=16)
                nh = None
                nh_list = []
                for hc in range(NHC):
                    for dc in range(8):
                        w = []
                        if dc == 0:
                            w = [(d_gp, wl_hc[hc]), (d_sy, nxl)]
                            if g1_war is not None:
                                w.append(g1_war)
                        nmm = P.emit("tensor", lambda e, sp=sp, hc=hc, dc=dc: e.matmul(
                            pa1[:],
                            W1B[:, dc * H + hc * 128: dc * H + (hc + 1) * 128],
                            XSTR[:, sp * 4096 + dc * 512: sp * 4096 + (dc + 1) * 512],
                            start=(dc == 0), stop=(dc == 7)),
                            waits=w, inc=(c_pe if dc == 7 else None))
                    a1mm = nmm
                    for dc in range(8):
                        w = []
                        if dc == 0 and g3_war is not None:
                            w = [g3_war]
                        nmm = P.emit("tensor", lambda e, sp=sp, hc=hc, dc=dc: e.matmul(
                            pa3[:],
                            W3B[:, dc * H + hc * 128: dc * H + (hc + 1) * 128],
                            XSTR[:, sp * 4096 + dc * 512: sp * 4096 + (dc + 1) * 512],
                            start=(dc == 0), stop=(dc == 7)),
                            waits=w, inc=(c_pe if dc == 7 else None))
                    a3mm = nmm
                    last_a_mm = a3mm
                    if tc == 7:
                        a_mm_hc[hc] = a3mm
                    bslice = (SB1S[:, hc:hc + 1] if shared
                              else B1S[:, e_i * NHC + hc: e_i * NHC + hc + 1])
                    b3slice = (SB3S[:, hc:hc + 1] if shared
                               else B3S[:, e_i * NHC + hc: e_i * NHC + hc + 1])
                    w = [(c_pe, a1mm)]
                    if pp_war is not None:
                        w.append(pp_war)
                    ng1 = P.emit("scalar", lambda e, bslice=bslice: e.activation(
                        GS1[:], pa1[:], AF.Identity, bias=bslice), waits=w, inc=c_act)
                    g1_war = (c_act, ng1)
                    w = [(c_pe, a3mm)]
                    if pp_war is not None:
                        w.append(pp_war)
                    ng3 = P.emit("vector", lambda e, b3slice=b3slice: e.tensor_scalar(
                        GS3[:], pa3[:], b3slice, None, ALU.add), waits=w, inc=c_dve)
                    g3_war = (c_dve, ng3)
                    npp = P.emit("vector", lambda e: e.tensor_tensor(
                        PSC[:], GS1[:], GS3[:], ALU.mult),
                        waits=[(c_act, ng1), (c_dve, ng3)], inc=c_dve)
                    pp_war = (c_dve, npp)
                    w = [(c_dve, npp)]
                    if ht_war[tc] is not None:
                        w.append(ht_war[tc])
                    nh = P.emit("scalar", lambda e, hc=hc: e.activation(
                        HT[:, hc * 512:(hc + 1) * 512], PSC[:], AF.Silu),
                        waits=w, inc=c_act)
                    nh_list.append(nh)
                h_done = nh
                last_ev = None
                for tsub in range(4):
                    for dh in range(2):
                        pb = (tsub * 2 + dh) % 2
                        for hc in range(NHC):
                            w = [(c_act, nh_list[hc])]
                            if hc == 0:
                                w.append((d_gp, wl))
                                if po_free[pb] is not None:
                                    w.append(po_free[pb])
                            nmm = P.emit("tensor", lambda e, tsub=tsub, dh=dh, hc=hc, pb=pb: e.matmul(
                                po[pb][:],
                                HT[:, hc * 512 + tsub * 128: hc * 512 + (tsub + 1) * 128],
                                W2B[:, hc * D + dh * 512: hc * D + dh * 512 + 512],
                                start=(hc == 0), stop=(hc == NHC - 1)),
                                waits=w, inc=(c_pe if hc == NHC - 1 else None))
                        last_o_mm = nmm
                        ht_war[tc] = (c_pe, nmm)
                        dst = OESB[:, tsub * D + dh * 512: tsub * D + dh * 512 + 512]
                        w = [(c_pe, nmm)]
                        if oesb_war is not None:
                            w.append(oesb_war)
                        if last_ev is not None:
                            w.append((c_dve, last_ev))
                        if shared:
                            last_ev = P.emit("vector", lambda e, dst=dst, pb=pb: e.tensor_copy(
                                dst, po[pb][:]), waits=w, inc=c_dve)
                        else:
                            rwsl = RW[:, (4 * tc + tsub) * 8 + e_i:
                                      (4 * tc + tsub) * 8 + e_i + 1]
                            last_ev = P.emit("vector",
                                lambda e, dst=dst, pb=pb, rwsl=rwsl: e.tensor_scalar(
                                    dst, po[pb][:], rwsl, None, ALU.mult),
                                waits=w, inc=c_dve)
                        po_free[pb] = (c_dve, last_ev)
                w = [(c_dve, last_ev), (d_sy, init_all)]
                if region_last[tc] is not None:
                    w.append((d_gp, region_last[tc]))
                nadd = P.emit("gpsimd", lambda e, tc=tc: e.dma_start(
                    out=y_r[:, 4 * tc:4 * tc + 4, :],
                    in_=OESB[:].rearrange("p (T d) -> p T d", d=D),
                    accum_op=ALU.add), waits=w, inc=d_gp, inc_amt=16)
                region_last[tc] = nadd
                oesb_war = (d_gp, nadd)
                xstr_war[sp] = (c_pe, last_a_mm)

        yb_cons = {}
        for t in range(NT):
            slot = t % 2
            w = [(d_gp, region_last[t // 4])]
            if slot in yb_cons:
                w += [(c_dve, yb_cons[slot][0]), (c_act, yb_cons[slot][1])]
            nl = P.emit("sync", lambda e, t=t, slot=slot: e.dma_start(
                out=YBF[:, slot * D:(slot + 1) * D], in_=y_t[t]), waits=w,
                inc=d_sy, inc_amt=16)
            src = YBF[:, slot * D:(slot + 1) * D]
            nd = P.emit("vector", lambda e, src=src, t=t: e.reduce_sum(
                ln2.sall[:, t:t + 1], src, axis=AX),
                waits=[(d_sy, nl), (c_dve, ln2.q_zeroed)], inc=c_dve)
            ln2.stat_dves.append(nd)
            na = P.emit("scalar", lambda e, src=src, t=t: e.activation(
                sqscr[:], src, AF.Square, accum_out=ln2.qall[:, t:t + 1]),
                waits=[(d_sy, nl), (c_dve, ln2.q_zeroed)], inc=c_act)
            ln2.stat_acts.append(na)
            yb_cons[slot] = (nd, na)
        ln2.emit_math(P, c_act, c_dve, eps128, invd128, neg1128, eps_wait=(d_gp, n_eps))

        yn_war = {}
        ot_free = {}
        for t in range(NT):
            slot = t % 2
            p = t % 2
            w = [(c_dve, x_cons[slot][0]), (c_act, x_cons[slot][1])]
            nlx = P.emit("sync", lambda e, t=t, slot=slot: e.dma_start(
                out=XB[:, slot * D:(slot + 1) * D], in_=x_t[t]), waits=w,
                inc=d_sy, inc_amt=16)
            w = [(c_dve, yb_cons[slot][0]), (c_act, yb_cons[slot][1])]
            nly = P.emit("sync", lambda e, t=t, slot=slot: e.dma_start(
                out=YBF[:, slot * D:(slot + 1) * D], in_=y_t[t]), waits=w,
                inc=d_sy, inc_amt=16)
            sc, bi = ln2.apply(t)
            w = [(d_sy, nly), (c_dve, ln2.ready)]
            if p in yn_war:
                w.append(yn_war[p])
            na = P.emit("scalar", lambda e, p=p, sc=sc, bi=bi, slot=slot: e.activation(
                TMP[:, p * D:(p + 1) * D], YBF[:, slot * D:(slot + 1) * D],
                AF.Identity, bias=bi, scale=sc), waits=w, inc=c_act)
            yb_cons[slot] = (yb_cons[slot][0], na)
            w = [(d_sy, nlx), (c_act, na)]
            if p in ot_free:
                w.append(ot_free[p])
            no = P.emit("vector", lambda e, t=t, p=p, slot=slot: e.tensor_tensor(
                OT[:, p * D:(p + 1) * D], XB[:, slot * D:(slot + 1) * D],
                TMP[:, p * D:(p + 1) * D], ALU.add), waits=w, inc=c_dve)
            yn_war[p] = (c_dve, no)
            x_cons[slot] = (no, x_cons[slot][1])
            ns = P.emit("sync", lambda e, t=t, p=p: e.dma_start(
                out=o_t[t], in_=OT[:, p * D:(p + 1) * D]), waits=[(c_dve, no)],
                inc=d_sy, inc_amt=16)
            ot_free[p] = (d_sy, ns)
            _heat(P, c_pe, pa1, idt, (d_sy, nlx), (c_act, P.val(c_act)))

    return


def _l0ffn_inmaps(inputs):
    b1 = inputs["L0_ffn_b1"]
    b3 = inputs["L0_ffn_b3"]
    return {
        "f_w1": inputs["L0_ffn_w1"],
        "f_w3": inputs["L0_ffn_w3"],
        "f_w2": inputs["L0_ffn_w2"],
        "f_sw1": inputs["L0_ffn_sw1"],
        "f_sw3": inputs["L0_ffn_sw3"],
        "f_sw2": inputs["L0_ffn_sw2"],
        "f_b1r": np.ascontiguousarray(b1.reshape(8, 11, 128).transpose(2, 0, 1).reshape(128, 88)),
        "f_b3r": np.ascontiguousarray(b3.reshape(8, 11, 128).transpose(2, 0, 1).reshape(128, 88)),
        "f_b2m": inputs["L0_ffn_b2"],
        "f_sb1r": np.ascontiguousarray(inputs["L0_ffn_sb1"].reshape(11, 128).T),
        "f_sb3r": np.ascontiguousarray(inputs["L0_ffn_sb3"].reshape(11, 128).T),
        "f_sb2r": inputs["L0_ffn_sb2"].reshape(1, D),
        "f_gwt": np.ascontiguousarray(inputs["L0_ffn_gate_w"].T),
        "f_gbr": inputs["L0_ffn_gate_b"].reshape(1, 8),
    }






def emit_l0attn(nc, P, sems, x_in, out, ident, outer=None):
    c_pe, c_act, c_dve, d_sy, d_gp = sems
    wqt = nc.dram_tensor("a_wqt", [D, D], F32, kind="ExternalInput").ap()
    wvt = nc.dram_tensor("a_wvt", [D, D], F32, kind="ExternalInput").ap()
    wat = nc.dram_tensor("a_wat", [64, 64], F32, kind="ExternalInput").ap()
    wot = nc.dram_tensor("a_wot", [D, D], F32, kind="ExternalInput").ap()
    borow = nc.dram_tensor("a_borow", [1, D], F32, kind="ExternalInput").ap()
    abuf = nc.dram_tensor("a_abuf", [NT * 128, D], BF16, kind="Internal").ap()
    atb = nc.dram_tensor("a_atb", [D, NT * 128], BF16, kind="Internal").ap()

    x_t = x_in.rearrange("(t p) d -> t p d", p=128)
    o_t = out.rearrange("(t p) d -> t p d", p=128)
    a_t = abuf.rearrange("(t p) d -> t p d", p=128)
    at_w = atb.rearrange("(c r) t -> r c t", r=128)
    at_r = atb.rearrange("(h a) t -> a h t", a=64)
    wqt_r = wqt.rearrange("(dc p) q -> p dc q", p=128)
    wvt_r = wvt.rearrange("(dc p) q -> p dc q", p=128)
    wot_r = wot.rearrange("(ic p) q -> p ic q", p=128)

    with (outer if outer is not None else contextlib.ExitStack()) as stack:
        sb = lambda nm, sh, dt: stack.enter_context(nc.sbuf_tensor("a" + nm, sh, dt))
        ps = lambda nm, sh, dt: stack.enter_context(nc.psum_tensor("a" + nm, sh, dt))

        XB = sb("XB", [128, 2 * D], F32)
        XH = sb("XH", [128, 2 * D], BF16)
        XFT = sb("XFT", [128, 2 * D], BF16)
        WQT = sb("WQT", [128, 8 * D], BF16)
        WVT = sb("WVT", [128, 8 * D], BF16)
        WOT = sb("WOT", [128, 8 * D], BF16)
        WAT = sb("WAT", [128, 64], BF16)
        BOR = sb("BOR", [1, D], BF16)
        QKF = sb("QKF", [64, 2 * 2048], BF16)
        VEX = sb("VEX", [128, 2 * D], BF16)
        SCT = sb("SCT", [128, D], F32)
        ETL = sb("ETL", [128, D], F32)
        ATN = sb("ATN", [128, 2 * D], BF16)
        ATT = sb("ATT", [128, 2 * D], BF16)
        LRC = sb("LRC", [128, 16], F32)
        LRC2 = sb("LRC2", [128, 16], F32)
        RS = sb("RS", [64, 16], F32)
        OU2S = sb("OU2S", [64, 16 * 64], BF16)
        ATL = sb("ATL", [64, 2 * 2 * D], BF16)
        FOF = sb("FOF", [128, 2 * D], BF16)
        OB = sb("OB", [128, 2 * D], F32)
        ABL = sb("ABL", [128, 2 * D], BF16)
        YN = sb("YN", [128, 2 * D], F32)
        OT = sb("OT", [128, 2 * D], F32)
        onesf = sb("onesf", [1, 128], F32)
        onesb = sb("onesb", [1, 128], BF16)
        onec = sb("onec", [128, 1], BF16)
        eps128 = sb("eps128", [128, 1], F32)
        invd128 = sb("invd128", [128, 1], F32)
        neg1128 = sb("neg1128", [128, 1], F32)
        idt = sb("idt", [128, 128], BF16)
        sqscr = sb("sqscr", [128, D], F32)
        ptr = [ps("ptr0", [128, 128], BF16)]
        pA = ps("pA", [128, 512], F32)
        pB = ps("pB", [128, 512], F32)
        psc = [ps("psc0", [128, 512], F32), ps("psc1", [128, 512], F32)]
        psums = ps("psums", [64, 16], F32)
        OUACC = sb("OUACC", [64, 16 * 64], F32)
        SACC = sb("SACC", [64, 16], F32)

        ln1 = LNPhase(stack, nc, "aln1")
        ln2 = LNPhase(stack, nc, "aln2")

        ln1.emit_zero(P, c_dve)
        ln2.emit_zero(P, c_dve)
        P.emit("vector", lambda e: e.memset(OUACC[:], 0.0))
        acc_zero = P.emit("vector", lambda e: e.memset(SACC[:], 0.0), inc=c_dve)

        d_id = P.emit("sync", lambda e: e.dma_start(out=idt[:], in_=ident),
                      inc=d_sy, inc_amt=16)
        d_wq = P.emit("gpsimd", lambda e: e.dma_start(
            out=WQT[:].rearrange("p (dc q) -> p dc q", q=D), in_=wqt_r),
            inc=d_gp, inc_amt=16)
        d_wv = P.emit("gpsimd", lambda e: e.dma_start(
            out=WVT[:].rearrange("p (dc q) -> p dc q", q=D), in_=wvt_r),
            inc=d_gp, inc_amt=16)
        d_wo = P.emit("gpsimd", lambda e: e.dma_start(
            out=WOT[:].rearrange("p (ic q) -> p ic q", q=D), in_=wot_r),
            inc=d_gp, inc_amt=16)
        P.emit("gpsimd", lambda e: e.dma_start(out=WAT[0:64, :], in_=wat),
               inc=d_gp, inc_amt=16)
        d_wa = P.emit("gpsimd", lambda e: e.dma_start(out=WAT[64:128, :], in_=wat),
                      inc=d_gp, inc_amt=16)
        d_bo = P.emit("gpsimd", lambda e: e.dma_start(out=BOR[:], in_=borow),
                      inc=d_gp, inc_amt=16)
        P.emit("gpsimd", lambda e: e.memset(onesf[:], 1.0))
        P.emit("gpsimd", lambda e: e.memset(onesb[:], 1.0))
        P.emit("gpsimd", lambda e: e.memset(onec[:], 1.0))
        P.emit("gpsimd", lambda e: e.memset(eps128[:], LN_EPS))
        P.emit("gpsimd", lambda e: e.memset(invd128[:], INV_D))
        n_eps = P.emit("gpsimd", lambda e: e.memset(neg1128[:], -1.0), inc=d_gp)

        x_cons = {}
        for t in range(NT):
            slot = t % 2
            w = [(c_dve, x_cons[slot][0]) if slot in x_cons else None,
                 (c_act, x_cons[slot][1]) if slot in x_cons else None]
            nl = P.emit("sync", lambda e, t=t, slot=slot: e.dma_start(
                out=XB[:, slot * D:(slot + 1) * D], in_=x_t[t]), waits=w,
                inc=d_sy, inc_amt=16)
            src = XB[:, slot * D:(slot + 1) * D]
            nd = P.emit("vector", lambda e, src=src, t=t: e.reduce_sum(
                ln1.sall[:, t:t + 1], src, axis=AX),
                waits=[(d_sy, nl), (c_dve, ln1.q_zeroed)], inc=c_dve)
            ln1.stat_dves.append(nd)
            na = P.emit("scalar", lambda e, src=src, t=t: e.activation(
                sqscr[:], src, AF.Square, accum_out=ln1.qall[:, t:t + 1]),
                waits=[(d_sy, nl), (c_dve, ln1.q_zeroed)], inc=c_act)
            ln1.stat_acts.append(na)
            x_cons[slot] = (nd, na)
            _heat(P, c_pe, pA, idt, (d_sy, nl))
        ln1.emit_math(P, c_act, c_dve, eps128, invd128, neg1128, eps_wait=(d_gp, n_eps))

        xh_free = {}
        tr_free = None
        xft_cons = {}
        qkf_cons = {}
        vex_cons = {}
        atn_cons = {}
        att_store = {}
        sct_war = None
        etl_war = None
        lrc_war = None
        pa_free = None
        pb_free = None
        psc_free = [None, None]
        at_stores = []
        ou_last = [None, None]
        ps_last = None
        for t in range(NT):
            slot = t % 2
            p = t % 2
            w = [(c_dve, x_cons[slot][0]), (c_act, x_cons[slot][1])]
            nl = P.emit("sync", lambda e, t=t, slot=slot: e.dma_start(
                out=XB[:, slot * D:(slot + 1) * D], in_=x_t[t]), waits=w,
                inc=d_sy, inc_amt=16)
            src = XB[:, slot * D:(slot + 1) * D]
            w = [(d_sy, nl), (c_dve, ln1.ready)]
            if p in xh_free:
                w.append((c_pe, xh_free[p]))
            sc, bi = ln1.apply(t)
            nap = P.emit("scalar", lambda e, src=src, p=p, sc=sc, bi=bi: e.activation(
                XH[:, p * D:(p + 1) * D], src, AF.Identity, bias=bi, scale=sc),
                waits=w, inc=c_act)
            x_cons[slot] = (x_cons[slot][0], nap)
            last_tp = None
            last_ev = None
            for c in range(8):
                w = [(c_act, nap)] if c == 0 else []
                if tr_free is not None:
                    w.append((c_dve, tr_free))
                if t == 0 and c == 0:
                    w.append((d_sy, d_id))
                if c == 0 and p in xft_cons:
                    w.append(xft_cons[p])
                last_tp = P.emit("tensor", lambda e, p=p, c=c: e.transpose(
                    ptr[0][:], XH[:, p * D + c * 128: p * D + (c + 1) * 128],
                    idt[:]), waits=w, inc=c_pe)
                last_ev = P.emit("vector", lambda e, p=p, c=c: e.tensor_copy(
                    XFT[:, p * D + c * 128: p * D + (c + 1) * 128], ptr[0][:]),
                    waits=[(c_pe, last_tp)], inc=c_dve)
                tr_free = last_ev
            xh_free[p] = last_tp
            xft_ready = last_ev
            qk_evh = {}
            qk_evs = None
            for h in range(16):
                bank = pA if h % 2 == 0 else pB
                col = ((h // 2) % 2) * 128
                for dc in range(8):
                    w = []
                    if dc == 0:
                        w = [(c_dve, xft_ready), (d_gp, d_wq)]
                        if h < 2:
                            if h % 2 == 0 and pa_free is not None:
                                w.append(pa_free)
                            if h % 2 == 1 and pb_free is not None:
                                w.append(pb_free)
                        if h >= 2:
                            w.append((c_dve, qk_evh[h - 2]))
                        if p in qkf_cons and h < 2:
                            w.append(qkf_cons[p])
                    nmm = P.emit("tensor", lambda e, p=p, h=h, dc=dc, bank=bank, col=col: e.matmul(
                        bank[0:64, col:col + 128],
                        WQT[:, dc * D + h * 64: dc * D + h * 64 + 64],
                        XFT[:, p * D + dc * 128: p * D + (dc + 1) * 128],
                        start=(dc == 0), stop=(dc == 7)),
                        waits=w, inc=(c_pe if dc == 7 else None))
                qk_evs = P.emit("vector", lambda e, p=p, h=h, bank=bank, col=col: e.tensor_copy(
                    QKF[:, p * 2048 + h * 128: p * 2048 + (h + 1) * 128],
                    bank[0:64, col:col + 128]),
                    waits=[(c_pe, nmm)], inc=c_dve)
                qk_evh[h] = qk_evs
            pa_free = (c_dve, qk_evh[14])
            pb_free0 = (c_dve, qk_evh[15])
            v_evs = None
            for half in range(2):
                for dc in range(8):
                    w = []
                    if dc == 0:
                        w = [(c_dve, xft_ready), (d_gp, d_wv), pb_free0]
                        if pb_free is not None:
                            w.append(pb_free)
                        if half == 1:
                            w.append((c_dve, v_evs))
                    nmm = P.emit("tensor", lambda e, p=p, half=half, dc=dc: e.matmul(
                        pB[:],
                        XFT[:, p * D + dc * 128: p * D + (dc + 1) * 128],
                        WVT[:, dc * D + half * 512: dc * D + half * 512 + 512],
                        start=(dc == 0), stop=(dc == 7)),
                        waits=w, inc=(c_pe if dc == 7 else None))
                w = [(c_pe, nmm)]
                if p in vex_cons:
                    w.append(vex_cons[p])
                v_evs = P.emit("vector", lambda e, p=p, half=half: e.tensor_copy(
                    VEX[:, p * D + half * 512: p * D + half * 512 + 512], pB[:]),
                    waits=w, inc=c_dve)
            pb_free = (c_dve, v_evs)
            vex_ready = v_evs
            exp_acts = []
            for bank in range(2):
                nmm = None
                for hh in range(8):
                    h = bank * 8 + hh
                    w = [(c_dve, qk_evs), (d_gp, d_wa)]
                    if hh == 0 and psc_free[bank] is not None:
                        w.append(psc_free[bank])
                    nmm = P.emit("tensor", lambda e, p=p, h=h, hh=hh, bank=bank: e.matmul(
                        psc[bank][:, hh * 64:(hh + 1) * 64],
                        QKF[0:64, p * 2048 + h * 128: p * 2048 + (h + 1) * 128],
                        WAT[0:64, :], start=True, stop=True),
                        waits=(w if hh == 0 else []), inc=(c_pe if hh == 7 else None))
                w = [(c_pe, nmm)]
                if sct_war is not None:
                    w.append(sct_war)
                nex = P.emit("scalar", lambda e, bank=bank: e.activation(
                    ETL[:, bank * 512:(bank + 1) * 512], psc[bank][:], AF.Exp,
                    scale=0.125), waits=w, inc=c_act)
                exp_acts.append(nex)
                psc_free[bank] = (c_act, nex)
            sct_war = (c_act, exp_acts[-1])
            w = [(c_act, exp_acts[-1])]
            if lrc_war is not None:
                w.append(lrc_war)
            for hq in range(16):
                nls = P.emit("vector", lambda e, hq=hq: e.reduce_sum(
                    LRC[:, hq:hq + 1], ETL[:, hq * 64:(hq + 1) * 64], axis=AX),
                    waits=(w if hq == 0 else []), inc=(c_dve if hq == 15 else None))
            nlr = P.emit("vector", lambda e: e.reciprocal(LRC2[:], LRC[:]),
                         waits=[(c_dve, nls)], inc=c_dve)
            at_ops_d = None
            at_ops_a = None
            for h in range(16):
                lsl = LRC2[:, h:h + 1]
                esl = ETL[:, h * 64:(h + 1) * 64]
                dsl = ATN[:, p * D + h * 64: p * D + (h + 1) * 64]
                w = [(c_dve, nlr)]
                if h < 2 and p in atn_cons:
                    w.append(atn_cons[p])
                if h % 2 == 0:
                    at_ops_d = P.emit("vector", lambda e, dsl=dsl, esl=esl, lsl=lsl: e.tensor_scalar(
                        dsl, esl, lsl, None, ALU.mult), waits=w, inc=c_dve)
                else:
                    at_ops_a = P.emit("scalar", lambda e, dsl=dsl, esl=esl, lsl=lsl: e.activation(
                        dsl, esl, AF.Identity, scale=lsl), waits=w, inc=c_act)
            lrc_war = (c_dve, at_ops_d)
            atn_ready = [(c_dve, at_ops_d), (c_act, at_ops_a)]
            add_waits = [(c_dve, acc_zero)]
            for bank in range(2):
                nmm = None
                for hh in range(8):
                    h = bank * 8 + hh
                    w = list(atn_ready) + [(c_dve, vex_ready)]
                    if t == 0 and h == 0:
                        w.append((d_gp, n_eps))
                    if hh == 0:
                        w.append(psc_free[bank])
                    nmm = P.emit("tensor", lambda e, p=p, h=h, hh=hh, bank=bank: e.matmul(
                        psc[bank][0:64, hh * 64:(hh + 1) * 64],
                        ATN[:, p * D + h * 64: p * D + (h + 1) * 64],
                        VEX[:, p * D + h * 64: p * D + (h + 1) * 64],
                        start=True, stop=True),
                        waits=(w if hh == 0 else list(atn_ready)[:0]),
                        inc=(c_pe if hh == 7 else None))
                nadd = P.emit("vector", lambda e, bank=bank: e.tensor_tensor(
                    OUACC[:, bank * 512:(bank + 1) * 512],
                    OUACC[:, bank * 512:(bank + 1) * 512],
                    psc[bank][0:64, :], ALU.add),
                    waits=[(c_pe, nmm)] + add_waits, inc=c_dve)
                psc_free[bank] = (c_dve, nadd)
            nms = None
            for h in range(16):
                w = []
                if h == 0:
                    w = list(atn_ready)
                    if ps_last is not None:
                        w.append(ps_last)
                nms = P.emit("tensor", lambda e, p=p, h=h: e.matmul(
                    psums[:, h:h + 1],
                    ATN[:, p * D + h * 64: p * D + (h + 1) * 64],
                    onec[:], start=True, stop=True),
                    waits=w, inc=(c_pe if h == 15 else None))
            nsadd = P.emit("vector", lambda e: e.tensor_tensor(
                SACC[:], SACC[:], psums[0:64, :], ALU.add),
                waits=[(c_pe, nms)] + add_waits, inc=c_dve)
            ps_last = (c_dve, nsadd)
            last_ev2 = None
            for c in range(8):
                w = list(atn_ready) if c == 0 else []
                if tr_free is not None:
                    w.append((c_dve, tr_free))
                if c == 0 and p in att_store:
                    w.append(att_store[p])
                ntp = P.emit("tensor", lambda e, p=p, c=c: e.transpose(
                    ptr[0][:], ATN[:, p * D + c * 128: p * D + (c + 1) * 128],
                    idt[:]), waits=w, inc=c_pe)
                last_ev2 = P.emit("vector", lambda e, p=p, c=c: e.tensor_copy(
                    ATT[:, p * D + c * 128: p * D + (c + 1) * 128], ptr[0][:]),
                    waits=[(c_pe, ntp)], inc=c_dve)
                tr_free = last_ev2
            nst = P.emit("sync", lambda e, t=t, p=p: e.dma_start(
                out=at_w[:, :, t * 128:(t + 1) * 128],
                in_=ATT[:, p * D:(p + 1) * D].rearrange("p (c i) -> p c i", i=128)),
                waits=[(c_dve, last_ev2)], inc=d_sy, inc_amt=16)
            att_store[p] = (d_sy, nst)
            at_stores.append(nst)
            atn_cons[p] = (c_pe, P.val(c_pe))
            qkf_cons[p] = (c_pe, P.val(c_pe))
            vex_cons[p] = (c_pe, P.val(c_pe))
            xft_cons[p] = (c_pe, P.val(c_pe))
        nrc = P.emit("vector", lambda e: e.reciprocal(RS[:], SACC[:]),
                     waits=[ps_last, psc_free[0], psc_free[1]], inc=c_dve)
        prev = nrc
        for h in range(16):
            prev = P.emit("vector", lambda e, h=h: e.tensor_scalar(
                OU2S[:, h * 64:(h + 1) * 64], OUACC[:, h * 64:(h + 1) * 64],
                RS[:, h:h + 1], None, ALU.mult), waits=[(c_dve, prev)], inc=c_dve)
        ou2_done = (c_dve, prev)

        atl_cons = {}
        fof_cons = {}
        ob_war = {}
        pfo_free = [None, None]
        pab_free = [None, None]
        ab_stores = []
        for t in range(NT):
            p = t % 2
            w = [(d_sy, at_stores[t])]
            if p in atl_cons:
                w.append(atl_cons[p])
            nla = P.emit("sync", lambda e, t=t, p=p: e.dma_start(
                out=ATL[0:64, p * 2 * D:(p + 1) * 2 * D].rearrange(
                    "a (h i) -> a h i", i=128),
                in_=at_r[:, :, t * 128:(t + 1) * 128]), waits=w,
                inc=d_sy, inc_amt=16)
            evs = None
            ev_hist = {}
            for h in range(16):
                bank = h % 2
                col = ((h // 2) % 4) * 128
                w = [(d_sy, nla)]
                if h == 0:
                    w.append(ou2_done)
                if h < 2 and psc_free[bank] is not None:
                    w.append(psc_free[bank])
                if h >= 2:
                    w.append((c_dve, ev_hist[h - 2]))
                nmm = P.emit("tensor", lambda e, h=h, bank=bank, col=col, p=p: e.matmul(
                    psc[bank][0:64, col:col + 128],
                    OU2S[:, h * 64:(h + 1) * 64],
                    ATL[0:64, p * 2 * D + h * 128: p * 2 * D + (h + 1) * 128],
                    start=True, stop=True), waits=w, inc=c_pe)
                drow = (h % 2) * 64
                dcol = (h // 2) * 128
                w = [(c_pe, nmm)]
                if h < 2 and p in fof_cons:
                    w.append(fof_cons[p])
                evs = P.emit("vector", lambda e, h=h, bank=bank, col=col, drow=drow, dcol=dcol, p=p: e.tensor_copy(
                    FOF[drow:drow + 64, p * D + dcol: p * D + dcol + 128],
                    psc[bank][0:64, col:col + 128]),
                    waits=w, inc=c_dve)
                ev_hist[h] = evs
                psc_free[bank] = (c_dve, evs)
            fof_ready = evs
            ob_evs = []
            for half in range(2):
                bank = pA if half == 0 else pB
                for ic in range(8):
                    w = []
                    if ic == 0:
                        w = [(c_dve, fof_ready), (d_gp, d_wo)]
                        w.append(pa_free if half == 0 else pb_free)
                    P.emit("tensor", lambda e, p=p, half=half, ic=ic, bank=bank: e.matmul(
                        bank[:],
                        FOF[:, p * D + ic * 128: p * D + (ic + 1) * 128],
                        WOT[:, ic * D + half * 512: ic * D + half * 512 + 512],
                        start=(ic == 0), stop=False), waits=w)
                nmm = P.emit("tensor", lambda e, half=half, bank=bank: e.matmul(
                    bank[:], onesb[:], BOR[:, half * 512: half * 512 + 512],
                    start=False, stop=True), waits=[(d_gp, d_bo)], inc=c_pe)
                w = [(c_pe, nmm)]
                if p in ob_war:
                    w += list(ob_war[p])
                nev = P.emit("scalar", lambda e, p=p, half=half, bank=bank: e.activation(
                    OB[:, p * D + half * 512: p * D + half * 512 + 512], bank[:],
                    AF.Identity), waits=w, inc=c_act)
                if half == 0:
                    pa_free = (c_act, nev)
                else:
                    pb_free = (c_act, nev)
                ob_evs.append(nev)
            nd2 = P.emit("vector", lambda e, t=t, p=p: e.reduce_sum(
                ln2.sall[:, t:t + 1], OB[:, p * D:(p + 1) * D], axis=AX),
                waits=[(c_act, ob_evs[-1]), (c_dve, ln2.q_zeroed)], inc=c_dve)
            ln2.stat_dves.append(nd2)
            na2 = P.emit("scalar", lambda e, t=t, p=p: e.activation(
                sqscr[:], OB[:, p * D:(p + 1) * D], AF.Square,
                accum_out=ln2.qall[:, t:t + 1]),
                waits=[(c_act, ob_evs[-1]), (c_dve, ln2.q_zeroed)], inc=c_act)
            ln2.stat_acts.append(na2)
            nab = P.emit("scalar", lambda e, t=t, p=p: e.copy(
                ABL[:, p * D:(p + 1) * D], OB[:, p * D:(p + 1) * D]),
                waits=([ob_war[p][0]] if p in ob_war else []), inc=c_act)
            nst = P.emit("sync", lambda e, t=t, p=p: e.dma_start(
                out=a_t[t], in_=ABL[:, p * D:(p + 1) * D]),
                waits=[(c_act, nab)], inc=d_sy, inc_amt=16)
            ab_stores.append(nst)
            ob_war[p] = ((d_sy, nst), (c_dve, nd2))
            atl_cons[p] = (c_pe, P.val(c_pe))
            fof_cons[p] = (c_pe, P.val(c_pe))
        ln2.emit_math(P, c_act, c_dve, eps128, invd128, neg1128, eps_wait=(d_gp, n_eps))

        yn_war = {}
        ot_free = {}
        abl_cons = {}
        for t in range(NT):
            slot = t % 2
            p = t % 2
            w = [(c_dve, x_cons[slot][0]), (c_act, x_cons[slot][1])]
            nlx = P.emit("sync", lambda e, t=t, slot=slot: e.dma_start(
                out=XB[:, slot * D:(slot + 1) * D], in_=x_t[t]), waits=w,
                inc=d_sy, inc_amt=16)
            w = [(d_sy, ab_stores[t])]
            if p in abl_cons:
                w.append(abl_cons[p])
            else:
                w.append((c_act, P.val(c_act)))
            nly = P.emit("sync", lambda e, t=t, p=p: e.dma_start(
                out=ABL[:, p * D:(p + 1) * D], in_=a_t[t]), waits=w,
                inc=d_sy, inc_amt=16)
            sc, bi = ln2.apply(t)
            w = [(d_sy, nly), (c_dve, ln2.ready)]
            if p in yn_war:
                w.append(yn_war[p])
            na = P.emit("scalar", lambda e, p=p, sc=sc, bi=bi: e.activation(
                YN[:, p * D:(p + 1) * D], ABL[:, p * D:(p + 1) * D],
                AF.Identity, bias=bi, scale=sc), waits=w, inc=c_act)
            abl_cons[p] = (c_act, na)
            w = [(d_sy, nlx), (c_act, na)]
            if p in ot_free:
                w.append(ot_free[p])
            no = P.emit("vector", lambda e, t=t, p=p, slot=slot: e.tensor_tensor(
                OT[:, p * D:(p + 1) * D], XB[:, slot * D:(slot + 1) * D],
                YN[:, p * D:(p + 1) * D], ALU.add), waits=w, inc=c_dve)
            yn_war[p] = (c_dve, no)
            x_cons[slot] = (no, x_cons[slot][1])
            ns = P.emit("sync", lambda e, t=t, p=p: e.dma_start(
                out=o_t[t], in_=OT[:, p * D:(p + 1) * D]), waits=[(c_dve, no)],
                inc=d_sy, inc_amt=16)
            ot_free[p] = (d_sy, ns)
            _heat(P, c_pe, pA, idt, (d_sy, nlx), (c_act, P.val(c_act)))
    return


def _l0attn_inmaps(inputs):
    wemb = inputs["L0_attn_W_emb"]
    return {
        "a_wqt": np.ascontiguousarray(wemb[:1024, :].T),
        "a_wvt": np.ascontiguousarray(wemb[1024:, :].T),
        "a_wat": np.ascontiguousarray(inputs["L0_attn_W_anchor"].T),
        "a_wot": np.ascontiguousarray(inputs["L0_attn_W_out"].T),
        "a_borow": inputs["L0_attn_b_out"].reshape(1, D),
    }






def emit_l1attn(nc, P, sems, x_in, out, ident, outer=None):
    c_pe, c_act, c_dve, d_sy, d_gp = sems
    wu = nc.dram_tensor("h_wu", [D, D], F32, kind="ExternalInput").ap()
    wv = nc.dram_tensor("h_wv", [D, D], F32, kind="ExternalInput").ap()
    wo = nc.dram_tensor("h_wo", [D, D], F32, kind="ExternalInput").ap()
    bur = nc.dram_tensor("h_bur", [1, D], F32, kind="ExternalInput").ap()
    bvr = nc.dram_tensor("h_bvr", [1, D], F32, kind="ExternalInput").ap()
    bor = nc.dram_tensor("h_bor", [1, D], F32, kind="ExternalInput").ap()
    w1f = nc.dram_tensor("h_w1f", [64, 64], F32, kind="ExternalInput").ap()
    w2f = nc.dram_tensor("h_w2f", [64, 64], F32, kind="ExternalInput").ap()
    w3f = nc.dram_tensor("h_w3f", [64, D], F32, kind="ExternalInput").ap()
    b1c = nc.dram_tensor("h_b1c", [64, 1], F32, kind="ExternalInput").ap()
    b2c = nc.dram_tensor("h_b2c", [64, 1], F32, kind="ExternalInput").ap()
    b3r = nc.dram_tensor("h_b3r", [1, D], F32, kind="ExternalInput").ap()
    pet = nc.dram_tensor("h_pet", [64, NT * 128], BF16, kind="ExternalInput").ap()
    dect = nc.dram_tensor("h_dect", [128, NT], F32, kind="ExternalInput").ap()
    f1c = nc.dram_tensor("h_f1c", [32, 97], BF16, kind="ExternalInput").ap()
    cosf = nc.dram_tensor("h_cosf", [128, 33 * 128], BF16, kind="ExternalInput").ap()
    sinf = nc.dram_tensor("h_sinf", [128, 33 * 128], BF16, kind="ExternalInput").ap()
    nsinf = nc.dram_tensor("h_nsinf", [128, 33 * 128], BF16, kind="ExternalInput").ap()
    cosi = nc.dram_tensor("h_cosi", [128, 33 * 128], BF16, kind="ExternalInput").ap()
    sini = nc.dram_tensor("h_sini", [128, 33 * 128], BF16, kind="ExternalInput").ap()
    nsini = nc.dram_tensor("h_nsini", [128, 33 * 128], BF16, kind="ExternalInput").ap()
    abc = nc.dram_tensor("h_abc", [66, 32], BF16, kind="ExternalInput").ap()
    hbuf = nc.dram_tensor("h_hbuf", [NT * 128, D], BF16, kind="Internal").ap()
    ubuf = nc.dram_tensor("h_ubuf", [NT * 128, D], BF16, kind="Internal").ap()
    vbuf = nc.dram_tensor("h_vbuf", [NT * 128, D], BF16, kind="Internal").ap()
    dbuf = nc.dram_tensor("h_dbuf", [66, 128, D], BF16, kind="Internal").ap()
    ycbuf = nc.dram_tensor("h_ycbuf", [NT * 128, D], BF16, kind="Internal").ap()
    abuf = nc.dram_tensor("h_abuf", [NT * 128, D], BF16, kind="Internal").ap()

    x_t = x_in.rearrange("(t p) d -> t p d", p=128)
    o_t = out.rearrange("(t p) d -> t p d", p=128)
    h_t = hbuf.rearrange("(t p) d -> t p d", p=128)
    u_t = ubuf.rearrange("(t p) d -> t p d", p=128)
    v_t = vbuf.rearrange("(t p) d -> t p d", p=128)
    yc_t = ycbuf.rearrange("(t p) d -> t p d", p=128)
    a_t = abuf.rearrange("(t p) d -> t p d", p=128)
    h_f = hbuf.rearrange("(t1 t2) d -> t1 t2 d", t2=128)
    v_f = vbuf.rearrange("(t1 t2) d -> t1 t2 d", t2=128)
    yc_f = ycbuf.rearrange("(t1 t2) d -> t1 t2 d", t2=128)
    db_w = dbuf.rearrange("K t d -> t K d")
    wu_r = wu.rearrange("(dc p) j -> p dc j", p=128)
    wv_r = wv.rearrange("(dc p) j -> p dc j", p=128)
    wo_r = wo.rearrange("(ic p) j -> p ic j", p=128)

    with (outer if outer is not None else contextlib.ExitStack()) as stack:
        sb = lambda nm, sh, dt: stack.enter_context(nc.sbuf_tensor("h" + nm, sh, dt))
        ps = lambda nm, sh, dt: stack.enter_context(nc.psum_tensor("h" + nm, sh, dt))

        POOL = sb("POOL", [128, 24576], BF16)
        POOL2 = sb("POOL2", [128, 8448], F32)
        P2B = POOL2[:].bitcast(BF16)
        XB = sb("XB", [128, 2 * D], F32)
        XH = sb("XH", [128, 2 * D], BF16)
        XFT = sb("XFT", [128, 2 * D], BF16)
        UT = sb("UT", [128, 2 * D], BF16)
        VT = sb("VT", [128, 2 * D], BF16)
        AT1 = sb("AT1", [128, 64 * 97], BF16)
        HS = sb("HS", [128, 4224], BF16)
        DSB = sb("DSB", [128, 4224], BF16)
        ABL = sb("ABL", [128, 2 * D], BF16)
        F1CB = sb("F1CB", [32, 97], BF16)
        COSF = sb("COSF", [128, 4224], BF16)
        SINF = sb("SINF", [128, 4224], BF16)
        NSINF = sb("NSINF", [128, 4224], BF16)
        COSI = sb("COSI", [128, 4224], BF16)
        SINI = sb("SINI", [128, 4224], BF16)
        NSINI = sb("NSINI", [128, 4224], BF16)
        ABC = sb("ABC", [66, 32], BF16)
        DECT = sb("DECT", [128, NT], F32)
        W1C = sb("W1C", [64, 64], BF16)
        W2C = sb("W2C", [64, 64], BF16)
        B1C = sb("B1C", [64, 1], F32)
        B2C = sb("B2C", [64, 1], F32)
        B3R = sb("B3R", [1, D], BF16)
        BUR = sb("BUR", [1, D], BF16)
        BVR = sb("BVR", [1, D], BF16)
        BOR = sb("BOR", [1, D], BF16)
        onesb = sb("onesb", [1, 128], BF16)
        eps128 = sb("eps128", [128, 1], F32)
        invd128 = sb("invd128", [128, 1], F32)
        neg1128 = sb("neg1128", [128, 1], F32)
        idt = sb("idt", [128, 128], BF16)
        sqscr = sb("sqscr", [128, D], F32)
        ptr = [ps("ptr0", [128, 128], BF16)]
        pA = ps("pA", [128, 512], F32)
        pB = ps("pB", [128, 512], F32)
        pf = [ps("pf0", [128, 388], F32), ps("pf1", [128, 388], F32)]
        ps2 = [ps("ps2a", [128, 64], F32), ps("ps2b", [128, 64], F32)]
        pC = ps("pC", [32, 512], F32)

        ln1 = LNPhase(stack, nc, "hln1")
        ln2 = LNPhase(stack, nc, "hln2")

        ln1.emit_zero(P, c_dve)
        ln2.emit_zero(P, c_dve)

        ENG_SEMS = {"tensor": (c_pe, 1), "vector": (c_dve, 1), "scalar": (c_act, 1),
                    "sync": (d_sy, 16), "gpsimd": (d_gp, 16)}

        def barrier():
            bar = [(c_pe, P.val(c_pe)), (c_act, P.val(c_act)), (c_dve, P.val(c_dve)),
                   (d_sy, P.val(d_sy)), (d_gp, P.val(d_gp))]
            P.set_barrier(bar)

        pe1 = POOL[0:64, 0:4096]
        h1v = POOL[0:64, 4096:8192]
        h2v = POOL[0:64, 8192:12288]
        w3c = POOL[0:64, 12288:13312]
        hbt = POOL[:, 13312:15360]
        wub = POOL[:, 0:8192]
        wvb = POOL[:, 8192:16384]
        btv = POOL[0:32, 0:8192]
        dsw = POOL[0:66, 8192:16384]
        ycs = POOL[0:32, 16384:24576]
        wob = POOL[:, 0:8192]
        ulv = POOL[:, 8192:10240]
        yclv = POOL[:, 10240:12288]
        y2t = POOL[:, 12288:14336]
        y2f = POOL[:, 14336:16384]
        csv = P2B[:, 0:4224]
        ysv = P2B[:, 4224:8448]
        obv = POOL2[:, 4224:6272]
        ynv = POOL2[:, 6272:8320]
        otv = POOL2[:, 0:2048]

        small_loads = [(F1CB, f1c), (ABC, abc), (DECT, dect), (idt, ident),
                       (B1C, b1c), (B2C, b2c)]
        d_small = None
        for dst, s_ap in small_loads:
            d_small = P.emit("sync", lambda e, dst=dst, s_ap=s_ap: e.dma_start(
                out=dst[:], in_=s_ap), inc=d_sy, inc_amt=16)
        for dst, s_ap in [(COSF, cosf), (SINF, sinf), (NSINF, nsinf),
                          (COSI, cosi), (SINI, sini), (NSINI, nsini)]:
            d_small = P.emit("sync", lambda e, dst=dst, s_ap=s_ap: e.dma_start(
                out=dst[:], in_=s_ap), inc=d_sy, inc_amt=16)
        d_rows = None
        for dst, s_ap in [(B3R, b3r), (BUR, bur), (BVR, bvr), (BOR, bor),
                          (W1C, w1f), (W2C, w2f)]:
            d_rows = P.emit("gpsimd", lambda e, dst=dst, s_ap=s_ap: e.dma_start(
                out=dst[:], in_=s_ap), inc=d_gp, inc_amt=16)
        P.emit("gpsimd", lambda e: e.memset(onesb[:], 1.0))
        P.emit("gpsimd", lambda e: e.memset(eps128[:], LN_EPS))
        P.emit("gpsimd", lambda e: e.memset(invd128[:], INV_D))
        n_eps = P.emit("gpsimd", lambda e: e.memset(neg1128[:], -1.0), inc=d_gp)
        d_pe1 = P.emit("gpsimd", lambda e: e.dma_start(out=pe1, in_=pet),
                       inc=d_gp, inc_amt=16)
        d_w3 = P.emit("gpsimd", lambda e: e.dma_start(out=w3c, in_=w3f),
                      inc=d_gp, inc_amt=16)

        ev_prev = None
        for src_v, dst_v, wgt, bias in ((pe1, h1v, W1C, B1C), (h1v, h2v, W2C, B2C)):
            for cc in range(8):
                bank = pA if cc % 2 == 0 else pB
                w = [(d_sy, d_small), (d_gp, d_pe1), (d_gp, d_rows)]
                if cc < 2:
                    if ev_prev is not None:
                        w.append(ev_prev)
                else:
                    w.append(ev_hist2[cc - 2])
                nmm = P.emit("tensor", lambda e, src_v=src_v, cc=cc, wgt=wgt, bank=bank: e.matmul(
                    bank[0:64, :], wgt[:], src_v[:, cc * 512:(cc + 1) * 512],
                    start=True, stop=True), waits=w, inc=c_pe)
                nev = P.emit("scalar", lambda e, dst_v=dst_v, cc=cc, bias=bias, bank=bank: e.activation(
                    dst_v[:, cc * 512:(cc + 1) * 512], bank[0:64, :],
                    AF.Gelu_apprx_tanh, bias=bias[:]), waits=[(c_pe, nmm)], inc=c_act)
                if cc == 0:
                    ev_hist2 = {}
                ev_hist2[cc] = (c_act, nev)
            ev_prev = (c_act, nev)
        h2_done = (c_act, nev)
        hb_war = {}
        for t in range(NT):
            p = t % 2
            evs = []
            for half in range(2):
                bank = pA if half == 0 else pB
                w = [h2_done, (d_gp, d_w3)]
                if (t, half) != (0, 0) and (t > 1 or half > 0):
                    pass
                if t >= 1:
                    w.append(pbank_free[half])
                P.emit("tensor", lambda e, t=t, half=half, bank=bank: e.matmul(
                    bank[:], h2v[:, t * 128:(t + 1) * 128],
                    w3c[:, half * 512:(half + 1) * 512], start=True, stop=False),
                    waits=w)
                nmm = P.emit("tensor", lambda e, half=half, bank=bank: e.matmul(
                    bank[:], onesb[:], B3R[:, half * 512:(half + 1) * 512],
                    start=False, stop=True), waits=[(d_gp, d_rows)], inc=c_pe)
                w = [(c_pe, nmm)]
                if p in hb_war:
                    w.append(hb_war[p])
                nev = P.emit("vector", lambda e, t=t, p=p, half=half, bank=bank: e.tensor_scalar(
                    hbt[:, p * D + half * 512: p * D + half * 512 + 512], bank[:],
                    DECT[:, t:t + 1], None, ALU.mult), waits=w, inc=c_dve)
                if t == 0:
                    pbank_free = [None, None]
                pbank_free[half] = (c_dve, nev)
                evs.append(nev)
            nst = P.emit("sync", lambda e, t=t, p=p: e.dma_start(
                out=h_t[t], in_=hbt[:, p * D:(p + 1) * D]),
                waits=[(c_dve, evs[-1])], inc=d_sy, inc_amt=16)
            hb_war[p] = (d_sy, nst)
        barrier()

        x_cons = {}
        for t in range(NT):
            slot = t % 2
            w = [(c_dve, x_cons[slot][0]) if slot in x_cons else None,
                 (c_act, x_cons[slot][1]) if slot in x_cons else None]
            nl = P.emit("sync", lambda e, t=t, slot=slot: e.dma_start(
                out=XB[:, slot * D:(slot + 1) * D], in_=x_t[t]), waits=w,
                inc=d_sy, inc_amt=16)
            src = XB[:, slot * D:(slot + 1) * D]
            nd = P.emit("vector", lambda e, src=src, t=t: e.reduce_sum(
                ln1.sall[:, t:t + 1], src, axis=AX),
                waits=[(d_sy, nl), (c_dve, ln1.q_zeroed)], inc=c_dve)
            ln1.stat_dves.append(nd)
            na = P.emit("scalar", lambda e, src=src, t=t: e.activation(
                sqscr[:], src, AF.Square, accum_out=ln1.qall[:, t:t + 1]),
                waits=[(d_sy, nl), (c_dve, ln1.q_zeroed)], inc=c_act)
            ln1.stat_acts.append(na)
            x_cons[slot] = (nd, na)
            _heat(P, c_pe, pA, idt, (d_sy, nl))
        ln1.emit_math(P, c_act, c_dve, eps128, invd128, neg1128, eps_wait=(d_gp, n_eps))

        d_wu = P.emit("gpsimd", lambda e: e.dma_start(
            out=wub.rearrange("p (dc j) -> p dc j", j=D), in_=wu_r), inc=d_gp, inc_amt=16)
        d_wv = P.emit("gpsimd", lambda e: e.dma_start(
            out=wvb.rearrange("p (dc j) -> p dc j", j=D), in_=wv_r), inc=d_gp, inc_amt=16)

        xh_free = {}
        tr_free = [None, None]
        uv_war = {}
        pbank_free = [None, None]
        for t in range(NT):
            slot = t % 2
            p = t % 2
            w = [(c_dve, x_cons[slot][0]), (c_act, x_cons[slot][1])]
            nl = P.emit("sync", lambda e, t=t, slot=slot: e.dma_start(
                out=XB[:, slot * D:(slot + 1) * D], in_=x_t[t]), waits=w,
                inc=d_sy, inc_amt=16)
            src = XB[:, slot * D:(slot + 1) * D]
            w = [(d_sy, nl), (c_dve, ln1.ready)]
            if p in xh_free:
                w.append((c_pe, xh_free[p]))
            sc, bi = ln1.apply(t)
            nap = P.emit("scalar", lambda e, src=src, p=p, sc=sc, bi=bi: e.activation(
                XH[:, p * D:(p + 1) * D], src, AF.Identity, bias=bi, scale=sc),
                waits=w, inc=c_act)
            x_cons[slot] = (x_cons[slot][0], nap)
            last_tp = None
            last_ev = None
            for c in range(8):
                w = [(c_act, nap)] if c == 0 else []
                if tr_free[0] is not None:
                    w.append((c_dve, tr_free[0]))
                last_tp = P.emit("tensor", lambda e, p=p, c=c: e.transpose(
                    ptr[0][:], XH[:, p * D + c * 128: p * D + (c + 1) * 128],
                    idt[:]), waits=w, inc=c_pe)
                last_ev = P.emit("vector", lambda e, p=p, c=c: e.tensor_copy(
                    XFT[:, p * D + c * 128: p * D + (c + 1) * 128], ptr[0][:]),
                    waits=[(c_pe, last_tp)], inc=c_dve)
                tr_free[0] = last_ev
            xh_free[p] = last_tp
            xft_ready = last_ev
            for which, wgt, brow, dst, dbf, dwait in (
                    (0, wub, BUR, UT, u_t, d_wu), (1, wvb, BVR, VT, v_t, d_wv)):
                evs = []
                for half in range(2):
                    bank = pA if half == 0 else pB
                    for dc in range(8):
                        w = []
                        if dc == 0:
                            w = [(c_dve, xft_ready), (d_gp, dwait)]
                            if pbank_free[half] is not None:
                                w.append(pbank_free[half])
                        P.emit("tensor", lambda e, p=p, wgt=wgt, half=half, dc=dc, bank=bank: e.matmul(
                            bank[:], XFT[:, p * D + dc * 128: p * D + (dc + 1) * 128],
                            wgt[:, dc * D + half * 512: dc * D + half * 512 + 512],
                            start=(dc == 0), stop=False), waits=w)
                    nmm = P.emit("tensor", lambda e, brow=brow, half=half, bank=bank: e.matmul(
                        bank[:], onesb[:], brow[:, half * 512: half * 512 + 512],
                        start=False, stop=True), waits=[(d_gp, d_rows)], inc=c_pe)
                    w = [(c_pe, nmm)]
                    if (p, which) in uv_war:
                        w.append(uv_war[(p, which)])
                    nev = P.emit("scalar", lambda e, p=p, dst=dst, half=half, bank=bank: e.activation(
                        dst[:, p * D + half * 512: p * D + half * 512 + 512], bank[:],
                        AF.Identity), waits=w, inc=c_act)
                    pbank_free[half] = (c_act, nev)
                    evs.append(nev)
                nst = P.emit("sync", lambda e, t=t, p=p, dst=dst, dbf=dbf: e.dma_start(
                    out=dbf[t], in_=dst[:, p * D:(p + 1) * D]),
                    waits=[(c_act, evs[-1])], inc=d_sy, inc_amt=16)
                uv_war[(p, which)] = (d_sy, nst)
        barrier()

        def emit_fft_fwd(P, srcf, hc, dst_tile, bt_war, at_war, dst_war, ps2_war):
            d0 = hc * 64
            nbl = P.emit("sync", lambda e: e.dma_start(
                out=btv.rearrange("q (a b) -> q a b", b=64),
                in_=srcf[:, :, d0:d0 + 64]), waits=[bt_war], inc=d_sy, inc_amt=16)
            ev1 = {}
            nev = None
            s1_last = None
            for d in range(64):
                bank = (d // 4) % 2
                w = []
                if d % 4 == 0:
                    w = [(d_sy, nbl), (d_sy, d_small)]
                    if ev1.get(bank) is not None:
                        w.append((c_dve, ev1[bank]))
                    elif at_war is not None and d < 8:
                        w.append(at_war)
                nmm = P.emit("tensor", lambda e, d=d, bank=bank: e.matmul(
                    pf[bank][:, (d % 4) * 97:(d % 4) * 97 + 97],
                    btv[:, d::64][0:32, :],
                    F1CB[:], start=True, stop=True),
                    waits=w, inc=(c_pe if d % 4 == 3 else None))
                if d % 4 == 3:
                    s1_last = nmm
                    nev = P.emit("vector", lambda e, d=d, bank=bank: e.tensor_copy(
                        AT1[:, (d - 3) * 97:(d + 1) * 97], pf[bank][:]),
                        waits=[(c_pe, nmm)], inc=c_dve)
                    ev1[bank] = nev
            at_done = nev
            ev2 = {}
            nevc = None
            for k1 in range(33):
                for ri in range(2):
                    bank = (k1 * 2 + ri) % 2
                    if ri == 0:
                        l1, l2 = COSF, SINF
                        r1o, r2o = 0, 64
                    else:
                        l1, l2 = COSF, NSINF
                        r1o, r2o = 64, 0
                    w = [(c_dve, at_done)]
                    if ev2.get(bank) is not None:
                        w.append((c_act, ev2[bank]))
                    elif ps2_war is not None:
                        w.append(ps2_war)
                    P.emit("tensor", lambda e, k1=k1, l1=l1, r1o=r1o, bank=bank: e.matmul(
                        ps2[bank][:, 0:64], l1[:, k1 * 128:(k1 + 1) * 128],
                        AT1[:, r1o + k1::97][:, 0:64], start=True, stop=False),
                        waits=w)
                    nmm = P.emit("tensor", lambda e, k1=k1, l2=l2, r2o=r2o, bank=bank: e.matmul(
                        ps2[bank][:, 0:64], l2[:, k1 * 128:(k1 + 1) * 128],
                        AT1[:, r2o + k1::97][:, 0:64], start=False, stop=True),
                        inc=c_pe)
                    w = [(c_pe, nmm), dst_war]
                    nevc = P.emit("scalar", lambda e, k1=k1, ri=ri, bank=bank, dst_tile=dst_tile: e.copy(
                        dst_tile[:, k1 * 128 + ri * 64: k1 * 128 + ri * 64 + 64],
                        ps2[bank][:, 0:64]), waits=w, inc=c_act)
                    ev2[bank] = nevc
            return (c_pe, s1_last), (c_dve, at_done), (c_act, nevc)

        bt_war = None
        hs_done = None
        dsw_war = None
        ycs_war = None
        yc_stores = []
        at_war = None
        cs_war = None
        ys_war = None
        ds_war = None
        pend_stagec = None

        def emit_stagec(hc, ndsw):
            nonlocal ycs_war, dsw_war
            nyev = None
            prev_ev = None
            for ch in range(16):
                w = [(d_sy, ndsw), (d_sy, d_small)]
                if prev_ev is not None:
                    w.append((c_dve, prev_ev))
                if ycs_war is not None and ch == 0:
                    w.append(ycs_war)
                nmm = P.emit("tensor", lambda e, ch=ch: e.matmul(
                    pC[:], ABC[:], dsw[:, ch * 512:(ch + 1) * 512],
                    start=True, stop=True), waits=w, inc=c_pe)
                nyev = P.emit("vector", lambda e, ch=ch: e.tensor_copy(
                    ycs[:, ch * 512:(ch + 1) * 512], pC[:]),
                    waits=[(c_pe, nmm)], inc=c_dve)
                prev_ev = nyev
            nyst = P.emit("sync", lambda e, hc=hc: e.dma_start(
                out=yc_f[:, :, hc * 64:(hc + 1) * 64],
                in_=ycs.rearrange("q (a b) -> q a b", b=64)),
                waits=[(c_dve, nyev)], inc=d_sy, inc_amt=16)
            yc_stores.append(nyst)
            ycs_war = (d_sy, nyst)
            dsw_war = (c_pe, P.val(c_pe))
            return None

        for hc in range(16):
            d0 = hc * 64
            bt_war, at_war, hs_done = emit_fft_fwd(P, h_f, hc, HS, bt_war, at_war,
                                                   cs_war, ds_war)
            bt_war, at_war, cs_done = emit_fft_fwd(P, v_f, hc, csv, bt_war, at_war,
                                                   cs_war, hs_done)
            if pend_stagec is not None:
                pend_stagec = emit_stagec(*pend_stagec)
            def plane(tile, ri):
                return tile[:].rearrange("p (k r d) -> p k r d", r=2, d=64)[:, :, ri, :]
            w0 = [cs_done, hs_done]
            if ys_war is not None:
                w0.append(ys_war)
            if ds_war is not None:
                w0.append(ds_war)
            tmp1 = DSB[:, 0:2112]
            tmp2 = DSB[:, 2112:4224]
            n_ = P.emit("vector", lambda e, tmp1=tmp1: e.tensor_tensor(
                tmp1, plane(csv, 0), plane(HS, 0), ALU.mult), waits=w0, inc=c_dve)
            n_ = P.emit("vector", lambda e, tmp2=tmp2: e.tensor_tensor(
                tmp2, plane(csv, 1), plane(HS, 1), ALU.mult),
                waits=[(c_dve, n_)], inc=c_dve)
            n_ = P.emit("vector", lambda e, tmp1=tmp1, tmp2=tmp2: e.tensor_tensor(
                plane(ysv, 0), tmp1, tmp2, ALU.subtract), waits=[(c_dve, n_)], inc=c_dve)
            n_ = P.emit("vector", lambda e, tmp1=tmp1: e.tensor_tensor(
                tmp1, plane(csv, 0), plane(HS, 1), ALU.mult), waits=[(c_dve, n_)], inc=c_dve)
            n_ = P.emit("vector", lambda e, tmp2=tmp2: e.tensor_tensor(
                tmp2, plane(csv, 1), plane(HS, 0), ALU.mult), waits=[(c_dve, n_)], inc=c_dve)
            n6_ = P.emit("vector", lambda e, tmp1=tmp1, tmp2=tmp2: e.tensor_tensor(
                plane(ysv, 1), tmp1, tmp2, ALU.add), waits=[(c_dve, n_)], inc=c_dve)
            ys_done = (c_dve, n6_)
            ys_war = ys_done
            cs_war = ys_done
            ev2 = {}
            nevd = None
            for k1 in range(33):
                for ri in range(2):
                    bank = (k1 * 2 + ri) % 2
                    if ri == 0:
                        l1, l2 = COSI, NSINI
                        s1, s2 = 0, 1
                    else:
                        l1, l2 = COSI, SINI
                        s1, s2 = 1, 0
                    w = [ys_done]
                    if ev2.get(bank) is not None:
                        w.append((c_dve, ev2[bank]))
                    P.emit("tensor", lambda e, k1=k1, l1=l1, s1=s1, bank=bank: e.matmul(
                        ps2[bank][:, 0:64], l1[:, k1 * 128:(k1 + 1) * 128],
                        ysv[:, k1 * 128 + s1 * 64: k1 * 128 + s1 * 64 + 64],
                        start=True, stop=False), waits=w)
                    nmm = P.emit("tensor", lambda e, k1=k1, l2=l2, s2=s2, bank=bank: e.matmul(
                        ps2[bank][:, 0:64], l2[:, k1 * 128:(k1 + 1) * 128],
                        ysv[:, k1 * 128 + s2 * 64: k1 * 128 + s2 * 64 + 64],
                        start=False, stop=True), inc=c_pe)
                    w = [(c_pe, nmm)]
                    if ds_war is not None and k1 == 0 and ri == 0:
                        w.append(ds_war)
                    nevd = P.emit("vector", lambda e, k1=k1, ri=ri, bank=bank: e.tensor_copy(
                        DSB[:, (ri * 33 + k1) * 64:(ri * 33 + k1) * 64 + 64],
                        ps2[bank][:, 0:64]), waits=w, inc=c_dve)
                    ev2[bank] = nevd
            ds_done = (c_dve, nevd)
            nds = P.emit("sync", lambda e, hc=hc: e.dma_start(
                out=db_w[:, :, hc * 64:(hc + 1) * 64],
                in_=DSB[:].rearrange("p (K d) -> p K d", d=64)),
                waits=[ds_done], inc=d_sy, inc_amt=16)
            ds_war = (d_sy, nds)
            w = [(d_sy, nds)]
            if dsw_war is not None:
                w.append(dsw_war)
            ndsw = P.emit("sync", lambda e, hc=hc: e.dma_start(
                out=dsw.rearrange("q (a b) -> q a b", b=64),
                in_=dbuf[:, :, hc * 64:(hc + 1) * 64]), waits=w, inc=d_sy, inc_amt=16)
            pend_stagec = (hc, ndsw)
        pend_stagec = emit_stagec(*pend_stagec)
        barrier()

        d_wo = P.emit("gpsimd", lambda e: e.dma_start(
            out=wob.rearrange("p (ic j) -> p ic j", j=D), in_=wo_r), inc=d_gp, inc_amt=16)
        ul_war = {}
        y2_war = {}
        ob_war = {}
        ab_stores = []
        pbank_free = [None, None]
        tr_free = [None, None]
        for t in range(NT):
            p = t % 2
            w = [(d_sy, P.val(d_sy))] if t == 0 else []
            if p in ul_war:
                w.append(ul_war[p])
            nlu = P.emit("sync", lambda e, t=t, p=p: e.dma_start(
                out=ulv[:, p * D:(p + 1) * D], in_=u_t[t]), waits=w, inc=d_sy, inc_amt=16)
            nly = P.emit("sync", lambda e, t=t, p=p: e.dma_start(
                out=yclv[:, p * D:(p + 1) * D], in_=yc_t[t]), inc=d_sy, inc_amt=16)
            w = [(d_sy, nly)]
            if p in y2_war:
                w.append(y2_war[p])
            ny2 = P.emit("vector", lambda e, p=p: e.tensor_tensor(
                y2t[:, p * D:(p + 1) * D], ulv[:, p * D:(p + 1) * D],
                yclv[:, p * D:(p + 1) * D], ALU.mult), waits=w, inc=c_dve)
            ul_war[p] = (c_dve, ny2)
            last_tp = None
            last_ev = None
            for c in range(8):
                w = [(c_dve, ny2)] if c == 0 else []
                if tr_free[0] is not None:
                    w.append((c_dve, tr_free[0]))
                if c == 0 and p in y2_war:
                    w.append(y2_war[p])
                last_tp = P.emit("tensor", lambda e, p=p, c=c: e.transpose(
                    ptr[0][:], y2t[:, p * D + c * 128: p * D + (c + 1) * 128],
                    idt[:]), waits=w, inc=c_pe)
                last_ev = P.emit("vector", lambda e, p=p, c=c: e.tensor_copy(
                    y2f[:, p * D + c * 128: p * D + (c + 1) * 128], ptr[0][:]),
                    waits=[(c_pe, last_tp)], inc=c_dve)
                tr_free[0] = last_ev
            y2_war[p] = (c_pe, last_tp)
            y2f_ready = last_ev
            ob_evs = []
            for half in range(2):
                bank = pA if half == 0 else pB
                for ic in range(8):
                    w = []
                    if ic == 0:
                        w = [(c_dve, y2f_ready), (d_gp, d_wo)]
                        if pbank_free[half] is not None:
                            w.append(pbank_free[half])
                    P.emit("tensor", lambda e, p=p, half=half, ic=ic, bank=bank: e.matmul(
                        bank[:], y2f[:, p * D + ic * 128: p * D + (ic + 1) * 128],
                        wob[:, ic * D + half * 512: ic * D + half * 512 + 512],
                        start=(ic == 0), stop=False), waits=w)
                nmm = P.emit("tensor", lambda e, half=half, bank=bank: e.matmul(
                    bank[:], onesb[:], BOR[:, half * 512: half * 512 + 512],
                    start=False, stop=True), waits=[(d_gp, d_rows)], inc=c_pe)
                w = [(c_pe, nmm)]
                if p in ob_war:
                    w += list(ob_war[p])
                nev = P.emit("scalar", lambda e, p=p, half=half, bank=bank: e.activation(
                    obv[:, p * D + half * 512: p * D + half * 512 + 512], bank[:],
                    AF.Identity), waits=w, inc=c_act)
                pbank_free[half] = (c_act, nev)
                ob_evs.append(nev)
            nd2 = P.emit("vector", lambda e, t=t, p=p: e.reduce_sum(
                ln2.sall[:, t:t + 1], obv[:, p * D:(p + 1) * D], axis=AX),
                waits=[(c_act, ob_evs[-1]), (c_dve, ln2.q_zeroed)], inc=c_dve)
            ln2.stat_dves.append(nd2)
            na2 = P.emit("scalar", lambda e, t=t, p=p: e.activation(
                sqscr[:], obv[:, p * D:(p + 1) * D], AF.Square,
                accum_out=ln2.qall[:, t:t + 1]),
                waits=[(c_act, ob_evs[-1]), (c_dve, ln2.q_zeroed)], inc=c_act)
            ln2.stat_acts.append(na2)
            nab = P.emit("scalar", lambda e, t=t, p=p: e.copy(
                ABL[:, p * D:(p + 1) * D], obv[:, p * D:(p + 1) * D]),
                waits=([ob_war[p][0]] if p in ob_war else []), inc=c_act)
            nst = P.emit("sync", lambda e, t=t, p=p: e.dma_start(
                out=a_t[t], in_=ABL[:, p * D:(p + 1) * D]),
                waits=[(c_act, nab)], inc=d_sy, inc_amt=16)
            ab_stores.append(nst)
            ob_war[p] = ((d_sy, nst), (c_dve, nd2))
        ln2.emit_math(P, c_act, c_dve, eps128, invd128, neg1128, eps_wait=(d_gp, n_eps))

        yn_war = {}
        ot_free = {}
        abl_cons = {}
        for t in range(NT):
            slot = t % 2
            p = t % 2
            w = [(c_dve, x_cons[slot][0]), (c_act, x_cons[slot][1])]
            if t < 2:
                w.append((c_dve, P.val(c_dve)))
            nlx = P.emit("sync", lambda e, t=t, slot=slot: e.dma_start(
                out=XB[:, slot * D:(slot + 1) * D], in_=x_t[t]), waits=w,
                inc=d_sy, inc_amt=16)
            w = [(d_sy, ab_stores[t])]
            if p in abl_cons:
                w.append(abl_cons[p])
            else:
                w.append((c_act, P.val(c_act)))
            nly = P.emit("sync", lambda e, t=t, p=p: e.dma_start(
                out=ABL[:, p * D:(p + 1) * D], in_=a_t[t]), waits=w,
                inc=d_sy, inc_amt=16)
            sc, bi = ln2.apply(t)
            w = [(d_sy, nly), (c_dve, ln2.ready)]
            if p in yn_war:
                w.append(yn_war[p])
            na = P.emit("scalar", lambda e, p=p, sc=sc, bi=bi: e.activation(
                ynv[:, p * D:(p + 1) * D], ABL[:, p * D:(p + 1) * D],
                AF.Identity, bias=bi, scale=sc), waits=w, inc=c_act)
            abl_cons[p] = (c_act, na)
            w = [(d_sy, nlx), (c_act, na)]
            if p in ot_free:
                w.append(ot_free[p])
            no = P.emit("vector", lambda e, t=t, p=p, slot=slot: e.tensor_tensor(
                otv[:, p * D:(p + 1) * D], XB[:, slot * D:(slot + 1) * D],
                ynv[:, p * D:(p + 1) * D], ALU.add), waits=w, inc=c_dve)
            yn_war[p] = (c_dve, no)
            x_cons[slot] = (no, x_cons[slot][1])
            ns = P.emit("sync", lambda e, t=t, p=p: e.dma_start(
                out=o_t[t], in_=otv[:, p * D:(p + 1) * D]), waits=[(c_dve, no)],
                inc=d_sy, inc_amt=16)
            ot_free[p] = (d_sy, ns)
            _heat(P, c_pe, pA, idt, (d_sy, nlx), (c_act, P.val(c_act)))

    return


def _l1attn_consts():
    N = 8192
    t1 = np.arange(32)
    k1r = np.arange(33)
    F1C = np.zeros((32, 97), dtype=np.float32)
    F1C[:, 0:33] = np.cos(2 * np.pi * np.outer(t1, k1r) / 64)
    F1C[:, 64:97] = -np.sin(2 * np.pi * np.outer(t1, k1r) / 64)
    t2 = np.arange(128)
    k2 = np.arange(128)
    cosf = np.zeros((128, 33 * 128), dtype=np.float32)
    sinf = np.zeros((128, 33 * 128), dtype=np.float32)
    cosi = np.zeros((128, 33 * 128), dtype=np.float32)
    sini = np.zeros((128, 33 * 128), dtype=np.float32)
    for k1 in range(33):
        th = 2 * np.pi * np.outer(t2, (k1 + 64 * k2)) / N
        cosf[:, k1 * 128:(k1 + 1) * 128] = np.cos(th)
        sinf[:, k1 * 128:(k1 + 1) * 128] = np.sin(th)
        thi = th.T
        cosi[:, k1 * 128:(k1 + 1) * 128] = np.cos(thi)
        sini[:, k1 * 128:(k1 + 1) * 128] = np.sin(thi)
    ABc = np.zeros((66, 32), dtype=np.float32)
    for k1 in range(33):
        c = 1.0 if k1 in (0, 32) else 2.0
        ABc[k1, :] = c * np.cos(2 * np.pi * t1 * k1 / 64) / N
        if k1 not in (0, 32):
            ABc[33 + k1, :] = -2.0 * np.sin(2 * np.pi * t1 * k1 / 64) / N
    L, FO = 4096, 64
    positions = np.arange(L, dtype=np.float32)[:, None]
    div_term = np.exp(np.arange(0, FO, 2, dtype=np.float32) * -(np.log(10000.0) / FO))
    pe = np.zeros((L, FO), np.float32)
    pe[:, 0::2] = np.sin(positions * div_term)
    pe[:, 1::2] = np.cos(positions * div_term)
    dec = np.exp(-0.01 * np.arange(L, dtype=np.float32)).reshape(32, 128).T
    bf = ml_dtypes.bfloat16
    return {
        "f1c": F1C.astype(bf), "cosf": cosf.astype(bf), "sinf": sinf.astype(bf),
        "nsinf": (-sinf).astype(bf), "cosi": cosi.astype(bf), "sini": sini.astype(bf),
        "nsini": (-sini).astype(bf), "abc": ABc.astype(bf),
        "pet": np.ascontiguousarray(pe.T).astype(bf),
        "dect": np.ascontiguousarray(dec),
    }


_L1A_CONSTS = None


def _l1attn_inmaps(inputs):
    global _L1A_CONSTS
    if _L1A_CONSTS is None:
        _L1A_CONSTS = _l1attn_consts()
    m = {
        "h_wu": inputs["L1_attn_Wu"],
        "h_wv": inputs["L1_attn_Wv"],
        "h_wo": inputs["L1_attn_Wo"],
        "h_bur": inputs["L1_attn_bu"].reshape(1, D),
        "h_bvr": inputs["L1_attn_bv"].reshape(1, D),
        "h_bor": inputs["L1_attn_bo"].reshape(1, D),
        "h_w1f": inputs["L1_attn_W1"],
        "h_w2f": inputs["L1_attn_W2"],
        "h_w3f": inputs["L1_attn_W3"],
        "h_b1c": inputs["L1_attn_b1"].reshape(64, 1),
        "h_b2c": inputs["L1_attn_b2"].reshape(64, 1),
        "h_b3r": inputs["L1_attn_b3"].reshape(1, D),
    }
    m.update({"h_" + k: v for k, v in _L1A_CONSTS.items()})
    return m

def _heat(P, c_pe, bank, idt, wait, wait2=None):
    P.emit("tensor", lambda e: e.matmul(bank[0:1, 0:1], idt[0:1, 0:1],
                                        idt[0:1, 0:1], start=True, stop=True),
           waits=[wait, wait2], inc=c_pe)


def _bar(P, sems):
    c_pe, c_act, c_dve, d_sy, d_gp = sems
    P.set_barrier([(c_pe, P.val(c_pe)), (c_act, P.val(c_act)), (c_dve, P.val(c_dve)),
                   (d_sy, P.val(d_sy)), (d_gp, P.val(d_gp))])


def _finalize(nc, P, sems):
    c_pe, c_act, c_dve, d_sy, d_gp = sems
    P.emit("sync", lambda e: e.nop(), waits=[(d_sy, P.val(d_sy))])
    P.emit("gpsimd", lambda e: e.nop(), waits=[(d_gp, P.val(d_gp))])
    with nc.Block() as block:
        P.run(block)


def build_fused(dbg=False):
    nc = bass.Bass("TRN2", target_bir_lowering=False, debug=False)
    kind = "ExternalOutput" if dbg else "Internal"
    x0 = nc.dram_tensor("x", [NT * 128, D], F32, kind="ExternalInput").ap()
    out = nc.dram_tensor("out", [NT * 128, D], F32, kind="ExternalOutput").ap()
    ident = nc.dram_tensor("ident", [128, 128], BF16, kind="ExternalInput").ap()
    x1b = nc.dram_tensor("x1b", [NT * 128, D], F32, kind=kind).ap()
    x2b = nc.dram_tensor("x2b", [NT * 128, D], F32, kind=kind).ap()
    x3b = nc.dram_tensor("x3b", [NT * 128, D], F32, kind=kind).ap()
    P = Prog()
    with contextlib.ExitStack() as stack:
        sems = tuple(stack.enter_context(nc.semaphore(nm))
                     for nm in ("c_pe", "c_act", "c_dve", "d_sy", "d_gp"))
        emit_l0attn(nc, P, sems, x0, x1b, ident)
        _bar(P, sems)
        emit_l0ffn(nc, P, sems, x1b, x2b, ident)
        _bar(P, sems)
        emit_l1attn(nc, P, sems, x2b, x3b, ident)
        _bar(P, sems)
        emit_l1ffn(nc, P, sems, x3b, out, ident)
        _finalize(nc, P, sems)
    return nc


def _fused_inmaps(x_core, inputs):
    m = {"x": np.ascontiguousarray(x_core),
         "ident": np.eye(128, dtype=ml_dtypes.bfloat16)}
    m.update(_l0attn_inmaps(inputs))
    m.update(_l0ffn_inmaps(inputs))
    m.update(_l1attn_inmaps(inputs))
    m.update(_l1ffn_inmaps(inputs))
    return m


def _run_block(emit_fn, inmaps_fn, x_full, inputs, n_cores=8):
    nc = bass.Bass("TRN2", target_bir_lowering=False, debug=False)
    x0 = nc.dram_tensor("x", [NT * 128, D], F32, kind="ExternalInput").ap()
    out = nc.dram_tensor("out", [NT * 128, D], F32, kind="ExternalOutput").ap()
    ident = nc.dram_tensor("ident", [128, 128], BF16, kind="ExternalInput").ap()
    P = Prog()
    with contextlib.ExitStack() as stack:
        sems = tuple(stack.enter_context(nc.semaphore(nm))
                     for nm in ("c_pe", "c_act", "c_dve", "d_sy", "d_gp"))
        inner = contextlib.ExitStack()
        stack.enter_context(inner)
        emit_fn(nc, P, sems, x0, out, ident, outer=inner)
        _finalize(nc, P, sems)
    in_maps = []
    for c in range(n_cores):
        m = {"x": np.ascontiguousarray(x_full[c]),
             "ident": np.eye(128, dtype=ml_dtypes.bfloat16)}
        m.update(inmaps_fn(inputs))
        in_maps.append(m)
    res = run_bass_kernel_spmd(nc, in_maps, core_ids=list(range(n_cores)))
    return np.stack([r["out"] for r in res.results], axis=0)


def run_l0attn(x_full, inputs, n_cores=8):
    return _run_block(emit_l0attn, _l0attn_inmaps, x_full, inputs, n_cores)


def run_l0ffn(x_full, inputs, n_cores=8):
    return _run_block(emit_l0ffn, _l0ffn_inmaps, x_full, inputs, n_cores)


def run_l1attn(x_full, inputs, n_cores=8):
    return _run_block(emit_l1attn, _l1attn_inmaps, x_full, inputs, n_cores)


def run_l1ffn(x_full, inputs, n_cores=8):
    return _run_block(emit_l1ffn, _l1ffn_inmaps, x_full, inputs, n_cores)


def kernel(**inputs):
    x = inputs["L0_attn_x"]
    nc = build_fused()
    in_maps = [_fused_inmaps(x[c], inputs) for c in range(8)]
    res = run_bass_kernel_spmd(nc, in_maps, core_ids=list(range(8)))
    return np.stack([r["out"] for r in res.results], axis=0)
